# version 22
# baseline (speedup 1.0000x reference)
import numpy as np
import ml_dtypes
import concourse.bass as bass
import concourse.mybir as mybir
from contextlib import ExitStack

F32 = mybir.dt.float32
BF16 = mybir.dt.bfloat16
AF = mybir.ActivationFunctionType
ALU = mybir.AluOpType
AX = mybir.AxisListType
AP = bass.AP

N = 4096
D = 1024
H = 16
HD = 64
TA = 256
NTA = N // TA
NEG = -1.0e30
C_OUTER = 8.0

ENGINES = ("sync", "gpsimd", "scalar", "vector", "tensor")
SKIP_SET = set()


class Sched:
    def __init__(self, nc):
        self.nc = nc
        self.ops = []
        self.last_writer = {}
        self.readers = {}
        self.NSEM = 6
        self.sem_comp = {e: nc.alloc_semaphore(f"sc_{e}") for e in ENGINES}
        self.sem_dma = {e: [nc.alloc_semaphore(f"sd_{e}{i}") for i in range(self.NSEM)]
                        for e in ("sync", "gpsimd", "scalar")}

    def add(self, eng, fn, reads=(), writes=(), dma=False, tag=""):
        op = {"eng": eng, "fn": fn, "dma": dma, "idx": len(self.ops), "deps": set(),
              "need_inc": False, "inc_no": None, "dma_no": None, "tag": tag, "barrier": False}
        deps = set()
        for r in reads:
            w = self.last_writer.get(r)
            if w is not None:
                deps.add(w["idx"])
        for r in writes:
            w = self.last_writer.get(r)
            if w is not None:
                deps.add(w["idx"])
            for rd in self.readers.get(r, ()):
                deps.add(rd["idx"])
        deps.discard(op["idx"])
        op["deps"] = deps
        self.ops.append(op)
        for r in reads:
            self.readers.setdefault(r, []).append(op)
        for r in writes:
            self.last_writer[r] = op
            self.readers[r] = []
        return op

    def cut(self):
        pass

    def barrier(self):
        op = {"eng": None, "fn": None, "dma": False, "idx": len(self.ops), "deps": set(),
              "need_inc": False, "inc_no": None, "dma_no": None, "tag": "barrier", "barrier": True}
        self.ops.append(op)
        self.last_writer = {}
        self.readers = {}
        return op

    def emit(self):
        nc = self.nc
        dma_count = {e: 0 for e in self.sem_dma}
        for op in self.ops:
            if op["barrier"]:
                continue
            if op["dma"]:
                dma_count[op["eng"]] += 1
                op["dma_no"] = dma_count[op["eng"]]
        SYNC_SAME = ("vector", "scalar", "gpsimd")
        last_op = {e: None for e in ENGINES}
        for op in self.ops:
            if op["barrier"]:
                for e in ENGINES:
                    if last_op[e] is not None and not last_op[e]["dma"]:
                        last_op[e]["need_inc"] = True
                    last_op[e] = None
                continue
            if op["eng"] in last_op:
                last_op[op["eng"]] = op
        for op in self.ops:
            if op["barrier"]:
                continue
            for d in op["deps"]:
                dep = self.ops[d]
                if dep["barrier"] or dep["dma"]:
                    continue
                if (dep["eng"] != op["eng"] or op["dma"]
                        or (dep["eng"] == op["eng"] and dep["eng"] in SYNC_SAME)):
                    dep["need_inc"] = True
        inc_no = {e: 0 for e in ENGINES}
        dma_no = {e: 0 for e in self.sem_dma}
        barrier_info = {}
        for op in self.ops:
            if op["barrier"]:
                pre = dict(inc_no)
                for e in ENGINES:
                    inc_no[e] += 1
                barrier_info[op["idx"]] = (dict(inc_no), dict(dma_no), pre)
                continue
            if op["dma"]:
                dma_no[op["eng"]] = op["dma_no"]
            elif op["need_inc"]:
                inc_no[op["eng"]] += 1
                op["inc_no"] = inc_no[op["eng"]]
        self.total_incs = dict(inc_no)
        self.total_dmas = dict(dma_no)

        def dma_wait_args(qe, dma_no):
            i = dma_no - 1
            return self.sem_dma[qe][i % self.NSEM], 16 * (i // self.NSEM + 1)

        def emit_engine(eng_name, eng):
            clock = {}
            def wait(sem, val):
                key = sem.num
                if clock.get(key, -1) >= val:
                    return
                eng.wait_ge(sem, val)
                clock[key] = val
            def wait_dma_full(qe, cnt):
                for k in range(cnt, max(cnt - self.NSEM, 0), -1):
                    s, v = dma_wait_args(qe, k)
                    wait(s, v)
            for op in self.ops:
                if op["barrier"]:
                    binc, bdma, pre = barrier_info[op["idx"]]
                    wait(self.sem_comp[eng_name], pre[eng_name])
                    for qe, cnt in bdma.items():
                        wait_dma_full(qe, cnt)
                    eng.nop().then_inc(self.sem_comp[eng_name], 1)
                    for e2 in ENGINES:
                        wait(self.sem_comp[e2], binc[e2])
                    continue
                if op["eng"] != eng_name:
                    continue
                for d in sorted(op["deps"]):
                    dep = self.ops[d]
                    if dep["barrier"]:
                        continue
                    if dep["dma"]:
                        s, v = dma_wait_args(dep["eng"], dep["dma_no"])
                        wait(s, v)
                    elif (dep["eng"] != eng_name or op["dma"]
                          or (dep["eng"] == eng_name and eng_name in ("vector", "scalar", "gpsimd"))):
                        if dep["inc_no"] is not None:
                            wait(self.sem_comp[dep["eng"]], dep["inc_no"])
                inst = op["fn"](eng)
                if op["dma"]:
                    s, _ = dma_wait_args(eng_name, op["dma_no"])
                    inst.then_inc(s, 16)
                elif op["need_inc"]:
                    inst.then_inc(self.sem_comp[eng_name], 1)
            for qe, cnt in self.total_dmas.items():
                wait_dma_full(qe, cnt)
            for e2 in ENGINES:
                if self.total_incs[e2]:
                    wait(self.sem_comp[e2], self.total_incs[e2])

        with nc.Block() as block:
            block.sync(lambda eng: emit_engine("sync", eng))
            block.gpsimd(lambda eng: emit_engine("gpsimd", eng))
            block.scalar(lambda eng: emit_engine("scalar", eng))
            block.vector(lambda eng: emit_engine("vector", eng))
            block.tensor(lambda eng: emit_engine("tensor", eng))



def rope_tables():
    f = np.arange(32, dtype=np.float64)
    inv = 1.0 / (10000.0 ** (2 * f / 64))
    t = np.arange(N, dtype=np.float64)
    ang = t[:, None] * inv[None, :]
    cos = np.cos(ang)
    sin = np.sin(ang)
    ctab = np.zeros((128, N), dtype=np.float64)
    stab = np.zeros((128, N), dtype=np.float64)
    for p in range(128):
        d = p % 64
        fi = d // 2
        ctab[p] = cos[:, fi]
        stab[p] = (-1.0 if d % 2 == 0 else 1.0) * sin[:, fi]
    return ctab.astype(ml_dtypes.bfloat16), stab.astype(ml_dtypes.bfloat16)


def outer_slide_mask():
    ki = np.arange(128)[:, None]
    u = np.arange(896)[None, :]
    return np.where(ki > u - 384, NEG, 0.0).astype(ml_dtypes.bfloat16)


def swap_matrix():
    P = np.zeros((128, 128), dtype=ml_dtypes.bfloat16)
    for i in range(128):
        P[i ^ 1, i] = 1.0
    return P


def xpos_tables():
    d = 64
    half = np.arange(0, d, 2, dtype=np.float64)
    inv = 1.0 / 10000.0 ** (half / d)
    t = np.arange(N, dtype=np.float64)
    fr = t[:, None] * inv[None, :]
    fr = np.concatenate([fr, fr], axis=1)
    base = (half + 0.4 * d) / (1.4 * d)
    power = (t - N // 2) / 256.0
    sc = base[None, :] ** power[:, None]
    sc = np.concatenate([sc, sc], axis=1)
    sign = np.where(np.arange(64) < 32, -1.0, 1.0)[None, :]
    cos, sin = np.cos(fr), np.sin(fr)
    tabs = np.concatenate([cos * sc, sign * sin * sc, cos / sc, sign * sin / sc], axis=1)
    return tabs.astype(np.float32)


def inner_masks():
    li = np.arange(128)[:, None]
    kk = np.arange(640)[None, :]
    base = np.where((kk >= li) & (kk <= li + 512), 0.0, NEG)
    w0 = [np.full((128, 512 - 128 * s), NEG) for s in range(4)]
    out = np.concatenate([base] + w0, axis=1)
    return out.astype(ml_dtypes.bfloat16)



def build_program(debug=(), b_stop=None, nwindows=8, do_mlp=True, nlayers=2, a_tiles=8, b_tiles=8, skip=()):
    nc = bass.Bass("TRN2", target_bir_lowering=False)
    S = Sched(nc)

    x0_d = nc.dram_tensor("x0", [D, N], F32, kind="ExternalInput")
    wqT_d = nc.dram_tensor("wqT", [D, D], F32, kind="ExternalInput")
    wkT_d = nc.dram_tensor("wkT", [D, D], F32, kind="ExternalInput")
    wvT_d = nc.dram_tensor("wvT", [D, D], F32, kind="ExternalInput")
    woT_d = nc.dram_tensor("woT", [D, D], F32, kind="ExternalInput")
    bo_d = nc.dram_tensor("boT", [128, 8], F32, kind="ExternalInput")

    x1_d = nc.dram_tensor("x1scratch", [D, N], F32, kind="Internal")
    lnx_d = nc.dram_tensor("lnxscratch", [D, N], BF16, kind="Internal")
    y_d = nc.dram_tensor("y", [D, N], F32, kind="ExternalOutput")
    wqkv_d = nc.dram_tensor("wqkv", [2, 1024, 1536], F32, kind="ExternalInput")
    woB_d = nc.dram_tensor("wo_in", [2, 512, 1024], F32, kind="ExternalInput")
    w1_d = nc.dram_tensor("w1_in", [2, 1024, 5460], F32, kind="ExternalInput")
    w2_d = nc.dram_tensor("w2_in", [2, 2730, 1024], F32, kind="ExternalInput")
    wqkvb_d = nc.dram_tensor("wqkvbd", [2, 1024, 1536], BF16, kind="Internal")
    wob_d = nc.dram_tensor("wobd", [2, 512, 1024], BF16, kind="Internal")
    w1b_d = nc.dram_tensor("w1bd", [2, 1024, 5460], BF16, kind="Internal")
    w2b_d = nc.dram_tensor("w2bd", [2, 2730, 1024], BF16, kind="Internal")
    b1T_d = nc.dram_tensor("b1T", [128, 88], F32, kind="ExternalInput")
    b2T_d = nc.dram_tensor("b2T", [128, 16], F32, kind="ExternalInput")
    lngT_d = nc.dram_tensor("lngT", [128, 16], F32, kind="ExternalInput")
    lnbT_d = nc.dram_tensor("lnbT", [128, 16], F32, kind="ExternalInput")
    qsbc_d = nc.dram_tensor("qsbcT", [2, 128, 512], F32, kind="ExternalInput")
    ksbc_d = nc.dram_tensor("ksbcT", [2, 128, 512], F32, kind="ExternalInput")
    xpos_d = nc.inline_tensor(xpos_tables(), name="xpostab")
    maskB_d = nc.inline_tensor(inner_masks(), name="maskinner")

    ctab, stab = rope_tables()
    cos_d = nc.inline_tensor(ctab, name="ropecos")
    sin_d = nc.inline_tensor(stab, name="ropesin")
    maskslide_d = nc.inline_tensor(outer_slide_mask(), name="maskslide")
    pswap_d = nc.inline_tensor(swap_matrix(), name="pswap")
    eye128_d = nc.inline_tensor(np.eye(128, dtype=ml_dtypes.bfloat16), name="eye128")
    onescol_d = nc.inline_tensor(np.ones((128, 1), dtype=ml_dtypes.bfloat16), name="onescol")
    ones64_d = nc.inline_tensor(np.ones((1, 64), dtype=np.float32), name="ones64")
    ones128_d = nc.inline_tensor(np.ones((1, 128), dtype=np.float32), name="ones128")

    dbg = {}
    for name, shape in (("kfm", [128, 8 * N]), ("vtm", [128, 32 * 1040]),
                        ("ln1", [128, 8 * TA]), ("qfm", [128, 8 * TA]),
                        ("x1", [D, N]), ("ofm", [128, 8 * TA]), ("u", [128, 8 * TA]),
                        ("rows2", [1, 4 * TA]),
                        ("zinit", [128, 8 * N]), ("z0a", [128, 8 * N]), ("z0f", [128, 8 * N]),
                        ("z1a", [128, 8 * N]), ("z1f", [128, 8 * N]),
                        ("bqr", [128, 512]), ("bkr", [128, 512]), ("bofm", [128, 4 * 512]),
                        ("be", [128, 640]), ("beT", [128, 640]), ("bden", [128, 8]),
                        ("be1", [128, 640]),
                        ("bkT", [128, 4 * 1024]), ("bvT", [128, 8 * 512])):
        if name in debug:
            dbg[name] = nc.dram_tensor("dbg_" + name, shape, F32, kind="ExternalOutput")

    with ExitStack() as ctx:
        sb = lambda name, shape, dt: ctx.enter_context(nc.sbuf_tensor(name, shape, dt))

        pswap = sb("sb_pswap", [128, 128], BF16)
        eye128 = sb("sb_eye128", [128, 128], BF16)
        onescol = sb("sb_onescol", [128, 1], BF16)
        ones128 = sb("sb_ones128", [1, 128], F32)
        bo_sb = sb("bo", [128, 8], F32)
        onesall = sb("sb_onesall", [128, 64], F32)
        consts = sb("sb_consts", [128, 2], F32)
        S.add("vector", lambda e: e.memset(onesall[:, :], 1.0), writes=[("onesall",)])
        S.add("vector", lambda e: e.memset(consts[:, 0:1], 1e-5), writes=[("c_eps",)])
        S.add("vector", lambda e: e.memset(consts[:, 1:2], -C_OUTER), writes=[("c_negC",)])

        for dst, src, res in ((pswap, pswap_d, "pswap"), (eye128, eye128_d, "eye128"),
                              (onescol, onescol_d, "onescol"),
                              (ones128, ones128_d, "ones128"), (bo_sb, bo_d, "bo")):
            S.add("sync", lambda e, dst=dst, src=src: e.dma_start(out=dst[:, :], in_=src.ap()),
                  writes=[(res,)], dma=True)


        _padc = [0]

        def pad_op(eng, buf, cols=256, rows=(0, 128)):
            _padc[0] += 1
            k = _padc[0]
            r0, r1 = rows
            if eng == "scalar":
                S.add("scalar", lambda e, buf=buf, cols=cols, r0=r0, r1=r1: e.activation(
                    out=buf[r0:r1, 0:cols], in_=buf[r0:r1, 0:cols], func=AF.Copy),
                    writes=[("pad", k)])
            else:
                S.add(eng, lambda e, buf=buf, cols=cols, r0=r0, r1=r1: e.tensor_copy(
                    out=buf[r0:r1, 0:cols], in_=buf[r0:r1, 0:cols]),
                    writes=[("pad", k)])

        TW = 512
        THF = 256
        NTW = N // TW
        with ExitStack() as ctxA:
            sbA = lambda name, shape, dt: ctxA.enter_context(nc.sbuf_tensor(name, shape, dt))
            k_fm = sbA("k_fm", [128, 8 * N], BF16)
            v_tm = sbA("v_tm", [128, 32 * 1040], BF16)
            wA = sbA("wA", [128, 8 * 1024], BF16)
            wB = sbA("wB", [128, 8 * 1024], BF16)

            def dma_wT(src_d, dst, res):
                S.add("gpsimd", lambda e, src_d=src_d, dst=dst: e.dma_start(
                    out=AP(dst, 0, [[8 * 1024, 128], [1024, 8], [1, 1024]]),
                    in_=AP(src_d, 0, [[D, 128], [128 * D, 8], [1, 1024]])),
                    writes=[(res,)], dma=True)

            def phaseA_sub(ctxS, is_a2):
                sbS = lambda name, shape, dt: ctxS.enter_context(
                    nc.sbuf_tensor(("a2_" if is_a2 else "a1_") + name, shape, dt))
                psS = lambda name, shape: ctxS.enter_context(
                    nc.psum_tensor(("a2p_" if is_a2 else "a1p_") + name, shape, F32))
                st = {}
                if is_a2:
                    st["lq"] = sbS("lq", [128, 8 * TW], BF16)
                    st["o_fm"] = sbS("o_fm", [128, 8 * TW], BF16)
                    st["qfu"] = sbS("qfu", [128, 8 * TW], BF16)
                else:
                    st["lq"] = sbS("lq", [128, 2 * 8 * TW], BF16)
                    st["xau"] = sbS("xau", [128, 8 * THF], F32)
                st["ropec"] = sbS("ropec", [128, TW], BF16)
                st["ropes"] = sbS("ropes", [128, TW], BF16)
                st["xb"] = sbS("xb", [128, 2 * THF], BF16)
                st["x2b"] = sbS("x2b", [128, 2 * THF], BF16)
                st["rows"] = sbS("rows", [1, 3 * THF], F32)
                st["kraw"] = sbS("kraw", [128, 2 * TW], BF16)
                st["t1"] = sbS("t1", [128, TW], F32)
                st["t2"] = sbS("t2", [128, TW], F32)
                if is_a2:
                    st["e_sb"] = sbS("e_sb", [128, 4 * TW], BF16)
                    st["maskd"] = sbS("maskd", [128, 896], BF16)
                    st["bc"] = sbS("bc", [128, TW], F32)
                if is_a2:
                    st["P"] = [psS("P0", [128, 512]), psS("P1", [128, 512]),
                               psS("P2", [128, 512]), psS("P3", [128, 512]),
                               psS("P45", [128, 1024]), psS("P67", [128, 1024])]
                else:
                    st["P"] = [psS(f"P{i}", [128, 512]) for i in range(8)]
                return st

            def ln_half(st, x_src_dma_fn, dst_col_fn, tag, xbuf=None, xres=("xau",), lq_res=("lq",), stats_ix=(6, 7)):
                P = st["P"]
                rows = st["rows"]
                xb, x2b = st["xb"], st["x2b"]
                xau = xbuf if xbuf is not None else st["xau"]
                if x_src_dma_fn is not None:
                    x_src_dma_fn()
                si0, si1 = stats_ix
                for c in range(8):
                    sl = c % 2
                    ss = slice(sl * THF, (sl + 1) * THF)
                    cs = slice(c * THF, (c + 1) * THF)
                    S.add("vector", lambda e, cs=cs, ss=ss: e.tensor_copy(out=xb[:, ss], in_=xau[:, cs]),
                          reads=[xres], writes=[("xb", sl)])
                    S.add("scalar", lambda e, cs=cs, ss=ss: e.activation(out=x2b[:, ss], in_=xau[:, cs], func=AF.Square),
                          reads=[xres], writes=[("x2b", sl)])
                    S.add("tensor", lambda e, ss=ss, c=c, si0=si0: e.matmul(
                        P[si0][0:1, 0:THF], onescol[:, :], xb[:, ss], start=(c == 0), stop=(c == 7)),
                        reads=[("xb", sl), ("onescol",)], writes=[("P", si0)])
                    S.add("tensor", lambda e, ss=ss, c=c, si1=si1: e.matmul(
                        P[si1][0:1, 0:THF], onescol[:, :], x2b[:, ss], start=(c == 0), stop=(c == 7)),
                        reads=[("x2b", sl), ("onescol",)], writes=[("P", si1)])
                mu = rows[:, 0 * THF:1 * THF]
                m2 = rows[:, 1 * THF:2 * THF]
                rr = m2
                mur = rows[:, 2 * THF:3 * THF]
                S.add("vector", lambda e, mu=mu, si0=si0: e.tensor_scalar_mul(out=mu, in0=P[si0][0:1, 0:THF], scalar1=1.0 / D),
                      reads=[("P", si0)], writes=[("r_mu",)])
                S.add("vector", lambda e, mu=mu, m2=m2: e.scalar_tensor_tensor(out=m2, in0=mu, scalar=-1.0, in1=mu,
                                                                 op0=ALU.mult, op1=ALU.mult),
                      reads=[("r_mu",)], writes=[("r_m2",)])
                S.add("vector", lambda e, m2=m2, si1=si1: e.scalar_tensor_tensor(out=m2, in0=P[si1][0:1, 0:THF], scalar=1.0 / D,
                                                                 in1=m2, op0=ALU.mult, op1=ALU.add),
                      reads=[("P", si1), ("r_m2",)], writes=[("r_var",)])
                S.add("scalar", lambda e, m2=m2, rr=rr: e.activation(out=rr, in_=m2, func=AF.Ln, bias=consts[0:1, 0:1]),
                      reads=[("r_var",), ("c_eps",)], writes=[("r_lnv",)])
                S.add("scalar", lambda e, rr=rr: e.activation(out=rr, in_=rr, func=AF.Exp, scale=-0.5),
                      reads=[("r_lnv",)], writes=[("r_r",)])
                S.add("vector", lambda e, mu=mu, rr=rr, mur=mur: e.tensor_mul(out=mur, in0=mu, in1=rr),
                      reads=[("r_mu",), ("r_r",)], writes=[("r_mur",)])
                S.add("tensor", lambda e, rr=rr: e.matmul(P[0][:, 0:THF], ones128[:, :], rr, start=True, stop=True),
                      reads=[("r_r",), ("ones128",)], writes=[("P", 0)])
                S.add("tensor", lambda e, mur=mur: e.matmul(P[1][:, 0:THF], ones128[:, :], mur, start=True, stop=True),
                      reads=[("r_mur",), ("ones128",)], writes=[("P", 1)])
                if tag == "ln":
                    lq = st["lq"]
                    t1 = st["t1"]
                    for c in range(8):
                        cs = slice(c * THF, (c + 1) * THF)
                        S.add("vector", lambda e, cs=cs: e.tensor_mul(out=t1[:, 0:THF], in0=xau[:, cs], in1=P[0][:, 0:THF]),
                              reads=[xres, ("P", 0)], writes=[("t1",)])
                        S.add("vector", lambda e, c=c, cs=cs: e.tensor_sub(
                            out=lq[:, dst_col_fn(c)], in0=t1[:, 0:THF], in1=P[1][:, 0:THF]),
                            reads=[("t1",), ("P", 1)], writes=[lq_res])
                else:
                    for c in range(8):
                        cs = slice(c * THF, (c + 1) * THF)
                        S.add("vector", lambda e, cs=cs: e.tensor_mul(out=xau[:, cs], in0=xau[:, cs], in1=P[0][:, 0:THF]),
                              reads=[xres, ("P", 0)], writes=[xres])
                        S.add("vector", lambda e, cs=cs: e.tensor_sub(out=xau[:, cs], in0=xau[:, cs], in1=P[1][:, 0:THF]),
                              reads=[xres, ("P", 1)], writes=[xres])

            def proj_rope(st, wsb, wres, out_sb, out_res, out_col_fn, ti, lq_base=0, lq_res=("lq",), lq_pitch=None,
                          adder=None, p_proj=2, p_swap=3, c_list=None, out_res_fn=None):
                A2_ = adder if adder is not None else S
                P = st["P"]
                lq, kraw, t1, t2 = st["lq"], st["kraw"], st["t1"], st["t2"]
                if lq_pitch is None:
                    lq_pitch = lq.shape[1]
                ropec, ropes = st["ropec"], st["ropes"]
                for c in (c_list if c_list is not None else range(8)):
                    for j in range(8):
                        A2_.add("tensor", lambda e, c=c, j=j, wsb=wsb, lq=lq, lq_base=lq_base, lq_pitch=lq_pitch: e.matmul(
                            P[p_proj][:, :], wsb[:, j * 1024 + c * 128: j * 1024 + (c + 1) * 128],
                            AP(lq, lq_base + j * TW, [[lq_pitch, 128], [1, TW]]),
                            start=(j == 0), stop=(j == 7), skip_group_check=True),
                            reads=[lq_res, (wres,)], writes=[("P", p_proj)])
                        if j == 3:
                            A2_.cut()
                    ks = c % 2
                    ksl = slice(ks * TW, (ks + 1) * TW)
                    A2_.cut()
                    A2_.add("scalar", lambda e, ksl=ksl: e.activation(
                        out=kraw[:, ksl], in_=P[p_proj][:, :], func=AF.Copy),
                        reads=[("P", p_proj)], writes=[("kraw", ks)])
                    A2_.add("tensor", lambda e, ksl=ksl: e.matmul(
                        P[p_swap][:, :], pswap[:, :], kraw[:, ksl], start=True, stop=True,
                        skip_group_check=True),
                        reads=[("kraw", ks), ("pswap",)], writes=[("P", p_swap)])
                    A2_.cut()
                    A2_.add("vector", lambda e, ksl=ksl: e.tensor_mul(
                        out=t1[:, :], in0=kraw[:, ksl], in1=ropec[:, :]),
                        reads=[("kraw", ks), ("ropec",)], writes=[("t1",)])
                    A2_.add("vector", lambda e: e.tensor_mul(
                        out=t2[:, :], in0=P[p_swap][:, :], in1=ropes[:, :]),
                        reads=[("P", p_swap), ("ropes",)], writes=[("t2",)])
                    A2_.add("vector", lambda e, c=c, out_sb=out_sb: e.tensor_add(
                        out=out_sb[:, out_col_fn(c)], in0=t1[:, :], in1=t2[:, :]),
                        reads=[("t1",), ("t2",)],
                        writes=[out_res_fn(c) if out_res_fn is not None else out_res])
                    A2_.cut()

            def dma_x_half(st, T0):
                S.add("sync", lambda e, T0=T0: e.dma_start(
                    out=AP(st["xau"], 0, [[8 * THF, 128], [THF, 8], [1, THF]]),
                    in_=AP(x0_d, T0, [[N, 128], [128 * N, 8], [1, THF]])),
                    writes=[("xau",)], dma=True)

            def dma_rope(st, ti):
                S.add("sync", lambda e, ti=ti: e.dma_start(
                    out=st["ropec"][:, :], in_=AP(cos_d, ti * TW, [[N, 128], [1, TW]])),
                    writes=[("ropec",)], dma=True)
                S.add("sync", lambda e, ti=ti: e.dma_start(
                    out=st["ropes"][:, :], in_=AP(sin_d, ti * TW, [[N, 128], [1, TW]])),
                    writes=[("ropes",)], dma=True)

            dma_wT(wkT_d, wA, "wA")
            dma_wT(wvT_d, wB, "wB")
            for srcd, dstd, rows, width, res in (
                    (wqkv_d, wqkvb_d, 2 * 1024, 1536, "wqkvC"),
                    (woB_d, wob_d, 2 * 512, 1024, "woC"),
                    (w1_d, w1b_d, 2 * 1024, 5460, "w1C"),
                    (w2_d, w2b_d, 2 * 2730, 1024, "w2C")):
                r0 = 0
                while r0 < rows:
                    rc = min(128, rows - r0)
                    S.add("gpsimd", lambda e, srcd=srcd, dstd=dstd, r0=r0, rc=rc, width=width: e.dma_start(
                        out=AP(dstd, r0 * width, [[width, rc], [1, width]]),
                        in_=AP(srcd, r0 * width, [[width, rc], [1, width]])),
                        writes=[(res, r0)], dma=True)
                    r0 += rc
            with ExitStack() as ctxS:
                st = phaseA_sub(ctxS, is_a2=False)
                P = st["P"]
                S.add("vector", lambda e: e.memset(v_tm[:, :], 1.0), writes=[("v_tm_init",)])

                def ln_tile(i):
                    par = i % 2
                    lqo = par * 8 * TW
                    lqres = ("lq", par)
                    for h2 in range(2):
                        ln_half(st, lambda h2=h2, i=i: dma_x_half(st, i * TW + h2 * THF),
                                lambda c, h2=h2, lqo=lqo: slice(lqo + c * TW + h2 * THF,
                                                                lqo + c * TW + (h2 + 1) * THF),
                                "ln", lq_res=lqres)
                    S.add("sync", lambda e, i=i, lqo=lqo, st=st: e.dma_start(
                        out=AP(lnx_d, i * TW, [[N, 128], [128 * N, 8], [1, TW]]),
                        in_=AP(st["lq"], lqo, [[16 * TW, 128], [TW, 8], [1, TW]])),
                        reads=[lqres], dma=True)

                ln_tile(0)
                for i in range(a_tiles):
                    par = i % 2
                    lqo = par * 8 * TW
                    lqres = ("lq", par)
                    dma_rope(st, i)
                    proj_rope(st, wA, "wA", k_fm, ("k_fm", i),
                              lambda c, i=i: slice(c * N + i * TW, c * N + (i + 1) * TW), i,
                              lq_base=lqo, lq_res=lqres, lq_pitch=16 * TW)
                    if i + 1 < a_tiles:
                        ln_tile(i + 1)
                    for s in range(4):
                        for half in (0, 1):
                            pv = P[4] if half == 0 else P[5]
                            pres = ("P", 4) if half == 0 else ("P", 5)
                            lq1 = st["lq"]
                            for j in range(8):
                                S.add("tensor", lambda e, s=s, half=half, pv=pv, j=j, lq1=lq1, lqo=lqo: e.matmul(
                                    pv[:, :], AP(lq1, lqo + j * TW + s * 128, [[16 * TW, 128], [1, 128]]),
                                    wB[:, j * 1024 + half * 512: j * 1024 + (half + 1) * 512],
                                    start=(j == 0), stop=(j == 7)),
                                    reads=[("wB",), lqres], writes=[pres])
                            blk = i * 4 + s
                            S.add("vector", lambda e, blk=blk, half=half, pv=pv: e.tensor_copy(
                                out=AP(v_tm, blk * 1040 + half * 8 * 65, [[32 * 1040, 128], [65, 8], [1, 64]]),
                                in_=AP(pv, 0, [[512, 128], [64, 8], [1, 64]])),
                                reads=[pres, ("v_tm_init",)], writes=[("v_tm", blk)])
                if "kfm" in dbg:
                    S.add("gpsimd", lambda e: e.dma_start(out=dbg["kfm"].ap(), in_=k_fm[:, :]),
                          reads=[("k_fm", i2) for i2 in range(NTW)], dma=True)
                if "vtm" in dbg:
                    S.add("gpsimd", lambda e: e.dma_start(out=dbg["vtm"].ap(), in_=v_tm[:, :]),
                          reads=[("v_tm", b) for b in range(32)] + [("v_tm_init",)], dma=True)
                S.barrier()

            dma_wT(wqT_d, wA, "wA")
            dma_wT(woT_d, wB, "wB")
            with ExitStack() as ctxS:
                st = phaseA_sub(ctxS, is_a2=True)
                P = st["P"]
                e_sb, maskd2, bc = st["e_sb"], st["maskd"], st["bc"]
                lq = st["lq"]
                q_fm = st["qfu"]
                o_fm = st["o_fm"]
                u_f32 = st["lq"].ap().bitcast(F32)
                S.add("sync", lambda e: e.dma_start(out=maskd2[:, :], in_=maskslide_d.ap()),
                      writes=[("maskd2",)], dma=True)

                class UAdder:
                    def __init__(self):
                        self.units = [[]]

                    def add(self, *aa, **kk):
                        self.units[-1].append((aa, kk))

                    def cut(self):
                        if self.units[-1]:
                            self.units.append([])

                    def take(self):
                        out = [u_ for u_ in self.units if u_]
                        self.units = [[]]
                        return out

                fill_q = []

                def pump(n=1):
                    for _ in range(n):
                        if not fill_q:
                            return
                        for aa, kk in fill_q.pop(0):
                            S.add(*aa, **kk)

                qcol = lambda c: slice(c * TW, (c + 1) * TW)
                qres = lambda c: ("qfu", c)

                def dma_lq(Jn):
                    S.add("sync", lambda e, Jn=Jn: e.dma_start(
                        out=AP(lq, 0, [[8 * TW, 128], [TW, 8], [1, TW]]),
                        in_=AP(lnx_d, Jn * TW, [[N, 128], [128 * N, 8], [1, TW]])),
                        writes=[("lq",)], dma=True)

                def enqueue_proj(Jn, c):
                    ua = UAdder()
                    proj_rope(st, wA, "wA", q_fm, None, qcol, Jn,
                              adder=ua, p_proj=0, p_swap=0, c_list=[c], out_res_fn=qres)
                    fill_q.extend(ua.take())

                dma_rope(st, 0)
                dma_lq(0)
                proj_rope(st, wA, "wA", q_fm, None, qcol, 0,
                          p_proj=0, p_swap=0, out_res_fn=qres)
                pending_div = []
                for J in range(a_tiles):
                    if J + 1 < a_tiles:
                        dma_rope(st, J + 1)
                        dma_lq(J + 1)
                    nkt = 4 * J + 4
                    for h in range(H):
                        hc = h // 2
                        hb = (h % 2) * 64
                        pO = P[2] if h % 2 == 0 else P[3]
                        pOres = ("P", 2) if h % 2 == 0 else ("P", 3)

                        def s_pair(kp, nkt=nkt, hb=hb, hc=hc, J=J):
                            pS = P[4] if kp % 2 == 0 else P[5]
                            pRes = ("P", 4) if kp % 2 == 0 else ("P", 5)
                            for half in range(2):
                                kt = 2 * kp + half
                                diag = kt >= nkt - 4
                                S.add("tensor", lambda e, kt=kt, pS=pS, diag=diag, hb=hb, hc=hc, half=half: e.matmul(
                                    pS[:, half * 512:(half + 1) * 512],
                                    k_fm[hb:hb + 64, hc * N + kt * 128: hc * N + (kt + 1) * 128],
                                    q_fm[hb:hb + 64, hc * TW:(hc + 1) * TW],
                                    start=True, stop=not diag, skip_group_check=True),
                                    reads=[("k_fm", kt // 4), ("qfu", hc)], writes=[pRes])
                                if diag:
                                    o = (kt - (nkt - 4)) * 128
                                    S.add("tensor", lambda e, pS=pS, o=o, half=half: e.matmul(
                                        pS[:, half * 512:(half + 1) * 512], eye128[:, :], maskd2[:, 384 - o: 896 - o],
                                        start=False, stop=True, skip_group_check=True),
                                        reads=[("maskd2",), ("eye128",)], writes=[pRes])
                            es = kp % 2
                            S.add("scalar", lambda e, pS=pS, es=es: e.activation(
                                out=e_sb[:, es * 2 * TW:(es + 1) * 2 * TW], in_=pS[:, :], func=AF.Exp,
                                scale=1.0 / HD, bias=consts[:, 1:2]),
                                reads=[pRes, ("c_negC",)], writes=[("e_sb", es)])

                        def pv_pair(kp, nkt=nkt, h=h, pO=pO, pOres=pOres):
                            es = kp % 2
                            for half in range(2):
                                kt = 2 * kp + half
                                S.add("tensor", lambda e, kt=kt, h=h, es=es, nkt=nkt, pO=pO, half=half: e.matmul(
                                    pO[0:65, :], v_tm[:, kt * 1040 + h * 65: kt * 1040 + h * 65 + 65],
                                    e_sb[:, es * 2 * TW + half * TW: es * 2 * TW + (half + 1) * TW],
                                    start=(kt == 0), stop=(kt == nkt - 1),
                                    skip_group_check=True),
                                    reads=[("e_sb", es), ("v_tm", kt), ("v_tm_init",)], writes=[pOres])

                        def emit_divide():
                            for aa, kk in pending_div:
                                S.add(*aa, **kk)
                            pending_div.clear()

                        nkp = nkt // 2
                        s_pair(0)
                        pump(1)
                        for kp in range(1, nkp):
                            s_pair(kp)
                            if pending_div:
                                emit_divide()
                            pump(1)
                            pv_pair(kp - 1)
                            pump(1)
                        pv_pair(nkp - 1)
                        if pending_div:
                            emit_divide()
                        pump(1)

                        p_ = h % 2
                        rbase = 64 if p_ == 0 else 32
                        rd = bc[rbase:rbase + 1, :]
                        S.add("vector", lambda e, rd=rd, pO=pO: e.reciprocal(out=rd, in_=pO[64:65, :]),
                              reads=[pOres], writes=[("r_rd", p_)])
                        pd = []
                        pd.append((("tensor", lambda e, rd=rd, rbase=rbase: e.matmul(
                            P[1][0:64, :], onesall[rbase:rbase + 1, :], rd, start=True, stop=True)),
                            dict(reads=[("r_rd", p_), ("onesall",)], writes=[("P", 1)])))
                        pd.append((("vector", lambda e: e.tensor_copy(out=bc[0:64, :], in_=P[1][0:64, :])),
                            dict(reads=[("P", 1)], writes=[("bc",), ("r_rd", 1)])))
                        if hb == 0:
                            pd.append((("vector", lambda e, hc=hc, pO=pO: e.tensor_mul(
                                out=o_fm[0:64, hc * TW:(hc + 1) * TW], in0=pO[0:64, :], in1=bc[0:64, :])),
                                dict(reads=[pOres, ("bc",)], writes=[("o_fm", hc, 0)])))
                        else:
                            pd.append((("vector", lambda e, hc=hc, pO=pO: e.tensor_mul(
                                out=o_fm[64:128, hc * TW:(hc + 1) * TW], in0=pO[0:64, :], in1=bc[0:64, :])),
                                dict(reads=[pOres, ("bc",)], writes=[("o_fm", hc, 1)])))
                        pending_div.extend((a2args + (kk,)) if False else (a2args, kk)
                                           for a2args, kk in pd)
                        pump(2)
                        if J + 1 < a_tiles and h % 2 == 1:
                            enqueue_proj(J + 1, h // 2)
                    if pending_div:
                        for aa, kk in pending_div:
                            S.add(*aa, **kk)
                        pending_div.clear()
                    pump(len(fill_q))
                    for h2 in range(2):
                        for c in range(8):
                            for j in range(8):
                                S.add("tensor", lambda e, c=c, j=j, h2=h2: e.matmul(
                                    P[2][:, 0:THF], wB[:, j * 1024 + c * 128: j * 1024 + (c + 1) * 128],
                                    o_fm[:, j * TW + h2 * THF: j * TW + (h2 + 1) * THF],
                                    start=(j == 0), stop=(j == 7), skip_group_check=True),
                                    reads=[("wB",), ("o_fm", j, 0), ("o_fm", j, 1)], writes=[("P", 2)])
                            S.add("vector", lambda e, c=c: e.tensor_scalar_add(
                                out=u_f32[:, c * THF:(c + 1) * THF], in0=P[2][:, 0:THF],
                                scalar1=bo_sb[:, c:c + 1]),
                                reads=[("P", 2), ("bo",)], writes=[("lq",)])
                        ln_half(st, None, None, "u", xbuf=u_f32, xres=("lq",), stats_ix=(4, 5))
                        T0 = J * TW + h2 * THF
                        u_dma = st["lq"].ap().bitcast(F32).rearrange("p (c t) -> p c t", c=8)
                        S.add("gpsimd", lambda e, T0=T0, u_dma=u_dma: e.dma_start(
                            out=u_dma,
                            in_=AP(x0_d, T0, [[N, 128], [128 * N, 8], [1, THF]]),
                            accum_op=ALU.add),
                            reads=[("lq",)], writes=[("lq",)], dma=True)
                        S.add("sync", lambda e, T0=T0, u_dma=u_dma: e.dma_start(
                            out=AP(x1_d, T0, [[N, 128], [128 * N, 8], [1, THF]]),
                            in_=u_dma),
                            reads=[("lq",)], dma=True)
                        if "x1" in dbg:
                            S.add("sync", lambda e, T0=T0, u_dma=u_dma: e.dma_start(
                                out=AP(dbg["x1"], T0, [[N, 128], [128 * N, 8], [1, THF]]),
                                in_=u_dma),
                                reads=[("lq",)], dma=True)
                S.barrier()

        S.barrier()
        TB = 512
        if "B" in skip:
            S.emit()
            return nc, dbg
        with ExitStack() as ctxB:
            sbB = lambda name, shape, dt: ctxB.enter_context(nc.sbuf_tensor(name, shape, dt))
            psB = lambda name, shape, dt: ctxB.enter_context(nc.psum_tensor(name, shape, dt))

            z_sb = sbB("z_sb", [128, 8 * N], F32)
            t1B = sbB("t1B", [128, TB], F32)
            t2B = sbB("t2B", [128, TB], F32)
            t3B = sbB("t3B", [128, TB], F32)
            xpb = sbB("xpb", [128, 2 * 256], F32)
            constsB = sbB("constsB", [128, 2], F32)
            colsB = sbB("colsB", [128, 24], F32)
            pa_col = sbB("pa_col", [128, 16], F32)
            padb = sbB("padb", [128, 256], BF16)

            pm_col = sbB("pm_col", [128, 16], F32)
            den_col = sbB("den_col", [128, 8], F32)

            b1T_sb = sbB("b1T_sb", [128, 88], F32)
            b2T_sb = sbB("b2T_sb", [128, 16], F32)
            lngT_sb = sbB("lngT_sb", [128, 16], F32)
            lnbT_sb = sbB("lnbT_sb", [128, 16], F32)
            for dst, srcd, res in ((b1T_sb, b1T_d, "b1T"), (b2T_sb, b2T_d, "b2T"),
                                   (lngT_sb, lngT_d, "lngT"), (lnbT_sb, lnbT_d, "lnbT")):
                S.add("sync", lambda e, dst=dst, srcd=srcd: e.dma_start(out=dst[:, :], in_=srcd.ap()),
                      writes=[(res,)], dma=True)

            PB_holder = {}

            S.add("vector", lambda e: e.memset(constsB[:, 0:1], 1e-5), writes=[("c_eps",)])
            S.add("vector", lambda e: e.memset(constsB[:, 1:2], 1e-24), writes=[("c_eps24",)])

            def z_ap(c, t0, n=TB):
                return z_sb[:, c * N + t0: c * N + t0 + n]

            def dma_xpos(tt_global, par):
                S.add("sync", lambda e, tt_global=tt_global, par=par: e.dma_start(
                    out=xpb[:, par * 256:(par + 1) * 256],
                    in_=AP(xpos_d, tt_global * 128 * 256, [[256, 128], [1, 256]])),
                    writes=[("xpb", par)], dma=True)

            def ln_statsB(src_fn, src_reads, rowsB, xbB, x2bB, src_is_bf16):
                PB = PB_holder["PB"]
                if "lnall" in SKIP_SET:
                    return
                for c in range(8):
                    sl = c % 2
                    ss_ = slice(sl * TB, (sl + 1) * TB)
                    s_ap = src_fn(c)
                    if src_is_bf16:
                        xv, xres = s_ap, list(src_reads)
                    else:
                        S.add("vector", lambda e, s_ap=s_ap, ss_=ss_: e.tensor_copy(out=xbB[:, ss_], in_=s_ap),
                              reads=list(src_reads), writes=[("xbB", sl)])
                        xv, xres = xbB[:, ss_], [("xbB", sl)]
                    S.add("scalar", lambda e, s_ap=s_ap, ss_=ss_: e.activation(out=x2bB[:, ss_], in_=s_ap, func=AF.Square),
                          reads=list(src_reads), writes=[("x2bB", sl)])
                    S.add("tensor", lambda e, xv=xv, c=c, PB=PB: e.matmul(
                        PB[2][0:1, 0:TB], onescol[:, :], xv, start=(c == 0), stop=(c == 7)),
                        reads=xres + [("onescol",)], writes=[("B", 2)])
                    S.add("tensor", lambda e, ss_=ss_, c=c, PB=PB: e.matmul(
                        PB[3][0:1, 0:TB], onescol[:, :], x2bB[:, ss_], start=(c == 0), stop=(c == 7)),
                        reads=[("x2bB", sl), ("onescol",)], writes=[("B", 3)])
                if "lnrows" in SKIP_SET:
                    return
                mu = rowsB[:, 0 * TB:1 * TB]
                m2 = rowsB[:, 1 * TB:2 * TB]
                rr = m2
                mur = rowsB[:, 2 * TB:3 * TB]
                S.add("vector", lambda e, mu=mu: e.tensor_scalar_mul(out=mu, in0=PB[2][0:1, 0:TB], scalar1=1.0 / D),
                      reads=[("B", 2)], writes=[("rB_mu",)])
                S.add("vector", lambda e, mu=mu, m2=m2: e.scalar_tensor_tensor(out=m2, in0=mu, scalar=-1.0, in1=mu,
                                                                 op0=ALU.mult, op1=ALU.mult),
                      reads=[("rB_mu",)], writes=[("rB_m2",)])
                S.add("vector", lambda e, m2=m2: e.scalar_tensor_tensor(out=m2, in0=PB[3][0:1, 0:TB], scalar=1.0 / D,
                                                                 in1=m2, op0=ALU.mult, op1=ALU.add),
                      reads=[("B", 3), ("rB_m2",)], writes=[("rB_var",)])
                S.add("scalar", lambda e, m2=m2, rr=rr: e.activation(out=rr, in_=m2, func=AF.Ln, bias=constsB[0:1, 0:1]),
                      reads=[("rB_var",), ("c_eps",)], writes=[("rB_lnv",)])
                S.add("scalar", lambda e, rr=rr: e.activation(out=rr, in_=rr, func=AF.Exp, scale=-0.5),
                      reads=[("rB_lnv",)], writes=[("rB_r",)])
                S.add("vector", lambda e, mu=mu, rr=rr, mur=mur: e.tensor_mul(out=mur, in0=mu, in1=rr),
                      reads=[("rB_mu",), ("rB_r",)], writes=[("rB_mur",)])
                S.add("tensor", lambda e, rr=rr, PB=PB: e.matmul(PB[0][:, 0:TB], ones128[:, :], rr, start=True, stop=True),
                      reads=[("rB_r",), ("ones128",)], writes=[("B", 0)])
                S.add("tensor", lambda e, mur=mur, PB=PB: e.matmul(PB[1][:, 0:TB], ones128[:, :], mur, start=True, stop=True),
                      reads=[("rB_mur",), ("ones128",)], writes=[("B", 1)])


            if "binit" in skip:
                b_init_tiles = 0
            else:
                b_init_tiles = b_tiles
            with ExitStack() as ctxI:
                sbI = lambda name, shape, dt: ctxI.enter_context(nc.sbuf_tensor(name, shape, dt))
                rowsB = sbI("rowsB", [1, 3 * TB], F32)
                xbB = sbI("xbB", [128, 2 * TB], BF16)
                x2bB = sbI("x2bB", [128, 2 * TB], BF16)
                xF = sbI("xF", [128, 8 * TB], F32)
                PB_holder["PB"] = [ctxI.enter_context(nc.psum_tensor(f"PBI{i}", [128, 512], F32)) for i in range(4)]
                PB = PB_holder["PB"]
                for i in range(b_init_tiles):
                    S.add("sync", lambda e, i=i: e.dma_start(
                        out=AP(xF, 0, [[8 * TB, 128], [TB, 8], [1, TB]]),
                        in_=AP(x1_d, i * TB, [[N, 128], [128 * N, 8], [1, TB]])),
                        writes=[("xF",)], dma=True)
                    ln_statsB(lambda c: xF[:, c * TB:(c + 1) * TB], [("xF",)], rowsB, xbB, x2bB, False)
                    if "lnapply" in SKIP_SET:
                        continue
                    for c in range(8):
                        S.add("vector", lambda e, c=c, PB=PB: e.tensor_mul(out=t1B[:, :], in0=xF[:, c * TB:(c + 1) * TB],
                                                                    in1=PB[0][:, 0:TB]),
                              reads=[("xF",), ("B", 0)], writes=[("t1B",)])
                        S.add("vector", lambda e, c=c, i=i, PB=PB: e.tensor_sub(out=z_ap(c, i * TB), in0=t1B[:, :],
                                                                         in1=PB[1][:, 0:TB]),
                              reads=[("t1B",), ("B", 1)], writes=[("z", i)])
            if "zinit" in dbg:
                S.add("sync", lambda e: e.dma_start(out=dbg["zinit"].ap(), in_=z_sb[:, :]),
                      reads=[("z", i2) for i2 in range(8)], dma=True)
            S.barrier()

            nlayers = 0 if b_stop == "init" else nlayers
            for l in range(nlayers):
                with ExitStack() as ctxT:
                    sbT = lambda name, shape, dt: ctxT.enter_context(nc.sbuf_tensor(f"{name}_{l}", shape, dt))
                    psT = lambda name, shape, dt: ctxT.enter_context(nc.psum_tensor(f"{name}_{l}", shape, dt))
                    wqkv_sb = sbT("wqkv_sb", [128, 8 * 1536], BF16)
                    wo_sb = sbT("wo_sb", [128, 4 * 1024], BF16)
                    kT = sbT("kT", [128, 4 * 1024], BF16)
                    vT = sbT("vT", [128, 8 * 512], BF16)
                    maskB = sbT("maskB", [128, 640 + 1280], BF16)
                    w0off = [640, 1152, 1536, 1792]
                    qsbc = sbT("qsbc", [128, 512], BF16)
                    ksbc = sbT("ksbc", [128, 512], BF16)
                    zb = sbT("zb", [128, 8 * 128], BF16)
                    q_sb = sbT("q_sb", [128, 512], BF16)
                    k_sb = sbT("k_sb", [128, 512], BF16)
                    sqB = sbT("sqB", [128, 512], BF16)
                    qT = sbT("qT", [128, 512], BF16)
                    eT2 = sbT("eT2", [128, 2 * 640], BF16)
                    e_sb2 = sbT("e_sb2", [128, 2 * 640], BF16)
                    o_fmB = sbT("o_fmB", [128, 4 * 512], BF16)
                    pS2 = [psT("pSa", [128, 640], F32), psT("pSb", [128, 640], F32)]
                    pO_ = psT("pO", [128, 512], F32)
                    pZ_ = psT("pZ", [128, 512], F32)
                    pT2 = [psT("pTa", [128, 1024], BF16), psT("pTb", [128, 1024], BF16)]

                    S.add("sync", lambda e, l=l: e.dma_start(
                        out=AP(wqkv_sb, 0, [[8 * 1536, 128], [1536, 8], [1, 1536]]),
                        in_=AP(wqkvb_d, l * 1024 * 1536, [[1536, 128], [128 * 1536, 8], [1, 1536]])),
                        reads=[("wqkvC", r0) for r0 in range(0, 2048, 128)],
                        writes=[("wqkvB",)], dma=True)
                    S.add("sync", lambda e, l=l: e.dma_start(
                        out=AP(wo_sb, 0, [[4 * 1024, 128], [1024, 4], [1, 1024]]),
                        in_=AP(wob_d, l * 512 * 1024, [[1024, 128], [128 * 1024, 4], [1, 1024]])),
                        reads=[("woC", r0) for r0 in range(0, 1024, 128)],
                        writes=[("woB",)], dma=True)
                    S.add("sync", lambda e: e.dma_start(out=maskB[:, :], in_=maskB_d.ap()),
                          writes=[("maskB",)], dma=True)
                    S.add("gpsimd", lambda e, l=l: e.dma_start(
                        out=qsbc[:, :], in_=AP(qsbc_d, l * 128 * 512, [[512, 128], [1, 512]])),
                        writes=[("qsbc",)], dma=True)
                    S.add("gpsimd", lambda e, l=l: e.dma_start(
                        out=ksbc[:, :], in_=AP(ksbc_d, l * 128 * 512, [[512, 128], [1, 512]])),
                        writes=[("ksbc",)], dma=True)
                    S.add("vector", lambda e: e.memset(kT[:, :], 0.0),
                          writes=[("kT", s2, b2) for s2 in (0, 1) for b2 in range(4)])
                    S.add("vector", lambda e: e.memset(vT[:, :], 0.0),
                          writes=[("vT", s2, b2) for s2 in (0, 1) for b2 in range(4)])
                    S.add("vector", lambda e: e.memset(pa_col[:, :], 0.0), writes=[("pa", 0), ("pa", 1)])

                    for w in range(nwindows):
                        W0 = w * TB
                        S.add("gpsimd", lambda e, w=w, W0=W0: e.tensor_copy(
                            out=pa_col[:, (w % 2) * 8:(w % 2) * 8 + 8],
                            in_=AP(z_sb, W0 + TB - 1, [[8 * N, 128], [N, 8], [1, 1]])),
                            reads=[("z", w)], writes=[("pa", w % 2)])
                        for tt in range(4):
                            T0 = W0 + tt * 128
                            par = (w * 4 + tt) % 2
                            dma_xpos(w * 4 + tt, par)
                            S.add("gpsimd", lambda e, T0=T0: e.tensor_copy(
                                out=zb[:, 0:512],
                                in_=AP(z_sb, T0, [[8 * N, 128], [N, 4], [1, 128]])),
                                reads=[("z", w)], writes=[("zb",)])
                            if tt == 0:
                                S.add("gpsimd", lambda e, w=w: e.tensor_copy(
                                    out=AP(zb, 512, [[8 * 128, 128], [128, 4], [1, 1]]),
                                    in_=pa_col[:, ((w + 1) % 2) * 8 + 4:((w + 1) % 2) * 8 + 8]),
                                    reads=[("pa", (w + 1) % 2)], writes=[("zb",)])
                                S.add("gpsimd", lambda e, T0=T0: e.tensor_copy(
                                    out=AP(zb, 513, [[8 * 128, 128], [128, 4], [1, 127]]),
                                    in_=AP(z_sb, 4 * N + T0, [[8 * N, 128], [N, 4], [1, 127]])),
                                    reads=[("z", w)], writes=[("zb",)])
                            else:
                                S.add("gpsimd", lambda e, T0=T0: e.tensor_copy(
                                    out=zb[:, 512:1024],
                                    in_=AP(z_sb, 4 * N + T0 - 1, [[8 * N, 128], [N, 4], [1, 128]])),
                                    reads=[("z", w)], writes=[("zb",)])
                            for j in range(8):
                                for qi, (bidx, col0) in enumerate(((0, 0), (1, 512), (2, 1024))):
                                    pb = pS2[0] if bidx < 2 else pS2[1]
                                    poff = 0 if bidx % 2 == 0 else 0
                                    tgt = (pS2[0], ("pSx", 0)) if bidx == 0 else (
                                          (pS2[1], ("pSx", 1)) if bidx == 1 else (pO_, ("pOx",)))
                                    S.add("tensor", lambda e, j=j, col0=col0, tgt=tgt: e.matmul(
                                        tgt[0][:, 0:512], zb[:, j * 128:(j + 1) * 128],
                                        wqkv_sb[:, j * 1536 + col0: j * 1536 + col0 + 512],
                                        start=(j == 0), stop=(j == 7), skip_group_check=True),
                                        reads=[("zb",), ("wqkvB",)], writes=[tgt[1]])
                            S.add("scalar", lambda e: e.activation(out=q_sb[:, :], in_=pS2[0][:, 0:512], func=AF.Copy),
                                  reads=[("pSx", 0)], writes=[("q_sb",)])
                            S.add("scalar", lambda e: e.activation(out=k_sb[:, :], in_=pS2[1][:, 0:512], func=AF.Copy),
                                  reads=[("pSx", 1)], writes=[("k_sb",)])
                            S.add("scalar", lambda e, w=w, tt=tt: e.activation(
                                out=vT[:, ((w % 2) * 4 + tt) * 512: ((w % 2) * 4 + tt + 1) * 512],
                                in_=pO_[:, 0:512], func=AF.Copy),
                                reads=[("pOx",)], writes=[("vT", w % 2, tt)])
                            for which, buf, bres, rinv_off, scbc, scres, toff in (
                                    ("q", q_sb, ("q_sb",), 8, qsbc, ("qsbc",), 0),
                                    ("k", k_sb, ("k_sb",), 16, ksbc, ("ksbc",), 128)):
                                S.add("scalar", lambda e, buf=buf, bres=bres: e.activation(out=sqB[:, :], in_=buf[:, :], func=AF.Square),
                                      reads=[bres], writes=[("sqB",)])
                                S.add("vector", lambda e: e.reduce_sum(
                                    out=colsB[:, 0:8], in_=AP(sqB, 0, [[512, 128], [64, 8], [1, 64]]), axis=AX.X),
                                    reads=[("sqB",)], writes=[("ssq",)])
                                S.add("scalar", lambda e, rinv_off=rinv_off: e.activation(
                                    out=colsB[:, rinv_off:rinv_off + 8], in_=colsB[:, 0:8], func=AF.Ln,
                                    bias=constsB[:, 1:2]),
                                    reads=[("ssq",), ("c_eps24",)], writes=[("rinv", which)])
                                S.add("scalar", lambda e, rinv_off=rinv_off: e.activation(
                                    out=colsB[:, rinv_off:rinv_off + 8], in_=colsB[:, rinv_off:rinv_off + 8],
                                    func=AF.Exp, scale=-0.5),
                                    reads=[("rinv", which)], writes=[("rinv", which)])
                                ek = "vector"
                                S.add(ek, lambda e, buf=buf, bres=bres, rinv_off=rinv_off: e.tensor_mul(
                                    out=t1B[:, :], in0=buf[:, :],
                                    in1=AP(colsB, rinv_off, [[24, 128], [1, 8], [0, 64]])),
                                    reads=[bres, ("rinv", which)], writes=[("t1B",)])
                                S.add(ek, lambda e, scbc=scbc, scres=scres: e.tensor_mul(
                                    out=t1B[:, :], in0=t1B[:, :], in1=scbc[:, :]),
                                    reads=[("t1B",), scres], writes=[("t1B",)])
                                S.add(ek, lambda e, par=par, toff=toff: e.tensor_mul(
                                    out=t2B[:, :], in0=t1B[:, :],
                                    in1=AP(xpb, par * 256 + toff, [[512, 128], [0, 8], [1, 64]])),
                                    reads=[("t1B",), ("xpb", par)], writes=[("t2B",)])
                                S.add(ek, lambda e, par=par, toff=toff: e.tensor_mul(
                                    out=t3B[:, :],
                                    in0=AP(t1B, 32, [[512, 128], [64, 8], [-32, 2], [1, 32]]),
                                    in1=AP(xpb, par * 256 + toff + 64, [[512, 128], [0, 8], [1, 64]])),
                                    reads=[("t1B",), ("xpb", par)], writes=[("t3B",)])
                                S.add(ek, lambda e, buf=buf, bres=bres: e.tensor_add(out=buf[:, :], in0=t2B[:, :], in1=t3B[:, :]),
                                      reads=[("t2B",), ("t3B",)], writes=[bres])
                            if "bqr" in dbg and l == 0 and w == 0 and tt == 0:
                                S.add("gpsimd", lambda e: e.dma_start(out=dbg["bqr"].ap(), in_=q_sb[:, :]),
                                      reads=[("q_sb",)], dma=True)
                                S.add("gpsimd", lambda e: e.dma_start(out=dbg["bkr"].ap(), in_=k_sb[:, :]),
                                      reads=[("k_sb",)], dma=True)
                            for hp in range(4):
                                S.add("tensor", lambda e, hp=hp: e.transpose(
                                    pT2[0][:, hp * 128:(hp + 1) * 128], q_sb[:, hp * 128:(hp + 1) * 128], eye128[:, :]),
                                    reads=[("q_sb",), ("eye128",)], writes=[("pT", 0)])
                                S.add("tensor", lambda e, hp=hp: e.transpose(
                                    pT2[0][:, 512 + hp * 128: 512 + (hp + 1) * 128], k_sb[:, hp * 128:(hp + 1) * 128], eye128[:, :]),
                                    reads=[("k_sb",), ("eye128",)], writes=[("pT", 0)])
                            S.add("vector", lambda e: e.tensor_copy(out=qT[:, :], in_=pT2[0][:, 0:512]),
                                  reads=[("pT", 0)], writes=[("qT",)])
                            S.add("vector", lambda e, w=w, tt=tt: e.tensor_copy(
                                out=AP(kT, (w % 2) * 512 + tt * 128, [[4 * 1024, 128], [1024, 4], [1, 128]]),
                                in_=pT2[0][:, 512:1024]),
                                reads=[("pT", 0)], writes=[("kT", w % 2, tt)])
                            spans = []
                            for j5 in range(5):
                                stp = tt + j5
                                slot = ((w + 1) % 2) if stp < 4 else (w % 2)
                                blk = stp if stp < 4 else stp - 4
                                spans.append((j5, slot, blk))

                            runs = []
                            for j5, slot, blk in spans:
                                if runs and runs[-1][1] == slot and runs[-1][2] + runs[-1][3] == blk \
                                        and runs[-1][0] + runs[-1][3] != 4:
                                    runs[-1] = (runs[-1][0], slot, runs[-1][2], runs[-1][3] + 1)
                                else:
                                    runs.append((j5, slot, blk, 1))
                            wide = []
                            for j5, slot, blk, ln_ in runs:
                                while ln_ > 0:
                                    lim = 4 - j5 if j5 < 4 else 1
                                    take = min(ln_, lim)
                                    wide.append((j5, slot, blk, take))
                                    j5 += take
                                    blk += take
                                    ln_ -= take

                            def stage_S(h):
                                hp, parity = h // 2, h % 2
                                hb = parity * 64
                                pS = pS2[h % 2]
                                pRes = ("pSx", h % 2)
                                seen = set()
                                for j5, slot, blk, ln_ in wide:
                                    bank = 0 if j5 < 4 else 1
                                    st_ = bank not in seen
                                    seen.add(bank)
                                    S.add("tensor", lambda e, j5=j5, slot=slot, blk=blk, ln_=ln_, st_=st_, hp=hp, hb=hb, pS=pS: e.matmul(
                                        pS[:, j5 * 128:(j5 + ln_) * 128],
                                        qT[hb:hb + 64, hp * 128:(hp + 1) * 128],
                                        kT[hb:hb + 64, hp * 1024 + slot * 512 + blk * 128: hp * 1024 + slot * 512 + (blk + ln_) * 128],
                                        start=st_, stop=False, skip_group_check=True),
                                        reads=[("qT",)] + [("kT", slot, blk + q_) for q_ in range(ln_)], writes=[pRes])
                                S.add("tensor", lambda e, pS=pS, w=w: e.matmul(
                                    pS[:, 0:512], eye128[:, :], maskB[:, 0:512],
                                    start=False, stop=(w != 0), skip_group_check=True),
                                    reads=[("maskB",), ("eye128",)], writes=[pRes])
                                S.add("tensor", lambda e, pS=pS: e.matmul(
                                    pS[:, 512:640], eye128[:, :], maskB[:, 512:640],
                                    start=False, stop=True, skip_group_check=True),
                                    reads=[("maskB",), ("eye128",)], writes=[pRes])
                                if w == 0:
                                    cut = 512 - 128 * tt
                                    S.add("tensor", lambda e, cut=cut, tt=tt, pS=pS: e.matmul(
                                        pS[:, 0:cut], eye128[:, :], maskB[:, w0off[tt]: w0off[tt] + cut],
                                        start=False, stop=True, skip_group_check=True),
                                        reads=[("maskB",), ("eye128",)], writes=[pRes])

                            def stage_soft1(h):
                                pS = pS2[h % 2]
                                pRes = ("pSx", h % 2)
                                dc = (h % 2) * 4
                                S.add("vector", lambda e, pS=pS, dc=dc: e.reduce_max(
                                    out=den_col[:, dc:dc + 1], in_=pS[:, 0:640], axis=AX.X),
                                    reads=[pRes], writes=[("mx", h % 2)])
                                S.add("vector", lambda e, dc=dc: e.tensor_scalar_mul(
                                    out=den_col[:, dc + 1:dc + 2], in0=den_col[:, dc:dc + 1], scalar1=-8.0),
                                    reads=[("mx", h % 2)], writes=[("mx8", h % 2)])

                            def stage_exp(h):
                                pS = pS2[h % 2]
                                pRes = ("pSx", h % 2)
                                dc = (h % 2) * 4
                                eo = (h % 2) * 640
                                S.add("scalar", lambda e, pS=pS, dc=dc, eo=eo: e.activation(
                                    out=e_sb2[:, eo:eo + 640], in_=pS[:, 0:640], func=AF.Exp, scale=8.0,
                                    bias=den_col[:, dc + 1:dc + 2], accum_out=den_col[:, dc + 2:dc + 3]),
                                    reads=[pRes, ("mx8", h % 2)], writes=[("e1", h % 2), ("d1", h % 2)])

                            def stage_soft2(h):
                                dc = (h % 2) * 4
                                eo = (h % 2) * 640
                                S.add("vector", lambda e, dc=dc: e.reciprocal(
                                    out=den_col[:, dc + 3:dc + 4], in_=den_col[:, dc + 2:dc + 3]),
                                    reads=[("d1", h % 2)], writes=[("rv", h % 2)])
                                S.add("vector", lambda e, dc=dc, eo=eo: e.tensor_scalar_mul(
                                    out=e_sb2[:, eo:eo + 640], in0=e_sb2[:, eo:eo + 640], scalar1=den_col[:, dc + 3:dc + 4]),
                                    reads=[("e1", h % 2), ("rv", h % 2)], writes=[("e1", h % 2)])

                            def stage_T(h):
                                eo = (h % 2) * 640
                                pT = pT2[h % 2]
                                for j5 in range(5):
                                    S.add("tensor", lambda e, j5=j5, eo=eo, pT=pT: e.transpose(
                                        pT[:, j5 * 128:(j5 + 1) * 128], e_sb2[:, eo + j5 * 128: eo + (j5 + 1) * 128], eye128[:, :]),
                                        reads=[("e1", h % 2), ("eye128",)], writes=[("pT", h % 2)])
                                S.add("vector", lambda e, eo=eo, pT=pT: e.tensor_copy(
                                    out=eT2[:, eo:eo + 640], in_=pT[:, 0:640]),
                                    reads=[("pT", h % 2)], writes=[("eT", h % 2)])

                            def stage_PV(h):
                                hp, parity = h // 2, h % 2
                                hb = parity * 64
                                eo = (h % 2) * 640
                                for j5, slot, blk in spans:
                                    kwargs = {}
                                    if parity == 1:
                                        kwargs["tile_position"] = (0, 64)
                                    S.add("tensor", lambda e, j5=j5, slot=slot, blk=blk, h=h, hb=hb, tt=tt, eo=eo, kwargs=kwargs: e.matmul(
                                        pO_[hb:hb + 64, tt * 128:(tt + 1) * 128],
                                        vT[:, (slot * 4 + blk) * 512 + h * 64: (slot * 4 + blk) * 512 + (h + 1) * 64],
                                        eT2[:, eo + j5 * 128: eo + (j5 + 1) * 128],
                                        start=(j5 == 0), stop=(j5 == 4),
                                        skip_group_check=True, **kwargs),
                                        reads=[("eT", h % 2), ("vT", slot, blk)], writes=[("pOx",)])
                                S.add("scalar", lambda e, hp=hp, tt=tt, hb=hb: e.activation(
                                    out=o_fmB[hb:hb + 64, hp * 512 + tt * 128: hp * 512 + (tt + 1) * 128],
                                    in_=pO_[hb:hb + 64, tt * 128:(tt + 1) * 128], func=AF.Copy),
                                    reads=[("pOx",)], writes=[("o_fmB", hp, tt, hb)])

                            stages = [stage_S, stage_soft1, stage_exp, stage_soft2, stage_T, stage_PV]
                            nst = len(stages)
                            for step in range(8 + nst - 1):
                                for si in range(nst - 1, -1, -1):
                                    h = step - si
                                    if 0 <= h < 8:
                                        stages[si](h)
                        if "bofm" in dbg and l == 0 and w == 0:
                            S.add("gpsimd", lambda e: e.dma_start(out=dbg["bofm"].ap(), in_=o_fmB[:, :]),
                                  reads=[("o_fmB", hp2, tt2, b64) for hp2 in range(4) for tt2 in range(4) for b64 in (0, 64)], dma=True)
                        for f in range(8):
                            for j in range(4):
                                S.add("tensor", lambda e, f=f, j=j: e.matmul(
                                    pZ_[:, 0:512], wo_sb[:, j * 1024 + f * 128: j * 1024 + (f + 1) * 128],
                                    o_fmB[:, j * 512:(j + 1) * 512],
                                    start=(j == 0), stop=(j == 3), skip_group_check=True),
                                    reads=[("woB",)] + [("o_fmB", j, tt2, b64) for tt2 in range(4) for b64 in (0, 64)], writes=[("pZx",)])
                            S.add("vector", lambda e, f=f, W0=W0: e.tensor_add(
                                out=z_ap(f, W0), in0=z_ap(f, W0), in1=pZ_[:, 0:512]),
                                reads=[("pZx",), ("z", w)], writes=[("z", w)])
                if ("z0a" in dbg and l == 0) or ("z1a" in dbg and l == 1):
                    key = "z0a" if l == 0 else "z1a"
                    S.add("sync", lambda e, key=key: e.dma_start(out=dbg[key].ap(), in_=z_sb[:, :]),
                          reads=[("z", i2) for i2 in range(8)], dma=True)
                S.barrier()
                if not do_mlp:
                    continue
                with ExitStack() as ctxM:
                    sbM = lambda name, shape, dt: ctxM.enter_context(nc.sbuf_tensor(f"{name}_{l}", shape, dt))
                    psM = lambda name: ctxM.enter_context(nc.psum_tensor(f"{name}_{l}", [128, 512], F32))
                    rowsB = sbM("rowsM", [1, 3 * TB], F32)
                    x2bB = sbM("x2bM", [128, 2 * TB], BF16)
                    hB = sbM("hB", [128, 8 * TB], BF16)
                    lnh = sbM("lnh", [128, 8 * TB], BF16)
                    glu = sbM("glu", [128, 22 * 512], BF16)
                    w1a = sbM("w1a", [128, 2 * 1024], BF16)
                    w1g = sbM("w1g", [128, 2 * 1024], BF16)
                    w2b = sbM("w2b", [128, 4 * 1024], BF16)
                    a_sb = sbM("a_sb", [128, 512], F32)
                    a2_sb = sbM("a2_sb", [128, 512], F32)
                    
                    gg = sbM("gg", [128, 2 * 512], BF16)
                    PBM = [psM(f"PBM{i}") for i in range(8)]
                    PB_holder["PB"] = PBM[0:4]

                    S.add("vector", lambda e: e.memset(pm_col[:, :], 0.0), writes=[("pm", 0), ("pm", 1)])
                    for i in range(8):
                        T0 = i * TB
                        S.add("gpsimd", lambda e, i=i, T0=T0: e.tensor_copy(
                            out=pm_col[:, (i % 2) * 8:(i % 2) * 8 + 8],
                            in_=AP(z_sb, T0 + TB - 1, [[8 * N, 128], [N, 8], [1, 1]])),
                            reads=[("z", i)], writes=[("pm", i % 2)])
                        S.add("gpsimd", lambda e, T0=T0: e.tensor_copy(
                            out=AP(hB, 0, [[8 * TB, 128], [TB, 4], [1, TB]]),
                            in_=AP(z_sb, T0, [[8 * N, 128], [N, 4], [1, TB]])),
                            reads=[("z", i)], writes=[("hB",)])
                        S.add("gpsimd", lambda e, i=i: e.tensor_copy(
                            out=AP(hB, 4 * TB, [[8 * TB, 128], [TB, 4], [1, 1]]),
                            in_=pm_col[:, ((i + 1) % 2) * 8 + 4:((i + 1) % 2) * 8 + 8]),
                            reads=[("pm", (i + 1) % 2)], writes=[("hB",)])
                        S.add("gpsimd", lambda e, T0=T0: e.tensor_copy(
                            out=AP(hB, 4 * TB + 1, [[8 * TB, 128], [TB, 4], [1, TB - 1]]),
                            in_=AP(z_sb, 4 * N + T0, [[8 * N, 128], [N, 4], [1, TB - 1]])),
                            reads=[("z", i)], writes=[("hB",)])
                        ln_statsB(lambda c: hB[:, c * TB:(c + 1) * TB], [("hB",)], rowsB, None, x2bB, True)
                        for c in range(8):
                            S.add("vector", lambda e, c=c: e.tensor_mul(
                                out=t1B[:, :], in0=hB[:, c * TB:(c + 1) * TB], in1=PBM[0][:, 0:TB]),
                                reads=[("hB",), ("B", 0)], writes=[("t1B",)])
                            S.add("vector", lambda e: e.tensor_sub(out=t1B[:, :], in0=t1B[:, :], in1=PBM[1][:, 0:TB]),
                                  reads=[("t1B",), ("B", 1)], writes=[("t1B",)])
                            S.add("vector", lambda e, c=c, l=l: e.tensor_scalar(
                                out=lnh[:, c * TB:(c + 1) * TB], in0=t1B[:, :],
                                scalar1=lngT_sb[:, l * 8 + c: l * 8 + c + 1],
                                scalar2=lnbT_sb[:, l * 8 + c: l * 8 + c + 1],
                                op0=ALU.mult, op1=ALU.add),
                                reads=[("t1B",), ("lngT",), ("lnbT",)], writes=[("lnh", c)])
                        for cc in range(22):
                            wc = 128 if cc < 21 else 42
                            par = cc % 2
                            pa_ = PBM[4 + (cc % 2)]
                            pg_ = PBM[6 + (cc % 2)]
                            ra_, rg_ = ("B", 4 + (cc % 2)), ("B", 6 + (cc % 2))
                            epar = cc % 2
                            S.add("sync", lambda e, cc=cc, wc=wc, par=par, l=l: e.dma_start(
                                out=AP(w1a, par * 1024, [[2 * 1024, 128], [128, 8], [1, wc]]),
                                in_=AP(w1b_d, l * 1024 * 5460 + cc * 128, [[5460, 128], [128 * 5460, 8], [1, wc]])),
                                writes=[("w1a", par)], dma=True)
                            S.add("sync", lambda e, cc=cc, wc=wc, par=par, l=l: e.dma_start(
                                out=AP(w1g, par * 1024, [[2 * 1024, 128], [128, 8], [1, wc]]),
                                in_=AP(w1b_d, l * 1024 * 5460 + 2730 + cc * 128, [[5460, 128], [128 * 5460, 8], [1, wc]])),
                                writes=[("w1g", par)], dma=True)
                            for j in range(8):
                                S.add("tensor", lambda e, wc=wc, par=par, j=j, pa_=pa_: e.matmul(
                                    pa_[0:wc, 0:512], w1a[:, par * 1024 + j * 128: par * 1024 + j * 128 + wc],
                                    lnh[:, j * TB:(j + 1) * TB], start=(j == 0), stop=(j == 7), skip_group_check=True),
                                    reads=[("w1a", par), ("lnh", j)], writes=[ra_])
                                S.add("tensor", lambda e, wc=wc, par=par, j=j, pg_=pg_: e.matmul(
                                    pg_[0:wc, 0:512], w1g[:, par * 1024 + j * 128: par * 1024 + j * 128 + wc],
                                    lnh[:, j * TB:(j + 1) * TB], start=(j == 0), stop=(j == 7), skip_group_check=True),
                                    reads=[("w1g", par), ("lnh", j)], writes=[rg_])
                            abuf = a_sb if epar == 0 else a2_sb
                            S.add("vector", lambda e, cc=cc, wc=wc, l=l, pa_=pa_, abuf=abuf: e.tensor_scalar_add(
                                out=abuf[0:wc, :], in0=pa_[0:wc, 0:512],
                                scalar1=b1T_sb[0:wc, l * 44 + cc: l * 44 + cc + 1]),
                                reads=[ra_, ("b1T",)], writes=[("a_sb", epar)])
                            S.add("scalar", lambda e, cc=cc, wc=wc, l=l, pg_=pg_, epar=epar: e.activation(
                                out=gg[0:wc, epar * 512:epar * 512 + 512], in_=pg_[0:wc, 0:512], func=AF.Gelu,
                                bias=b1T_sb[0:wc, l * 44 + 22 + cc: l * 44 + 22 + cc + 1]),
                                reads=[rg_, ("b1T",)], writes=[("gg", epar)])
                            S.add("vector", lambda e, cc=cc, wc=wc, epar=epar, abuf=abuf: e.tensor_mul(
                                out=glu[0:wc, cc * 512:(cc + 1) * 512],
                                in0=abuf[0:wc, :],
                                in1=gg[0:wc, epar * 512:epar * 512 + 512]),
                                reads=[("a_sb", epar), ("gg", epar)], writes=[("glu", cc)])
                        for cc in range(22):
                            wc = 128 if cc < 21 else 42
                            par = cc % 4
                            S.add("sync", lambda e, cc=cc, wc=wc, par=par, l=l: e.dma_start(
                                out=AP(w2b, par * 1024, [[4 * 1024, wc], [1, 1024]]),
                                in_=AP(w2b_d, l * 2730 * 1024 + cc * 128 * 1024, [[1024, wc], [1, 1024]])),
                                writes=[("w2b", par)], dma=True)
                            for f in range(8):
                                S.add("tensor", lambda e, cc=cc, wc=wc, par=par, f=f: e.matmul(
                                    PBM[f][:, 0:512], w2b[0:wc, par * 1024 + f * 128:par * 1024 + (f + 1) * 128],
                                    glu[0:wc, cc * 512:(cc + 1) * 512],
                                    start=(cc == 0), stop=(cc == 21), skip_group_check=True),
                                    reads=[("w2b", par), ("glu", cc)], writes=[("B", f)])
                        for f in range(8):
                            S.add("vector", lambda e, f=f, T0=T0, l=l: e.scalar_tensor_tensor(
                                out=z_ap(f, T0), in0=PBM[f][:, 0:512],
                                scalar=b2T_sb[:, l * 8 + f: l * 8 + f + 1],
                                in1=z_ap(f, T0), op0=ALU.add, op1=ALU.add),
                                reads=[("B", f), ("b2T",), ("z", i)], writes=[("z", i)])
                if ("z0f" in dbg and l == 0) or ("z1f" in dbg and l == 1):
                    key = "z0f" if l == 0 else "z1f"
                    S.add("sync", lambda e, key=key: e.dma_start(out=dbg[key].ap(), in_=z_sb[:, :]),
                          reads=[("z", i2) for i2 in range(8)], dma=True)
                S.barrier()

            if "bfinal" in skip:
                b_tiles = 0
            with ExitStack() as ctxF:
                sbF = lambda name, shape, dt: ctxF.enter_context(nc.sbuf_tensor(name, shape, dt))
                rowsB = sbF("rowsF", [1, 3 * TB], F32)
                xbB = sbF("xbF", [128, 2 * TB], BF16)
                x2bB = sbF("x2bF", [128, 2 * TB], BF16)
                xF = sbF("xFF", [128, 8 * TB], F32)
                PB_holder["PB"] = [ctxF.enter_context(nc.psum_tensor(f"PBF{i}", [128, 512], F32)) for i in range(4)]
                PB = PB_holder["PB"]
                for i in range(b_tiles):
                    S.add("sync", lambda e, i=i: e.dma_start(
                        out=AP(xF, 0, [[8 * TB, 128], [TB, 8], [1, TB]]),
                        in_=AP(x1_d, i * TB, [[N, 128], [128 * N, 8], [1, TB]])),
                        writes=[("xF",)], dma=True)
                    ln_statsB(lambda c, i=i: z_ap(c, i * TB), [("z", i)], rowsB, xbB, x2bB, False)
                    for c in range(8):
                        S.add("vector", lambda e, c=c, i=i, PB=PB: e.tensor_mul(
                            out=t1B[:, :], in0=z_ap(c, i * TB), in1=PB[0][:, 0:TB]),
                            reads=[("z", i), ("B", 0)], writes=[("t1B",)])
                        S.add("vector", lambda e, PB=PB: e.tensor_sub(out=t1B[:, :], in0=t1B[:, :], in1=PB[1][:, 0:TB]),
                              reads=[("t1B",), ("B", 1)], writes=[("t1B",)])
                        S.add("vector", lambda e, c=c: e.tensor_add(
                            out=xF[:, c * TB:(c + 1) * TB], in0=xF[:, c * TB:(c + 1) * TB], in1=t1B[:, :]),
                            reads=[("t1B",), ("xF",)], writes=[("xF",)])
                    S.add("sync", lambda e, i=i: e.dma_start(
                        out=AP(y_d, i * TB, [[N, 128], [128 * N, 8], [1, TB]]),
                        in_=AP(xF, 0, [[8 * TB, 128], [TB, 8], [1, TB]])),
                        reads=[("xF",)], dma=True)

        S.emit()
    return nc, dbg


def make_shared_inputs(ins):
    b1 = np.asarray(ins["L0_ffn_b1"], np.float32)
    b1T = np.zeros((128, 88), np.float32)
    for l in range(2):
        for cc in range(22):
            wc = 128 if cc < 21 else 42
            b1T[0:wc, l * 44 + cc] = b1[l, cc * 128: cc * 128 + wc]
            b1T[0:wc, l * 44 + 22 + cc] = b1[l, 2730 + cc * 128: 2730 + cc * 128 + wc]
    def cols8(v):
        out = np.zeros((128, 16), np.float32)
        for l in range(2):
            for c in range(8):
                out[:, l * 8 + c] = v[l, c * 128:(c + 1) * 128]
        return out
    qs = np.asarray(ins["L0_ffn_q_scale"], np.float32)
    ks = np.asarray(ins["L0_ffn_k_scale"], np.float32)
    qsbc = np.broadcast_to(np.tile(qs, (1, 8))[:, None, :], (2, 128, 512)).copy()
    ksbc = np.broadcast_to(np.tile(ks, (1, 8))[:, None, :], (2, 128, 512)).copy()
    return {
        "wqT": np.ascontiguousarray(np.asarray(ins["L0_attn_Wq"], np.float32).T),
        "wkT": np.ascontiguousarray(np.asarray(ins["L0_attn_Wk"], np.float32).T),
        "wvT": np.ascontiguousarray(np.asarray(ins["L0_attn_Wv"], np.float32).T),
        "woT": np.ascontiguousarray(np.asarray(ins["L0_attn_Wo"], np.float32).T),
        "boT": np.ascontiguousarray(np.asarray(ins["L0_attn_bo"], np.float32).reshape(8, 128).T),
        "wqkv": np.ascontiguousarray(np.asarray(ins["L0_ffn_wqkv"], np.float32)),
        "wo_in": np.ascontiguousarray(np.asarray(ins["L0_ffn_wo"], np.float32)),
        "w1_in": np.ascontiguousarray(np.asarray(ins["L0_ffn_w1"], np.float32)),
        "w2_in": np.ascontiguousarray(np.asarray(ins["L0_ffn_w2"], np.float32)),
        "b1T": b1T,
        "b2T": cols8(np.asarray(ins["L0_ffn_b2"], np.float32)),
        "lngT": cols8(np.asarray(ins["L0_ffn_ln_g"], np.float32)),
        "lnbT": cols8(np.asarray(ins["L0_ffn_ln_b"], np.float32)),
        "qsbcT": qsbc,
        "ksbcT": ksbc,
    }


def make_inputs_core(x_core, shared):
    d = dict(shared)
    d["x0"] = np.ascontiguousarray(x_core.T)
    return d



_PROGRAM_CACHE = {}


def kernel(**inputs):
    from concourse.bass_utils import run_bass_kernel_spmd
    ins = {k: np.asarray(v) for k, v in inputs.items()}
    if "nc" not in _PROGRAM_CACHE:
        _PROGRAM_CACHE["nc"] = build_program(debug=())[0]
    nc = _PROGRAM_CACHE["nc"]
    shared = make_shared_inputs(ins)
    x = np.asarray(ins["L0_attn_x"], np.float32)
    in_maps = [make_inputs_core(x[b], shared) for b in range(8)]
    res = run_bass_kernel_spmd(nc, in_maps, core_ids=list(range(8)))
    out = np.stack([np.ascontiguousarray(r["y"].T) for r in res.results], axis=0)
    return out.astype(np.float32)


# version 23
# speedup vs baseline: 1.0057x; 1.0057x over previous
import numpy as np
import ml_dtypes
import concourse.bass as bass
import concourse.mybir as mybir
from contextlib import ExitStack

F32 = mybir.dt.float32
BF16 = mybir.dt.bfloat16
AF = mybir.ActivationFunctionType
ALU = mybir.AluOpType
AX = mybir.AxisListType
AP = bass.AP

N = 4096
D = 1024
H = 16
HD = 64
TA = 256
NTA = N // TA
NEG = -1.0e30
C_OUTER = 8.0

ENGINES = ("sync", "gpsimd", "scalar", "vector", "tensor")
SKIP_SET = set()


class Sched:
    def __init__(self, nc):
        self.nc = nc
        self.ops = []
        self.last_writer = {}
        self.readers = {}
        self.NSEM = 6
        self.sem_comp = {e: nc.alloc_semaphore(f"sc_{e}") for e in ENGINES}
        self.sem_dma = {e: [nc.alloc_semaphore(f"sd_{e}{i}") for i in range(self.NSEM)]
                        for e in ("sync", "gpsimd", "scalar")}

    def add(self, eng, fn, reads=(), writes=(), dma=False, tag=""):
        op = {"eng": eng, "fn": fn, "dma": dma, "idx": len(self.ops), "deps": set(),
              "need_inc": False, "inc_no": None, "dma_no": None, "tag": tag, "barrier": False}
        deps = set()
        for r in reads:
            w = self.last_writer.get(r)
            if w is not None:
                deps.add(w["idx"])
        for r in writes:
            w = self.last_writer.get(r)
            if w is not None:
                deps.add(w["idx"])
            for rd in self.readers.get(r, ()):
                deps.add(rd["idx"])
        deps.discard(op["idx"])
        op["deps"] = deps
        self.ops.append(op)
        for r in reads:
            self.readers.setdefault(r, []).append(op)
        for r in writes:
            self.last_writer[r] = op
            self.readers[r] = []
        return op

    def cut(self):
        pass

    def barrier(self):
        op = {"eng": None, "fn": None, "dma": False, "idx": len(self.ops), "deps": set(),
              "need_inc": False, "inc_no": None, "dma_no": None, "tag": "barrier", "barrier": True}
        self.ops.append(op)
        self.last_writer = {}
        self.readers = {}
        return op

    def emit(self):
        nc = self.nc
        dma_count = {e: 0 for e in self.sem_dma}
        for op in self.ops:
            if op["barrier"]:
                continue
            if op["dma"]:
                dma_count[op["eng"]] += 1
                op["dma_no"] = dma_count[op["eng"]]
        SYNC_SAME = ("vector", "scalar", "gpsimd")
        last_op = {e: None for e in ENGINES}
        for op in self.ops:
            if op["barrier"]:
                for e in ENGINES:
                    if last_op[e] is not None and not last_op[e]["dma"]:
                        last_op[e]["need_inc"] = True
                    last_op[e] = None
                continue
            if op["eng"] in last_op:
                last_op[op["eng"]] = op
        for op in self.ops:
            if op["barrier"]:
                continue
            for d in op["deps"]:
                dep = self.ops[d]
                if dep["barrier"] or dep["dma"]:
                    continue
                if (dep["eng"] != op["eng"] or op["dma"]
                        or (dep["eng"] == op["eng"] and dep["eng"] in SYNC_SAME)):
                    dep["need_inc"] = True
        inc_no = {e: 0 for e in ENGINES}
        dma_no = {e: 0 for e in self.sem_dma}
        barrier_info = {}
        for op in self.ops:
            if op["barrier"]:
                pre = dict(inc_no)
                for e in ENGINES:
                    inc_no[e] += 1
                barrier_info[op["idx"]] = (dict(inc_no), dict(dma_no), pre)
                continue
            if op["dma"]:
                dma_no[op["eng"]] = op["dma_no"]
            elif op["need_inc"]:
                inc_no[op["eng"]] += 1
                op["inc_no"] = inc_no[op["eng"]]
        self.total_incs = dict(inc_no)
        self.total_dmas = dict(dma_no)

        def dma_wait_args(qe, dma_no):
            i = dma_no - 1
            return self.sem_dma[qe][i % self.NSEM], 16 * (i // self.NSEM + 1)

        def emit_engine(eng_name, eng):
            clock = {}
            def wait(sem, val):
                key = sem.num
                if clock.get(key, -1) >= val:
                    return
                eng.wait_ge(sem, val)
                clock[key] = val
            def wait_dma_full(qe, cnt):
                for k in range(cnt, max(cnt - self.NSEM, 0), -1):
                    s, v = dma_wait_args(qe, k)
                    wait(s, v)
            for op in self.ops:
                if op["barrier"]:
                    binc, bdma, pre = barrier_info[op["idx"]]
                    wait(self.sem_comp[eng_name], pre[eng_name])
                    for qe, cnt in bdma.items():
                        wait_dma_full(qe, cnt)
                    eng.nop().then_inc(self.sem_comp[eng_name], 1)
                    for e2 in ENGINES:
                        wait(self.sem_comp[e2], binc[e2])
                    continue
                if op["eng"] != eng_name:
                    continue
                for d in sorted(op["deps"]):
                    dep = self.ops[d]
                    if dep["barrier"]:
                        continue
                    if dep["dma"]:
                        s, v = dma_wait_args(dep["eng"], dep["dma_no"])
                        wait(s, v)
                    elif (dep["eng"] != eng_name or op["dma"]
                          or (dep["eng"] == eng_name and eng_name in ("vector", "scalar", "gpsimd"))):
                        if dep["inc_no"] is not None:
                            wait(self.sem_comp[dep["eng"]], dep["inc_no"])
                inst = op["fn"](eng)
                if op["dma"]:
                    s, _ = dma_wait_args(eng_name, op["dma_no"])
                    inst.then_inc(s, 16)
                elif op["need_inc"]:
                    inst.then_inc(self.sem_comp[eng_name], 1)
            for qe, cnt in self.total_dmas.items():
                wait_dma_full(qe, cnt)
            for e2 in ENGINES:
                if self.total_incs[e2]:
                    wait(self.sem_comp[e2], self.total_incs[e2])

        with nc.Block() as block:
            block.sync(lambda eng: emit_engine("sync", eng))
            block.gpsimd(lambda eng: emit_engine("gpsimd", eng))
            block.scalar(lambda eng: emit_engine("scalar", eng))
            block.vector(lambda eng: emit_engine("vector", eng))
            block.tensor(lambda eng: emit_engine("tensor", eng))



def rope_tables():
    f = np.arange(32, dtype=np.float64)
    inv = 1.0 / (10000.0 ** (2 * f / 64))
    t = np.arange(N, dtype=np.float64)
    ang = t[:, None] * inv[None, :]
    cos = np.cos(ang)
    sin = np.sin(ang)
    ctab = np.zeros((128, N), dtype=np.float64)
    stab = np.zeros((128, N), dtype=np.float64)
    for p in range(128):
        d = p % 64
        fi = d // 2
        ctab[p] = cos[:, fi]
        stab[p] = (-1.0 if d % 2 == 0 else 1.0) * sin[:, fi]
    return ctab.astype(ml_dtypes.bfloat16), stab.astype(ml_dtypes.bfloat16)


def outer_slide_mask():
    ki = np.arange(128)[:, None]
    u = np.arange(896)[None, :]
    return np.where(ki > u - 384, 0.0, 1.0).astype(ml_dtypes.bfloat16)


def swap_matrix():
    P = np.zeros((128, 128), dtype=ml_dtypes.bfloat16)
    for i in range(128):
        P[i ^ 1, i] = 1.0
    return P


def xpos_tables():
    d = 64
    half = np.arange(0, d, 2, dtype=np.float64)
    inv = 1.0 / 10000.0 ** (half / d)
    t = np.arange(N, dtype=np.float64)
    fr = t[:, None] * inv[None, :]
    fr = np.concatenate([fr, fr], axis=1)
    base = (half + 0.4 * d) / (1.4 * d)
    power = (t - N // 2) / 256.0
    sc = base[None, :] ** power[:, None]
    sc = np.concatenate([sc, sc], axis=1)
    sign = np.where(np.arange(64) < 32, -1.0, 1.0)[None, :]
    cos, sin = np.cos(fr), np.sin(fr)
    tabs = np.concatenate([cos * sc, sign * sin * sc, cos / sc, sign * sin / sc], axis=1)
    return tabs.astype(np.float32)


def inner_masks():
    li = np.arange(128)[:, None]
    kk = np.arange(640)[None, :]
    base = np.where((kk >= li) & (kk <= li + 512), 0.0, NEG)
    w0 = [np.full((128, 512 - 128 * s), NEG) for s in range(4)]
    out = np.concatenate([base] + w0, axis=1)
    return out.astype(ml_dtypes.bfloat16)



def build_program(debug=(), b_stop=None, nwindows=8, do_mlp=True, nlayers=2, a_tiles=8, b_tiles=8, skip=()):
    nc = bass.Bass("TRN2", target_bir_lowering=False)
    S = Sched(nc)

    x0_d = nc.dram_tensor("x0", [D, N], F32, kind="ExternalInput")
    wqT_d = nc.dram_tensor("wqT", [D, D], F32, kind="ExternalInput")
    wkT_d = nc.dram_tensor("wkT", [D, D], F32, kind="ExternalInput")
    wvT_d = nc.dram_tensor("wvT", [D, D], F32, kind="ExternalInput")
    woT_d = nc.dram_tensor("woT", [D, D], F32, kind="ExternalInput")
    bo_d = nc.dram_tensor("boT", [128, 8], F32, kind="ExternalInput")

    x1_d = nc.dram_tensor("x1scratch", [D, N], F32, kind="Internal")
    lnx_d = nc.dram_tensor("lnxscratch", [D, N], BF16, kind="Internal")
    y_d = nc.dram_tensor("y", [D, N], F32, kind="ExternalOutput")
    wqkv_d = nc.dram_tensor("wqkv", [2, 1024, 1536], F32, kind="ExternalInput")
    woB_d = nc.dram_tensor("wo_in", [2, 512, 1024], F32, kind="ExternalInput")
    w1_d = nc.dram_tensor("w1_in", [2, 1024, 5460], F32, kind="ExternalInput")
    w2_d = nc.dram_tensor("w2_in", [2, 2730, 1024], F32, kind="ExternalInput")
    wqkvb_d = nc.dram_tensor("wqkvbd", [2, 1024, 1536], BF16, kind="Internal")
    wob_d = nc.dram_tensor("wobd", [2, 512, 1024], BF16, kind="Internal")
    w1b_d = nc.dram_tensor("w1bd", [2, 1024, 5460], BF16, kind="Internal")
    w2b_d = nc.dram_tensor("w2bd", [2, 2730, 1024], BF16, kind="Internal")
    b1T_d = nc.dram_tensor("b1T", [128, 88], F32, kind="ExternalInput")
    b2T_d = nc.dram_tensor("b2T", [128, 16], F32, kind="ExternalInput")
    lngT_d = nc.dram_tensor("lngT", [128, 16], F32, kind="ExternalInput")
    lnbT_d = nc.dram_tensor("lnbT", [128, 16], F32, kind="ExternalInput")
    qsbc_d = nc.dram_tensor("qsbcT", [2, 128, 512], F32, kind="ExternalInput")
    ksbc_d = nc.dram_tensor("ksbcT", [2, 128, 512], F32, kind="ExternalInput")
    xpos_d = nc.inline_tensor(xpos_tables(), name="xpostab")
    maskB_d = nc.inline_tensor(inner_masks(), name="maskinner")

    ctab, stab = rope_tables()
    cos_d = nc.inline_tensor(ctab, name="ropecos")
    sin_d = nc.inline_tensor(stab, name="ropesin")
    maskslide_d = nc.inline_tensor(outer_slide_mask(), name="maskslide")
    pswap_d = nc.inline_tensor(swap_matrix(), name="pswap")
    eye128_d = nc.inline_tensor(np.eye(128, dtype=ml_dtypes.bfloat16), name="eye128")
    onescol_d = nc.inline_tensor(np.ones((128, 1), dtype=ml_dtypes.bfloat16), name="onescol")
    ones64_d = nc.inline_tensor(np.ones((1, 64), dtype=np.float32), name="ones64")
    ones128_d = nc.inline_tensor(np.ones((1, 128), dtype=np.float32), name="ones128")

    dbg = {}
    for name, shape in (("kfm", [128, 8 * N]), ("vtm", [128, 32 * 1040]),
                        ("ln1", [128, 8 * TA]), ("qfm", [128, 8 * TA]),
                        ("x1", [D, N]), ("ofm", [128, 8 * TA]), ("u", [128, 8 * TA]),
                        ("rows2", [1, 4 * TA]),
                        ("zinit", [128, 8 * N]), ("z0a", [128, 8 * N]), ("z0f", [128, 8 * N]),
                        ("z1a", [128, 8 * N]), ("z1f", [128, 8 * N]),
                        ("bqr", [128, 512]), ("bkr", [128, 512]), ("bofm", [128, 4 * 512]),
                        ("be", [128, 640]), ("beT", [128, 640]), ("bden", [128, 8]),
                        ("be1", [128, 640]),
                        ("bkT", [128, 4 * 1024]), ("bvT", [128, 8 * 512])):
        if name in debug:
            dbg[name] = nc.dram_tensor("dbg_" + name, shape, F32, kind="ExternalOutput")

    with ExitStack() as ctx:
        sb = lambda name, shape, dt: ctx.enter_context(nc.sbuf_tensor(name, shape, dt))

        pswap = sb("sb_pswap", [128, 128], BF16)
        eye128 = sb("sb_eye128", [128, 128], BF16)
        onescol = sb("sb_onescol", [128, 1], BF16)
        ones128 = sb("sb_ones128", [1, 128], F32)
        bo_sb = sb("bo", [128, 8], F32)
        onesall = sb("sb_onesall", [128, 64], F32)
        consts = sb("sb_consts", [128, 2], F32)
        S.add("vector", lambda e: e.memset(onesall[:, :], 1.0), writes=[("onesall",)])
        S.add("vector", lambda e: e.memset(consts[:, 0:1], 1e-5), writes=[("c_eps",)])
        S.add("vector", lambda e: e.memset(consts[:, 1:2], -C_OUTER), writes=[("c_negC",)])

        for dst, src, res in ((pswap, pswap_d, "pswap"), (eye128, eye128_d, "eye128"),
                              (onescol, onescol_d, "onescol"),
                              (ones128, ones128_d, "ones128"), (bo_sb, bo_d, "bo")):
            S.add("sync", lambda e, dst=dst, src=src: e.dma_start(out=dst[:, :], in_=src.ap()),
                  writes=[(res,)], dma=True)


        _padc = [0]

        def pad_op(eng, buf, cols=256, rows=(0, 128)):
            _padc[0] += 1
            k = _padc[0]
            r0, r1 = rows
            if eng == "scalar":
                S.add("scalar", lambda e, buf=buf, cols=cols, r0=r0, r1=r1: e.activation(
                    out=buf[r0:r1, 0:cols], in_=buf[r0:r1, 0:cols], func=AF.Copy),
                    writes=[("pad", k)])
            else:
                S.add(eng, lambda e, buf=buf, cols=cols, r0=r0, r1=r1: e.tensor_copy(
                    out=buf[r0:r1, 0:cols], in_=buf[r0:r1, 0:cols]),
                    writes=[("pad", k)])

        TW = 512
        THF = 256
        NTW = N // TW
        with ExitStack() as ctxA:
            sbA = lambda name, shape, dt: ctxA.enter_context(nc.sbuf_tensor(name, shape, dt))
            k_fm = sbA("k_fm", [128, 8 * N], BF16)
            v_tm = sbA("v_tm", [128, 32 * 1040], BF16)
            wA = sbA("wA", [128, 8 * 1024], BF16)
            wB = sbA("wB", [128, 8 * 1024], BF16)

            def dma_wT(src_d, dst, res):
                S.add("gpsimd", lambda e, src_d=src_d, dst=dst: e.dma_start(
                    out=AP(dst, 0, [[8 * 1024, 128], [1024, 8], [1, 1024]]),
                    in_=AP(src_d, 0, [[D, 128], [128 * D, 8], [1, 1024]])),
                    writes=[(res,)], dma=True)

            def phaseA_sub(ctxS, is_a2):
                sbS = lambda name, shape, dt: ctxS.enter_context(
                    nc.sbuf_tensor(("a2_" if is_a2 else "a1_") + name, shape, dt))
                psS = lambda name, shape: ctxS.enter_context(
                    nc.psum_tensor(("a2p_" if is_a2 else "a1p_") + name, shape, F32))
                st = {}
                if is_a2:
                    st["lq"] = sbS("lq", [128, 8 * TW], BF16)
                    st["o_fm"] = sbS("o_fm", [128, 8 * TW], BF16)
                    st["qfu"] = sbS("qfu", [128, 8 * TW], BF16)
                else:
                    st["lq"] = sbS("lq", [128, 2 * 8 * TW], BF16)
                    st["xau"] = sbS("xau", [128, 8 * THF], F32)
                st["ropec"] = sbS("ropec", [128, TW], BF16)
                st["ropes"] = sbS("ropes", [128, TW], BF16)
                st["xb"] = sbS("xb", [128, 2 * THF], BF16)
                st["x2b"] = sbS("x2b", [128, 2 * THF], BF16)
                st["rows"] = sbS("rows", [1, 3 * THF], F32)
                st["kraw"] = sbS("kraw", [128, 2 * TW], BF16)
                st["t1"] = sbS("t1", [128, TW], F32)
                st["t2"] = sbS("t2", [128, TW], F32)
                if is_a2:
                    st["e_sb"] = sbS("e_sb", [128, 4 * TW], BF16)
                    st["maskd"] = sbS("maskd", [128, 896], BF16)
                    st["bc"] = sbS("bc", [128, TW], F32)
                if is_a2:
                    st["P"] = [psS("P0", [128, 512]), psS("P1", [128, 512]),
                               psS("P2", [128, 512]), psS("P3", [128, 512]),
                               psS("P45", [128, 1024]), psS("P67", [128, 1024])]
                else:
                    st["P"] = [psS(f"P{i}", [128, 512]) for i in range(8)]
                return st

            def ln_half(st, x_src_dma_fn, dst_col_fn, tag, xbuf=None, xres=("xau",), lq_res=("lq",), stats_ix=(6, 7)):
                P = st["P"]
                rows = st["rows"]
                xb, x2b = st["xb"], st["x2b"]
                xau = xbuf if xbuf is not None else st["xau"]
                if x_src_dma_fn is not None:
                    x_src_dma_fn()
                si0, si1 = stats_ix
                for c in range(8):
                    sl = c % 2
                    ss = slice(sl * THF, (sl + 1) * THF)
                    cs = slice(c * THF, (c + 1) * THF)
                    S.add("vector", lambda e, cs=cs, ss=ss: e.tensor_copy(out=xb[:, ss], in_=xau[:, cs]),
                          reads=[xres], writes=[("xb", sl)])
                    S.add("scalar", lambda e, cs=cs, ss=ss: e.activation(out=x2b[:, ss], in_=xau[:, cs], func=AF.Square),
                          reads=[xres], writes=[("x2b", sl)])
                    S.add("tensor", lambda e, ss=ss, c=c, si0=si0: e.matmul(
                        P[si0][0:1, 0:THF], onescol[:, :], xb[:, ss], start=(c == 0), stop=(c == 7)),
                        reads=[("xb", sl), ("onescol",)], writes=[("P", si0)])
                    S.add("tensor", lambda e, ss=ss, c=c, si1=si1: e.matmul(
                        P[si1][0:1, 0:THF], onescol[:, :], x2b[:, ss], start=(c == 0), stop=(c == 7)),
                        reads=[("x2b", sl), ("onescol",)], writes=[("P", si1)])
                mu = rows[:, 0 * THF:1 * THF]
                m2 = rows[:, 1 * THF:2 * THF]
                rr = m2
                mur = rows[:, 2 * THF:3 * THF]
                S.add("vector", lambda e, mu=mu, si0=si0: e.tensor_scalar_mul(out=mu, in0=P[si0][0:1, 0:THF], scalar1=1.0 / D),
                      reads=[("P", si0)], writes=[("r_mu",)])
                S.add("vector", lambda e, mu=mu, m2=m2: e.scalar_tensor_tensor(out=m2, in0=mu, scalar=-1.0, in1=mu,
                                                                 op0=ALU.mult, op1=ALU.mult),
                      reads=[("r_mu",)], writes=[("r_m2",)])
                S.add("vector", lambda e, m2=m2, si1=si1: e.scalar_tensor_tensor(out=m2, in0=P[si1][0:1, 0:THF], scalar=1.0 / D,
                                                                 in1=m2, op0=ALU.mult, op1=ALU.add),
                      reads=[("P", si1), ("r_m2",)], writes=[("r_var",)])
                S.add("scalar", lambda e, m2=m2, rr=rr: e.activation(out=rr, in_=m2, func=AF.Ln, bias=consts[0:1, 0:1]),
                      reads=[("r_var",), ("c_eps",)], writes=[("r_lnv",)])
                S.add("scalar", lambda e, rr=rr: e.activation(out=rr, in_=rr, func=AF.Exp, scale=-0.5),
                      reads=[("r_lnv",)], writes=[("r_r",)])
                S.add("vector", lambda e, mu=mu, rr=rr, mur=mur: e.tensor_mul(out=mur, in0=mu, in1=rr),
                      reads=[("r_mu",), ("r_r",)], writes=[("r_mur",)])
                S.add("tensor", lambda e, rr=rr: e.matmul(P[0][:, 0:THF], ones128[:, :], rr, start=True, stop=True),
                      reads=[("r_r",), ("ones128",)], writes=[("P", 0)])
                S.add("tensor", lambda e, mur=mur: e.matmul(P[1][:, 0:THF], ones128[:, :], mur, start=True, stop=True),
                      reads=[("r_mur",), ("ones128",)], writes=[("P", 1)])
                if tag == "ln":
                    lq = st["lq"]
                    t1 = st["t1"]
                    for c in range(8):
                        cs = slice(c * THF, (c + 1) * THF)
                        S.add("vector", lambda e, cs=cs: e.tensor_mul(out=t1[:, 0:THF], in0=xau[:, cs], in1=P[0][:, 0:THF]),
                              reads=[xres, ("P", 0)], writes=[("t1",)])
                        S.add("vector", lambda e, c=c, cs=cs: e.tensor_sub(
                            out=lq[:, dst_col_fn(c)], in0=t1[:, 0:THF], in1=P[1][:, 0:THF]),
                            reads=[("t1",), ("P", 1)], writes=[lq_res])
                else:
                    for c in range(8):
                        cs = slice(c * THF, (c + 1) * THF)
                        S.add("vector", lambda e, cs=cs: e.tensor_mul(out=xau[:, cs], in0=xau[:, cs], in1=P[0][:, 0:THF]),
                              reads=[xres, ("P", 0)], writes=[xres])
                        S.add("vector", lambda e, cs=cs: e.tensor_sub(out=xau[:, cs], in0=xau[:, cs], in1=P[1][:, 0:THF]),
                              reads=[xres, ("P", 1)], writes=[xres])

            def proj_rope(st, wsb, wres, out_sb, out_res, out_col_fn, ti, lq_base=0, lq_res=("lq",), lq_pitch=None,
                          adder=None, p_proj=2, p_swap=3, c_list=None, out_res_fn=None):
                A2_ = adder if adder is not None else S
                P = st["P"]
                lq, kraw, t1, t2 = st["lq"], st["kraw"], st["t1"], st["t2"]
                if lq_pitch is None:
                    lq_pitch = lq.shape[1]
                ropec, ropes = st["ropec"], st["ropes"]
                for c in (c_list if c_list is not None else range(8)):
                    for j in range(8):
                        A2_.add("tensor", lambda e, c=c, j=j, wsb=wsb, lq=lq, lq_base=lq_base, lq_pitch=lq_pitch: e.matmul(
                            P[p_proj][:, :], wsb[:, j * 1024 + c * 128: j * 1024 + (c + 1) * 128],
                            AP(lq, lq_base + j * TW, [[lq_pitch, 128], [1, TW]]),
                            start=(j == 0), stop=(j == 7), skip_group_check=True),
                            reads=[lq_res, (wres,)], writes=[("P", p_proj)])
                        if j == 3:
                            A2_.cut()
                    ks = c % 2
                    ksl = slice(ks * TW, (ks + 1) * TW)
                    A2_.cut()
                    A2_.add("scalar", lambda e, ksl=ksl: e.activation(
                        out=kraw[:, ksl], in_=P[p_proj][:, :], func=AF.Copy),
                        reads=[("P", p_proj)], writes=[("kraw", ks)])
                    A2_.add("tensor", lambda e, ksl=ksl: e.matmul(
                        P[p_swap][:, :], pswap[:, :], kraw[:, ksl], start=True, stop=True,
                        skip_group_check=True),
                        reads=[("kraw", ks), ("pswap",)], writes=[("P", p_swap)])
                    A2_.cut()
                    A2_.add("vector", lambda e, ksl=ksl: e.tensor_mul(
                        out=t1[:, :], in0=kraw[:, ksl], in1=ropec[:, :]),
                        reads=[("kraw", ks), ("ropec",)], writes=[("t1",)])
                    A2_.add("vector", lambda e: e.tensor_mul(
                        out=t2[:, :], in0=P[p_swap][:, :], in1=ropes[:, :]),
                        reads=[("P", p_swap), ("ropes",)], writes=[("t2",)])
                    A2_.add("vector", lambda e, c=c, out_sb=out_sb: e.tensor_add(
                        out=out_sb[:, out_col_fn(c)], in0=t1[:, :], in1=t2[:, :]),
                        reads=[("t1",), ("t2",)],
                        writes=[out_res_fn(c) if out_res_fn is not None else out_res])
                    A2_.cut()

            def dma_x_half(st, T0):
                S.add("sync", lambda e, T0=T0: e.dma_start(
                    out=AP(st["xau"], 0, [[8 * THF, 128], [THF, 8], [1, THF]]),
                    in_=AP(x0_d, T0, [[N, 128], [128 * N, 8], [1, THF]])),
                    writes=[("xau",)], dma=True)

            def dma_rope(st, ti):
                S.add("sync", lambda e, ti=ti: e.dma_start(
                    out=st["ropec"][:, :], in_=AP(cos_d, ti * TW, [[N, 128], [1, TW]])),
                    writes=[("ropec",)], dma=True)
                S.add("sync", lambda e, ti=ti: e.dma_start(
                    out=st["ropes"][:, :], in_=AP(sin_d, ti * TW, [[N, 128], [1, TW]])),
                    writes=[("ropes",)], dma=True)

            dma_wT(wkT_d, wA, "wA")
            dma_wT(wvT_d, wB, "wB")
            for srcd, dstd, rows, width, res in (
                    (wqkv_d, wqkvb_d, 2 * 1024, 1536, "wqkvC"),
                    (woB_d, wob_d, 2 * 512, 1024, "woC"),
                    (w1_d, w1b_d, 2 * 1024, 5460, "w1C"),
                    (w2_d, w2b_d, 2 * 2730, 1024, "w2C")):
                r0 = 0
                while r0 < rows:
                    rc = min(128, rows - r0)
                    S.add("gpsimd", lambda e, srcd=srcd, dstd=dstd, r0=r0, rc=rc, width=width: e.dma_start(
                        out=AP(dstd, r0 * width, [[width, rc], [1, width]]),
                        in_=AP(srcd, r0 * width, [[width, rc], [1, width]])),
                        writes=[(res, r0)], dma=True)
                    r0 += rc
            with ExitStack() as ctxS:
                st = phaseA_sub(ctxS, is_a2=False)
                P = st["P"]
                S.add("vector", lambda e: e.memset(v_tm[:, :], 1.0), writes=[("v_tm_init",)])

                def ln_tile(i):
                    par = i % 2
                    lqo = par * 8 * TW
                    lqres = ("lq", par)
                    for h2 in range(2):
                        ln_half(st, lambda h2=h2, i=i: dma_x_half(st, i * TW + h2 * THF),
                                lambda c, h2=h2, lqo=lqo: slice(lqo + c * TW + h2 * THF,
                                                                lqo + c * TW + (h2 + 1) * THF),
                                "ln", lq_res=lqres)
                    S.add("sync", lambda e, i=i, lqo=lqo, st=st: e.dma_start(
                        out=AP(lnx_d, i * TW, [[N, 128], [128 * N, 8], [1, TW]]),
                        in_=AP(st["lq"], lqo, [[16 * TW, 128], [TW, 8], [1, TW]])),
                        reads=[lqres], dma=True)

                ln_tile(0)
                for i in range(a_tiles):
                    par = i % 2
                    lqo = par * 8 * TW
                    lqres = ("lq", par)
                    dma_rope(st, i)
                    proj_rope(st, wA, "wA", k_fm, ("k_fm", i),
                              lambda c, i=i: slice(c * N + i * TW, c * N + (i + 1) * TW), i,
                              lq_base=lqo, lq_res=lqres, lq_pitch=16 * TW)
                    if i + 1 < a_tiles:
                        ln_tile(i + 1)
                    for s in range(4):
                        for half in (0, 1):
                            pv = P[4] if half == 0 else P[5]
                            pres = ("P", 4) if half == 0 else ("P", 5)
                            lq1 = st["lq"]
                            for j in range(8):
                                S.add("tensor", lambda e, s=s, half=half, pv=pv, j=j, lq1=lq1, lqo=lqo: e.matmul(
                                    pv[:, :], AP(lq1, lqo + j * TW + s * 128, [[16 * TW, 128], [1, 128]]),
                                    wB[:, j * 1024 + half * 512: j * 1024 + (half + 1) * 512],
                                    start=(j == 0), stop=(j == 7)),
                                    reads=[("wB",), lqres], writes=[pres])
                            blk = i * 4 + s
                            S.add("vector", lambda e, blk=blk, half=half, pv=pv: e.tensor_copy(
                                out=AP(v_tm, blk * 1040 + half * 8 * 65, [[32 * 1040, 128], [65, 8], [1, 64]]),
                                in_=AP(pv, 0, [[512, 128], [64, 8], [1, 64]])),
                                reads=[pres, ("v_tm_init",)], writes=[("v_tm", blk)])
                if "kfm" in dbg:
                    S.add("gpsimd", lambda e: e.dma_start(out=dbg["kfm"].ap(), in_=k_fm[:, :]),
                          reads=[("k_fm", i2) for i2 in range(NTW)], dma=True)
                if "vtm" in dbg:
                    S.add("gpsimd", lambda e: e.dma_start(out=dbg["vtm"].ap(), in_=v_tm[:, :]),
                          reads=[("v_tm", b) for b in range(32)] + [("v_tm_init",)], dma=True)
                S.barrier()

            dma_wT(wqT_d, wA, "wA")
            dma_wT(woT_d, wB, "wB")
            with ExitStack() as ctxS:
                st = phaseA_sub(ctxS, is_a2=True)
                P = st["P"]
                e_sb, maskd2, bc = st["e_sb"], st["maskd"], st["bc"]
                lq = st["lq"]
                q_fm = st["qfu"]
                o_fm = st["o_fm"]
                u_f32 = st["lq"].ap().bitcast(F32)
                S.add("sync", lambda e: e.dma_start(out=maskd2[:, :], in_=maskslide_d.ap()),
                      writes=[("maskd2",)], dma=True)

                class UAdder:
                    def __init__(self):
                        self.units = [[]]

                    def add(self, *aa, **kk):
                        self.units[-1].append((aa, kk))

                    def cut(self):
                        if self.units[-1]:
                            self.units.append([])

                    def take(self):
                        out = [u_ for u_ in self.units if u_]
                        self.units = [[]]
                        return out

                fill_q = []

                def pump(n=1):
                    for _ in range(n):
                        if not fill_q:
                            return
                        for aa, kk in fill_q.pop(0):
                            S.add(*aa, **kk)

                qcol = lambda c: slice(c * TW, (c + 1) * TW)
                qres = lambda c: ("qfu", c)

                def dma_lq(Jn):
                    S.add("sync", lambda e, Jn=Jn: e.dma_start(
                        out=AP(lq, 0, [[8 * TW, 128], [TW, 8], [1, TW]]),
                        in_=AP(lnx_d, Jn * TW, [[N, 128], [128 * N, 8], [1, TW]])),
                        writes=[("lq",)], dma=True)

                def enqueue_proj(Jn, c):
                    ua = UAdder()
                    proj_rope(st, wA, "wA", q_fm, None, qcol, Jn,
                              adder=ua, p_proj=0, p_swap=0, c_list=[c], out_res_fn=qres)
                    fill_q.extend(ua.take())

                dma_rope(st, 0)
                dma_lq(0)
                proj_rope(st, wA, "wA", q_fm, None, qcol, 0,
                          p_proj=0, p_swap=0, out_res_fn=qres)
                pending_div = []
                for J in range(a_tiles):
                    if J + 1 < a_tiles:
                        dma_rope(st, J + 1)
                        dma_lq(J + 1)
                    nkt = 4 * J + 4
                    for h in range(H):
                        hc = h // 2
                        hb = (h % 2) * 64
                        pO = P[2] if h % 2 == 0 else P[3]
                        pOres = ("P", 2) if h % 2 == 0 else ("P", 3)

                        def s_pair(kp, nkt=nkt, hb=hb, hc=hc, J=J):
                            pS = P[4] if kp % 2 == 0 else P[5]
                            pRes = ("P", 4) if kp % 2 == 0 else ("P", 5)
                            for half in range(2):
                                kt = 2 * kp + half
                                S.add("tensor", lambda e, kt=kt, pS=pS, hb=hb, hc=hc, half=half: e.matmul(
                                    pS[:, half * 512:(half + 1) * 512],
                                    k_fm[hb:hb + 64, hc * N + kt * 128: hc * N + (kt + 1) * 128],
                                    q_fm[hb:hb + 64, hc * TW:(hc + 1) * TW],
                                    start=True, stop=False, skip_group_check=True),
                                    reads=[("k_fm", kt // 4), ("qfu", hc)], writes=[pRes])
                            es = kp % 2
                            S.add("scalar", lambda e, pS=pS, es=es: e.activation(
                                out=e_sb[:, es * 2 * TW:(es + 1) * 2 * TW], in_=pS[:, :], func=AF.Exp,
                                scale=1.0 / HD, bias=consts[:, 1:2]),
                                reads=[pRes, ("c_negC",)], writes=[("e_sb", es)])
                            for half in range(2):
                                kt = 2 * kp + half
                                if kt >= nkt - 4:
                                    o = (kt - (nkt - 4)) * 128
                                    S.add("vector", lambda e, es=es, half=half, o=o: e.tensor_mul(
                                        out=e_sb[:, es * 2 * TW + half * TW: es * 2 * TW + (half + 1) * TW],
                                        in0=e_sb[:, es * 2 * TW + half * TW: es * 2 * TW + (half + 1) * TW],
                                        in1=maskd2[:, 384 - o: 896 - o]),
                                        reads=[("e_sb", es), ("maskd2",)], writes=[("e_sb", es)])

                        def pv_pair(kp, nkt=nkt, h=h, pO=pO, pOres=pOres):
                            es = kp % 2
                            for half in range(2):
                                kt = 2 * kp + half
                                S.add("tensor", lambda e, kt=kt, h=h, es=es, nkt=nkt, pO=pO, half=half: e.matmul(
                                    pO[0:65, :], v_tm[:, kt * 1040 + h * 65: kt * 1040 + h * 65 + 65],
                                    e_sb[:, es * 2 * TW + half * TW: es * 2 * TW + (half + 1) * TW],
                                    start=(kt == 0), stop=(kt == nkt - 1),
                                    skip_group_check=True),
                                    reads=[("e_sb", es), ("v_tm", kt), ("v_tm_init",)], writes=[pOres])

                        def emit_divide():
                            for aa, kk in pending_div:
                                S.add(*aa, **kk)
                            pending_div.clear()

                        nkp = nkt // 2
                        s_pair(0)
                        pump(1)
                        for kp in range(1, nkp):
                            s_pair(kp)
                            if pending_div:
                                emit_divide()
                            pump(1)
                            pv_pair(kp - 1)
                            pump(1)
                        pv_pair(nkp - 1)
                        if pending_div:
                            emit_divide()
                        pump(1)

                        p_ = h % 2
                        rbase = 64 if p_ == 0 else 32
                        rd = bc[rbase:rbase + 1, :]
                        S.add("vector", lambda e, rd=rd, pO=pO: e.reciprocal(out=rd, in_=pO[64:65, :]),
                              reads=[pOres], writes=[("r_rd", p_)])
                        pd = []
                        pd.append((("tensor", lambda e, rd=rd, rbase=rbase: e.matmul(
                            P[1][0:64, :], onesall[rbase:rbase + 1, :], rd, start=True, stop=True)),
                            dict(reads=[("r_rd", p_), ("onesall",)], writes=[("P", 1)])))
                        pd.append((("vector", lambda e: e.tensor_copy(out=bc[0:64, :], in_=P[1][0:64, :])),
                            dict(reads=[("P", 1)], writes=[("bc",), ("r_rd", 1)])))
                        if hb == 0:
                            pd.append((("vector", lambda e, hc=hc, pO=pO: e.tensor_mul(
                                out=o_fm[0:64, hc * TW:(hc + 1) * TW], in0=pO[0:64, :], in1=bc[0:64, :])),
                                dict(reads=[pOres, ("bc",)], writes=[("o_fm", hc, 0)])))
                        else:
                            pd.append((("vector", lambda e, hc=hc, pO=pO: e.tensor_mul(
                                out=o_fm[64:128, hc * TW:(hc + 1) * TW], in0=pO[0:64, :], in1=bc[0:64, :])),
                                dict(reads=[pOres, ("bc",)], writes=[("o_fm", hc, 1)])))
                        pending_div.extend((a2args + (kk,)) if False else (a2args, kk)
                                           for a2args, kk in pd)
                        pump(2)
                        if J + 1 < a_tiles and h % 2 == 1:
                            enqueue_proj(J + 1, h // 2)
                    if pending_div:
                        for aa, kk in pending_div:
                            S.add(*aa, **kk)
                        pending_div.clear()
                    pump(len(fill_q))
                    for h2 in range(2):
                        for c in range(8):
                            for j in range(8):
                                S.add("tensor", lambda e, c=c, j=j, h2=h2: e.matmul(
                                    P[2][:, 0:THF], wB[:, j * 1024 + c * 128: j * 1024 + (c + 1) * 128],
                                    o_fm[:, j * TW + h2 * THF: j * TW + (h2 + 1) * THF],
                                    start=(j == 0), stop=(j == 7), skip_group_check=True),
                                    reads=[("wB",), ("o_fm", j, 0), ("o_fm", j, 1)], writes=[("P", 2)])
                            S.add("vector", lambda e, c=c: e.tensor_scalar_add(
                                out=u_f32[:, c * THF:(c + 1) * THF], in0=P[2][:, 0:THF],
                                scalar1=bo_sb[:, c:c + 1]),
                                reads=[("P", 2), ("bo",)], writes=[("lq",)])
                        ln_half(st, None, None, "u", xbuf=u_f32, xres=("lq",), stats_ix=(4, 5))
                        T0 = J * TW + h2 * THF
                        u_dma = st["lq"].ap().bitcast(F32).rearrange("p (c t) -> p c t", c=8)
                        S.add("gpsimd", lambda e, T0=T0, u_dma=u_dma: e.dma_start(
                            out=u_dma,
                            in_=AP(x0_d, T0, [[N, 128], [128 * N, 8], [1, THF]]),
                            accum_op=ALU.add),
                            reads=[("lq",)], writes=[("lq",)], dma=True)
                        S.add("sync", lambda e, T0=T0, u_dma=u_dma: e.dma_start(
                            out=AP(x1_d, T0, [[N, 128], [128 * N, 8], [1, THF]]),
                            in_=u_dma),
                            reads=[("lq",)], dma=True)
                        if "x1" in dbg:
                            S.add("sync", lambda e, T0=T0, u_dma=u_dma: e.dma_start(
                                out=AP(dbg["x1"], T0, [[N, 128], [128 * N, 8], [1, THF]]),
                                in_=u_dma),
                                reads=[("lq",)], dma=True)
                S.barrier()

        S.barrier()
        TB = 512
        if "B" in skip:
            S.emit()
            return nc, dbg
        with ExitStack() as ctxB:
            sbB = lambda name, shape, dt: ctxB.enter_context(nc.sbuf_tensor(name, shape, dt))
            psB = lambda name, shape, dt: ctxB.enter_context(nc.psum_tensor(name, shape, dt))

            z_sb = sbB("z_sb", [128, 8 * N], F32)
            t1B = sbB("t1B", [128, TB], F32)
            t2B = sbB("t2B", [128, TB], F32)
            t3B = sbB("t3B", [128, TB], F32)
            xpb = sbB("xpb", [128, 2 * 256], F32)
            constsB = sbB("constsB", [128, 2], F32)
            colsB = sbB("colsB", [128, 24], F32)
            pa_col = sbB("pa_col", [128, 16], F32)
            padb = sbB("padb", [128, 256], BF16)

            pm_col = sbB("pm_col", [128, 16], F32)
            den_col = sbB("den_col", [128, 8], F32)

            b1T_sb = sbB("b1T_sb", [128, 88], F32)
            b2T_sb = sbB("b2T_sb", [128, 16], F32)
            lngT_sb = sbB("lngT_sb", [128, 16], F32)
            lnbT_sb = sbB("lnbT_sb", [128, 16], F32)
            for dst, srcd, res in ((b1T_sb, b1T_d, "b1T"), (b2T_sb, b2T_d, "b2T"),
                                   (lngT_sb, lngT_d, "lngT"), (lnbT_sb, lnbT_d, "lnbT")):
                S.add("sync", lambda e, dst=dst, srcd=srcd: e.dma_start(out=dst[:, :], in_=srcd.ap()),
                      writes=[(res,)], dma=True)

            PB_holder = {}

            S.add("vector", lambda e: e.memset(constsB[:, 0:1], 1e-5), writes=[("c_eps",)])
            S.add("vector", lambda e: e.memset(constsB[:, 1:2], 1e-24), writes=[("c_eps24",)])

            def z_ap(c, t0, n=TB):
                return z_sb[:, c * N + t0: c * N + t0 + n]

            def dma_xpos(tt_global, par):
                S.add("sync", lambda e, tt_global=tt_global, par=par: e.dma_start(
                    out=xpb[:, par * 256:(par + 1) * 256],
                    in_=AP(xpos_d, tt_global * 128 * 256, [[256, 128], [1, 256]])),
                    writes=[("xpb", par)], dma=True)

            def ln_statsB(src_fn, src_reads, rowsB, xbB, x2bB, src_is_bf16):
                PB = PB_holder["PB"]
                if "lnall" in SKIP_SET:
                    return
                for c in range(8):
                    sl = c % 2
                    ss_ = slice(sl * TB, (sl + 1) * TB)
                    s_ap = src_fn(c)
                    if src_is_bf16:
                        xv, xres = s_ap, list(src_reads)
                    else:
                        S.add("vector", lambda e, s_ap=s_ap, ss_=ss_: e.tensor_copy(out=xbB[:, ss_], in_=s_ap),
                              reads=list(src_reads), writes=[("xbB", sl)])
                        xv, xres = xbB[:, ss_], [("xbB", sl)]
                    S.add("scalar", lambda e, s_ap=s_ap, ss_=ss_: e.activation(out=x2bB[:, ss_], in_=s_ap, func=AF.Square),
                          reads=list(src_reads), writes=[("x2bB", sl)])
                    S.add("tensor", lambda e, xv=xv, c=c, PB=PB: e.matmul(
                        PB[2][0:1, 0:TB], onescol[:, :], xv, start=(c == 0), stop=(c == 7)),
                        reads=xres + [("onescol",)], writes=[("B", 2)])
                    S.add("tensor", lambda e, ss_=ss_, c=c, PB=PB: e.matmul(
                        PB[3][0:1, 0:TB], onescol[:, :], x2bB[:, ss_], start=(c == 0), stop=(c == 7)),
                        reads=[("x2bB", sl), ("onescol",)], writes=[("B", 3)])
                if "lnrows" in SKIP_SET:
                    return
                mu = rowsB[:, 0 * TB:1 * TB]
                m2 = rowsB[:, 1 * TB:2 * TB]
                rr = m2
                mur = rowsB[:, 2 * TB:3 * TB]
                S.add("vector", lambda e, mu=mu: e.tensor_scalar_mul(out=mu, in0=PB[2][0:1, 0:TB], scalar1=1.0 / D),
                      reads=[("B", 2)], writes=[("rB_mu",)])
                S.add("vector", lambda e, mu=mu, m2=m2: e.scalar_tensor_tensor(out=m2, in0=mu, scalar=-1.0, in1=mu,
                                                                 op0=ALU.mult, op1=ALU.mult),
                      reads=[("rB_mu",)], writes=[("rB_m2",)])
                S.add("vector", lambda e, m2=m2: e.scalar_tensor_tensor(out=m2, in0=PB[3][0:1, 0:TB], scalar=1.0 / D,
                                                                 in1=m2, op0=ALU.mult, op1=ALU.add),
                      reads=[("B", 3), ("rB_m2",)], writes=[("rB_var",)])
                S.add("scalar", lambda e, m2=m2, rr=rr: e.activation(out=rr, in_=m2, func=AF.Ln, bias=constsB[0:1, 0:1]),
                      reads=[("rB_var",), ("c_eps",)], writes=[("rB_lnv",)])
                S.add("scalar", lambda e, rr=rr: e.activation(out=rr, in_=rr, func=AF.Exp, scale=-0.5),
                      reads=[("rB_lnv",)], writes=[("rB_r",)])
                S.add("vector", lambda e, mu=mu, rr=rr, mur=mur: e.tensor_mul(out=mur, in0=mu, in1=rr),
                      reads=[("rB_mu",), ("rB_r",)], writes=[("rB_mur",)])
                S.add("tensor", lambda e, rr=rr, PB=PB: e.matmul(PB[0][:, 0:TB], ones128[:, :], rr, start=True, stop=True),
                      reads=[("rB_r",), ("ones128",)], writes=[("B", 0)])
                S.add("tensor", lambda e, mur=mur, PB=PB: e.matmul(PB[1][:, 0:TB], ones128[:, :], mur, start=True, stop=True),
                      reads=[("rB_mur",), ("ones128",)], writes=[("B", 1)])


            if "binit" in skip:
                b_init_tiles = 0
            else:
                b_init_tiles = b_tiles
            with ExitStack() as ctxI:
                sbI = lambda name, shape, dt: ctxI.enter_context(nc.sbuf_tensor(name, shape, dt))
                rowsB = sbI("rowsB", [1, 3 * TB], F32)
                xbB = sbI("xbB", [128, 2 * TB], BF16)
                x2bB = sbI("x2bB", [128, 2 * TB], BF16)
                xF = sbI("xF", [128, 8 * TB], F32)
                PB_holder["PB"] = [ctxI.enter_context(nc.psum_tensor(f"PBI{i}", [128, 512], F32)) for i in range(4)]
                PB = PB_holder["PB"]
                for i in range(b_init_tiles):
                    S.add("sync", lambda e, i=i: e.dma_start(
                        out=AP(xF, 0, [[8 * TB, 128], [TB, 8], [1, TB]]),
                        in_=AP(x1_d, i * TB, [[N, 128], [128 * N, 8], [1, TB]])),
                        writes=[("xF",)], dma=True)
                    ln_statsB(lambda c: xF[:, c * TB:(c + 1) * TB], [("xF",)], rowsB, xbB, x2bB, False)
                    if "lnapply" in SKIP_SET:
                        continue
                    for c in range(8):
                        S.add("vector", lambda e, c=c, PB=PB: e.tensor_mul(out=t1B[:, :], in0=xF[:, c * TB:(c + 1) * TB],
                                                                    in1=PB[0][:, 0:TB]),
                              reads=[("xF",), ("B", 0)], writes=[("t1B",)])
                        S.add("vector", lambda e, c=c, i=i, PB=PB: e.tensor_sub(out=z_ap(c, i * TB), in0=t1B[:, :],
                                                                         in1=PB[1][:, 0:TB]),
                              reads=[("t1B",), ("B", 1)], writes=[("z", i)])
            if "zinit" in dbg:
                S.add("sync", lambda e: e.dma_start(out=dbg["zinit"].ap(), in_=z_sb[:, :]),
                      reads=[("z", i2) for i2 in range(8)], dma=True)
            S.barrier()

            nlayers = 0 if b_stop == "init" else nlayers
            for l in range(nlayers):
                with ExitStack() as ctxT:
                    sbT = lambda name, shape, dt: ctxT.enter_context(nc.sbuf_tensor(f"{name}_{l}", shape, dt))
                    psT = lambda name, shape, dt: ctxT.enter_context(nc.psum_tensor(f"{name}_{l}", shape, dt))
                    wqkv_sb = sbT("wqkv_sb", [128, 8 * 1536], BF16)
                    wo_sb = sbT("wo_sb", [128, 4 * 1024], BF16)
                    kT = sbT("kT", [128, 4 * 1024], BF16)
                    vT = sbT("vT", [128, 8 * 512], BF16)
                    maskB = sbT("maskB", [128, 640 + 1280], BF16)
                    w0off = [640, 1152, 1536, 1792]
                    qsbc = sbT("qsbc", [128, 512], BF16)
                    ksbc = sbT("ksbc", [128, 512], BF16)
                    zb = sbT("zb", [128, 8 * 128], BF16)
                    q_sb = sbT("q_sb", [128, 512], BF16)
                    k_sb = sbT("k_sb", [128, 512], BF16)
                    sqB = sbT("sqB", [128, 512], BF16)
                    qT = sbT("qT", [128, 512], BF16)
                    eT2 = sbT("eT2", [128, 2 * 640], BF16)
                    e_sb2 = sbT("e_sb2", [128, 2 * 640], BF16)
                    o_fmB = sbT("o_fmB", [128, 4 * 512], BF16)
                    pS2 = [psT("pSa", [128, 640], F32), psT("pSb", [128, 640], F32)]
                    pO_ = psT("pO", [128, 512], F32)
                    pZ_ = psT("pZ", [128, 512], F32)
                    pT2 = [psT("pTa", [128, 1024], BF16), psT("pTb", [128, 1024], BF16)]

                    S.add("sync", lambda e, l=l: e.dma_start(
                        out=AP(wqkv_sb, 0, [[8 * 1536, 128], [1536, 8], [1, 1536]]),
                        in_=AP(wqkvb_d, l * 1024 * 1536, [[1536, 128], [128 * 1536, 8], [1, 1536]])),
                        reads=[("wqkvC", r0) for r0 in range(0, 2048, 128)],
                        writes=[("wqkvB",)], dma=True)
                    S.add("sync", lambda e, l=l: e.dma_start(
                        out=AP(wo_sb, 0, [[4 * 1024, 128], [1024, 4], [1, 1024]]),
                        in_=AP(wob_d, l * 512 * 1024, [[1024, 128], [128 * 1024, 4], [1, 1024]])),
                        reads=[("woC", r0) for r0 in range(0, 1024, 128)],
                        writes=[("woB",)], dma=True)
                    S.add("sync", lambda e: e.dma_start(out=maskB[:, :], in_=maskB_d.ap()),
                          writes=[("maskB",)], dma=True)
                    S.add("gpsimd", lambda e, l=l: e.dma_start(
                        out=qsbc[:, :], in_=AP(qsbc_d, l * 128 * 512, [[512, 128], [1, 512]])),
                        writes=[("qsbc",)], dma=True)
                    S.add("gpsimd", lambda e, l=l: e.dma_start(
                        out=ksbc[:, :], in_=AP(ksbc_d, l * 128 * 512, [[512, 128], [1, 512]])),
                        writes=[("ksbc",)], dma=True)
                    S.add("vector", lambda e: e.memset(kT[:, :], 0.0),
                          writes=[("kT", s2, b2) for s2 in (0, 1) for b2 in range(4)])
                    S.add("vector", lambda e: e.memset(vT[:, :], 0.0),
                          writes=[("vT", s2, b2) for s2 in (0, 1) for b2 in range(4)])
                    S.add("vector", lambda e: e.memset(pa_col[:, :], 0.0), writes=[("pa", 0), ("pa", 1)])

                    for w in range(nwindows):
                        W0 = w * TB
                        S.add("gpsimd", lambda e, w=w, W0=W0: e.tensor_copy(
                            out=pa_col[:, (w % 2) * 8:(w % 2) * 8 + 8],
                            in_=AP(z_sb, W0 + TB - 1, [[8 * N, 128], [N, 8], [1, 1]])),
                            reads=[("z", w)], writes=[("pa", w % 2)])
                        for tt in range(4):
                            T0 = W0 + tt * 128
                            par = (w * 4 + tt) % 2
                            dma_xpos(w * 4 + tt, par)
                            S.add("gpsimd", lambda e, T0=T0: e.tensor_copy(
                                out=zb[:, 0:512],
                                in_=AP(z_sb, T0, [[8 * N, 128], [N, 4], [1, 128]])),
                                reads=[("z", w)], writes=[("zb",)])
                            if tt == 0:
                                S.add("gpsimd", lambda e, w=w: e.tensor_copy(
                                    out=AP(zb, 512, [[8 * 128, 128], [128, 4], [1, 1]]),
                                    in_=pa_col[:, ((w + 1) % 2) * 8 + 4:((w + 1) % 2) * 8 + 8]),
                                    reads=[("pa", (w + 1) % 2)], writes=[("zb",)])
                                S.add("gpsimd", lambda e, T0=T0: e.tensor_copy(
                                    out=AP(zb, 513, [[8 * 128, 128], [128, 4], [1, 127]]),
                                    in_=AP(z_sb, 4 * N + T0, [[8 * N, 128], [N, 4], [1, 127]])),
                                    reads=[("z", w)], writes=[("zb",)])
                            else:
                                S.add("gpsimd", lambda e, T0=T0: e.tensor_copy(
                                    out=zb[:, 512:1024],
                                    in_=AP(z_sb, 4 * N + T0 - 1, [[8 * N, 128], [N, 4], [1, 128]])),
                                    reads=[("z", w)], writes=[("zb",)])
                            for j in range(8):
                                for qi, (bidx, col0) in enumerate(((0, 0), (1, 512), (2, 1024))):
                                    pb = pS2[0] if bidx < 2 else pS2[1]
                                    poff = 0 if bidx % 2 == 0 else 0
                                    tgt = (pS2[0], ("pSx", 0)) if bidx == 0 else (
                                          (pS2[1], ("pSx", 1)) if bidx == 1 else (pO_, ("pOx",)))
                                    S.add("tensor", lambda e, j=j, col0=col0, tgt=tgt: e.matmul(
                                        tgt[0][:, 0:512], zb[:, j * 128:(j + 1) * 128],
                                        wqkv_sb[:, j * 1536 + col0: j * 1536 + col0 + 512],
                                        start=(j == 0), stop=(j == 7), skip_group_check=True),
                                        reads=[("zb",), ("wqkvB",)], writes=[tgt[1]])
                            S.add("scalar", lambda e: e.activation(out=q_sb[:, :], in_=pS2[0][:, 0:512], func=AF.Copy),
                                  reads=[("pSx", 0)], writes=[("q_sb",)])
                            S.add("scalar", lambda e: e.activation(out=k_sb[:, :], in_=pS2[1][:, 0:512], func=AF.Copy),
                                  reads=[("pSx", 1)], writes=[("k_sb",)])
                            S.add("scalar", lambda e, w=w, tt=tt: e.activation(
                                out=vT[:, ((w % 2) * 4 + tt) * 512: ((w % 2) * 4 + tt + 1) * 512],
                                in_=pO_[:, 0:512], func=AF.Copy),
                                reads=[("pOx",)], writes=[("vT", w % 2, tt)])
                            for which, buf, bres, rinv_off, scbc, scres, toff in (
                                    ("q", q_sb, ("q_sb",), 8, qsbc, ("qsbc",), 0),
                                    ("k", k_sb, ("k_sb",), 16, ksbc, ("ksbc",), 128)):
                                S.add("scalar", lambda e, buf=buf, bres=bres: e.activation(out=sqB[:, :], in_=buf[:, :], func=AF.Square),
                                      reads=[bres], writes=[("sqB",)])
                                S.add("vector", lambda e: e.reduce_sum(
                                    out=colsB[:, 0:8], in_=AP(sqB, 0, [[512, 128], [64, 8], [1, 64]]), axis=AX.X),
                                    reads=[("sqB",)], writes=[("ssq",)])
                                S.add("scalar", lambda e, rinv_off=rinv_off: e.activation(
                                    out=colsB[:, rinv_off:rinv_off + 8], in_=colsB[:, 0:8], func=AF.Ln,
                                    bias=constsB[:, 1:2]),
                                    reads=[("ssq",), ("c_eps24",)], writes=[("rinv", which)])
                                S.add("scalar", lambda e, rinv_off=rinv_off: e.activation(
                                    out=colsB[:, rinv_off:rinv_off + 8], in_=colsB[:, rinv_off:rinv_off + 8],
                                    func=AF.Exp, scale=-0.5),
                                    reads=[("rinv", which)], writes=[("rinv", which)])
                                ek = "vector"
                                S.add(ek, lambda e, buf=buf, bres=bres, rinv_off=rinv_off: e.tensor_mul(
                                    out=t1B[:, :], in0=buf[:, :],
                                    in1=AP(colsB, rinv_off, [[24, 128], [1, 8], [0, 64]])),
                                    reads=[bres, ("rinv", which)], writes=[("t1B",)])
                                S.add(ek, lambda e, scbc=scbc, scres=scres: e.tensor_mul(
                                    out=t1B[:, :], in0=t1B[:, :], in1=scbc[:, :]),
                                    reads=[("t1B",), scres], writes=[("t1B",)])
                                S.add(ek, lambda e, par=par, toff=toff: e.tensor_mul(
                                    out=t2B[:, :], in0=t1B[:, :],
                                    in1=AP(xpb, par * 256 + toff, [[512, 128], [0, 8], [1, 64]])),
                                    reads=[("t1B",), ("xpb", par)], writes=[("t2B",)])
                                S.add(ek, lambda e, par=par, toff=toff: e.tensor_mul(
                                    out=t3B[:, :],
                                    in0=AP(t1B, 32, [[512, 128], [64, 8], [-32, 2], [1, 32]]),
                                    in1=AP(xpb, par * 256 + toff + 64, [[512, 128], [0, 8], [1, 64]])),
                                    reads=[("t1B",), ("xpb", par)], writes=[("t3B",)])
                                S.add(ek, lambda e, buf=buf, bres=bres: e.tensor_add(out=buf[:, :], in0=t2B[:, :], in1=t3B[:, :]),
                                      reads=[("t2B",), ("t3B",)], writes=[bres])
                            if "bqr" in dbg and l == 0 and w == 0 and tt == 0:
                                S.add("gpsimd", lambda e: e.dma_start(out=dbg["bqr"].ap(), in_=q_sb[:, :]),
                                      reads=[("q_sb",)], dma=True)
                                S.add("gpsimd", lambda e: e.dma_start(out=dbg["bkr"].ap(), in_=k_sb[:, :]),
                                      reads=[("k_sb",)], dma=True)
                            for hp in range(4):
                                S.add("tensor", lambda e, hp=hp: e.transpose(
                                    pT2[0][:, hp * 128:(hp + 1) * 128], q_sb[:, hp * 128:(hp + 1) * 128], eye128[:, :]),
                                    reads=[("q_sb",), ("eye128",)], writes=[("pT", 0)])
                                S.add("tensor", lambda e, hp=hp: e.transpose(
                                    pT2[0][:, 512 + hp * 128: 512 + (hp + 1) * 128], k_sb[:, hp * 128:(hp + 1) * 128], eye128[:, :]),
                                    reads=[("k_sb",), ("eye128",)], writes=[("pT", 0)])
                            S.add("vector", lambda e: e.tensor_copy(out=qT[:, :], in_=pT2[0][:, 0:512]),
                                  reads=[("pT", 0)], writes=[("qT",)])
                            S.add("vector", lambda e, w=w, tt=tt: e.tensor_copy(
                                out=AP(kT, (w % 2) * 512 + tt * 128, [[4 * 1024, 128], [1024, 4], [1, 128]]),
                                in_=pT2[0][:, 512:1024]),
                                reads=[("pT", 0)], writes=[("kT", w % 2, tt)])
                            spans = []
                            for j5 in range(5):
                                stp = tt + j5
                                slot = ((w + 1) % 2) if stp < 4 else (w % 2)
                                blk = stp if stp < 4 else stp - 4
                                spans.append((j5, slot, blk))

                            runs = []
                            for j5, slot, blk in spans:
                                if runs and runs[-1][1] == slot and runs[-1][2] + runs[-1][3] == blk \
                                        and runs[-1][0] + runs[-1][3] != 4:
                                    runs[-1] = (runs[-1][0], slot, runs[-1][2], runs[-1][3] + 1)
                                else:
                                    runs.append((j5, slot, blk, 1))
                            wide = []
                            for j5, slot, blk, ln_ in runs:
                                while ln_ > 0:
                                    lim = 4 - j5 if j5 < 4 else 1
                                    take = min(ln_, lim)
                                    wide.append((j5, slot, blk, take))
                                    j5 += take
                                    blk += take
                                    ln_ -= take

                            def stage_S(h):
                                hp, parity = h // 2, h % 2
                                hb = parity * 64
                                pS = pS2[h % 2]
                                pRes = ("pSx", h % 2)
                                seen = set()
                                for j5, slot, blk, ln_ in wide:
                                    bank = 0 if j5 < 4 else 1
                                    st_ = bank not in seen
                                    seen.add(bank)
                                    S.add("tensor", lambda e, j5=j5, slot=slot, blk=blk, ln_=ln_, st_=st_, hp=hp, hb=hb, pS=pS: e.matmul(
                                        pS[:, j5 * 128:(j5 + ln_) * 128],
                                        qT[hb:hb + 64, hp * 128:(hp + 1) * 128],
                                        kT[hb:hb + 64, hp * 1024 + slot * 512 + blk * 128: hp * 1024 + slot * 512 + (blk + ln_) * 128],
                                        start=st_, stop=False, skip_group_check=True),
                                        reads=[("qT",)] + [("kT", slot, blk + q_) for q_ in range(ln_)], writes=[pRes])
                                S.add("tensor", lambda e, pS=pS, w=w: e.matmul(
                                    pS[:, 0:512], eye128[:, :], maskB[:, 0:512],
                                    start=False, stop=(w != 0), skip_group_check=True),
                                    reads=[("maskB",), ("eye128",)], writes=[pRes])
                                S.add("tensor", lambda e, pS=pS: e.matmul(
                                    pS[:, 512:640], eye128[:, :], maskB[:, 512:640],
                                    start=False, stop=True, skip_group_check=True),
                                    reads=[("maskB",), ("eye128",)], writes=[pRes])
                                if w == 0:
                                    cut = 512 - 128 * tt
                                    S.add("tensor", lambda e, cut=cut, tt=tt, pS=pS: e.matmul(
                                        pS[:, 0:cut], eye128[:, :], maskB[:, w0off[tt]: w0off[tt] + cut],
                                        start=False, stop=True, skip_group_check=True),
                                        reads=[("maskB",), ("eye128",)], writes=[pRes])

                            def stage_soft1(h):
                                pS = pS2[h % 2]
                                pRes = ("pSx", h % 2)
                                dc = (h % 2) * 4
                                S.add("vector", lambda e, pS=pS, dc=dc: e.reduce_max(
                                    out=den_col[:, dc:dc + 1], in_=pS[:, 0:640], axis=AX.X),
                                    reads=[pRes], writes=[("mx", h % 2)])
                                S.add("vector", lambda e, dc=dc: e.tensor_scalar_mul(
                                    out=den_col[:, dc + 1:dc + 2], in0=den_col[:, dc:dc + 1], scalar1=-8.0),
                                    reads=[("mx", h % 2)], writes=[("mx8", h % 2)])

                            def stage_exp(h):
                                pS = pS2[h % 2]
                                pRes = ("pSx", h % 2)
                                dc = (h % 2) * 4
                                eo = (h % 2) * 640
                                S.add("scalar", lambda e, pS=pS, dc=dc, eo=eo: e.activation(
                                    out=e_sb2[:, eo:eo + 640], in_=pS[:, 0:640], func=AF.Exp, scale=8.0,
                                    bias=den_col[:, dc + 1:dc + 2], accum_out=den_col[:, dc + 2:dc + 3]),
                                    reads=[pRes, ("mx8", h % 2)], writes=[("e1", h % 2), ("d1", h % 2)])

                            def stage_soft2(h):
                                dc = (h % 2) * 4
                                eo = (h % 2) * 640
                                S.add("vector", lambda e, dc=dc: e.reciprocal(
                                    out=den_col[:, dc + 3:dc + 4], in_=den_col[:, dc + 2:dc + 3]),
                                    reads=[("d1", h % 2)], writes=[("rv", h % 2)])
                                S.add("vector", lambda e, dc=dc, eo=eo: e.tensor_scalar_mul(
                                    out=e_sb2[:, eo:eo + 640], in0=e_sb2[:, eo:eo + 640], scalar1=den_col[:, dc + 3:dc + 4]),
                                    reads=[("e1", h % 2), ("rv", h % 2)], writes=[("e1", h % 2)])

                            def stage_T(h):
                                eo = (h % 2) * 640
                                pT = pT2[h % 2]
                                for j5 in range(5):
                                    S.add("tensor", lambda e, j5=j5, eo=eo, pT=pT: e.transpose(
                                        pT[:, j5 * 128:(j5 + 1) * 128], e_sb2[:, eo + j5 * 128: eo + (j5 + 1) * 128], eye128[:, :]),
                                        reads=[("e1", h % 2), ("eye128",)], writes=[("pT", h % 2)])
                                S.add("vector", lambda e, eo=eo, pT=pT: e.tensor_copy(
                                    out=eT2[:, eo:eo + 640], in_=pT[:, 0:640]),
                                    reads=[("pT", h % 2)], writes=[("eT", h % 2)])

                            def stage_PV(h):
                                hp, parity = h // 2, h % 2
                                hb = parity * 64
                                eo = (h % 2) * 640
                                for j5, slot, blk in spans:
                                    kwargs = {}
                                    if parity == 1:
                                        kwargs["tile_position"] = (0, 64)
                                    S.add("tensor", lambda e, j5=j5, slot=slot, blk=blk, h=h, hb=hb, tt=tt, eo=eo, kwargs=kwargs: e.matmul(
                                        pO_[hb:hb + 64, tt * 128:(tt + 1) * 128],
                                        vT[:, (slot * 4 + blk) * 512 + h * 64: (slot * 4 + blk) * 512 + (h + 1) * 64],
                                        eT2[:, eo + j5 * 128: eo + (j5 + 1) * 128],
                                        start=(j5 == 0), stop=(j5 == 4),
                                        skip_group_check=True, **kwargs),
                                        reads=[("eT", h % 2), ("vT", slot, blk)], writes=[("pOx",)])
                                S.add("scalar", lambda e, hp=hp, tt=tt, hb=hb: e.activation(
                                    out=o_fmB[hb:hb + 64, hp * 512 + tt * 128: hp * 512 + (tt + 1) * 128],
                                    in_=pO_[hb:hb + 64, tt * 128:(tt + 1) * 128], func=AF.Copy),
                                    reads=[("pOx",)], writes=[("o_fmB", hp, tt, hb)])

                            stages = [stage_S, stage_soft1, stage_exp, stage_soft2, stage_T, stage_PV]
                            nst = len(stages)
                            for step in range(8 + nst - 1):
                                for si in range(nst - 1, -1, -1):
                                    h = step - si
                                    if 0 <= h < 8:
                                        stages[si](h)
                        if "bofm" in dbg and l == 0 and w == 0:
                            S.add("gpsimd", lambda e: e.dma_start(out=dbg["bofm"].ap(), in_=o_fmB[:, :]),
                                  reads=[("o_fmB", hp2, tt2, b64) for hp2 in range(4) for tt2 in range(4) for b64 in (0, 64)], dma=True)
                        for f in range(8):
                            for j in range(4):
                                S.add("tensor", lambda e, f=f, j=j: e.matmul(
                                    pZ_[:, 0:512], wo_sb[:, j * 1024 + f * 128: j * 1024 + (f + 1) * 128],
                                    o_fmB[:, j * 512:(j + 1) * 512],
                                    start=(j == 0), stop=(j == 3), skip_group_check=True),
                                    reads=[("woB",)] + [("o_fmB", j, tt2, b64) for tt2 in range(4) for b64 in (0, 64)], writes=[("pZx",)])
                            S.add("vector", lambda e, f=f, W0=W0: e.tensor_add(
                                out=z_ap(f, W0), in0=z_ap(f, W0), in1=pZ_[:, 0:512]),
                                reads=[("pZx",), ("z", w)], writes=[("z", w)])
                if ("z0a" in dbg and l == 0) or ("z1a" in dbg and l == 1):
                    key = "z0a" if l == 0 else "z1a"
                    S.add("sync", lambda e, key=key: e.dma_start(out=dbg[key].ap(), in_=z_sb[:, :]),
                          reads=[("z", i2) for i2 in range(8)], dma=True)
                S.barrier()
                if not do_mlp:
                    continue
                with ExitStack() as ctxM:
                    sbM = lambda name, shape, dt: ctxM.enter_context(nc.sbuf_tensor(f"{name}_{l}", shape, dt))
                    psM = lambda name: ctxM.enter_context(nc.psum_tensor(f"{name}_{l}", [128, 512], F32))
                    rowsB = sbM("rowsM", [1, 3 * TB], F32)
                    x2bB = sbM("x2bM", [128, 2 * TB], BF16)
                    hB = sbM("hB", [128, 8 * TB], BF16)
                    lnh = sbM("lnh", [128, 8 * TB], BF16)
                    glu = sbM("glu", [128, 22 * 512], BF16)
                    w1a = sbM("w1a", [128, 2 * 1024], BF16)
                    w1g = sbM("w1g", [128, 2 * 1024], BF16)
                    w2b = sbM("w2b", [128, 4 * 1024], BF16)
                    a_sb = sbM("a_sb", [128, 512], F32)
                    a2_sb = sbM("a2_sb", [128, 512], F32)
                    
                    gg = sbM("gg", [128, 2 * 512], BF16)
                    PBM = [psM(f"PBM{i}") for i in range(8)]
                    PB_holder["PB"] = PBM[0:4]

                    S.add("vector", lambda e: e.memset(pm_col[:, :], 0.0), writes=[("pm", 0), ("pm", 1)])
                    for i in range(8):
                        T0 = i * TB
                        S.add("gpsimd", lambda e, i=i, T0=T0: e.tensor_copy(
                            out=pm_col[:, (i % 2) * 8:(i % 2) * 8 + 8],
                            in_=AP(z_sb, T0 + TB - 1, [[8 * N, 128], [N, 8], [1, 1]])),
                            reads=[("z", i)], writes=[("pm", i % 2)])
                        S.add("gpsimd", lambda e, T0=T0: e.tensor_copy(
                            out=AP(hB, 0, [[8 * TB, 128], [TB, 4], [1, TB]]),
                            in_=AP(z_sb, T0, [[8 * N, 128], [N, 4], [1, TB]])),
                            reads=[("z", i)], writes=[("hB",)])
                        S.add("gpsimd", lambda e, i=i: e.tensor_copy(
                            out=AP(hB, 4 * TB, [[8 * TB, 128], [TB, 4], [1, 1]]),
                            in_=pm_col[:, ((i + 1) % 2) * 8 + 4:((i + 1) % 2) * 8 + 8]),
                            reads=[("pm", (i + 1) % 2)], writes=[("hB",)])
                        S.add("gpsimd", lambda e, T0=T0: e.tensor_copy(
                            out=AP(hB, 4 * TB + 1, [[8 * TB, 128], [TB, 4], [1, TB - 1]]),
                            in_=AP(z_sb, 4 * N + T0, [[8 * N, 128], [N, 4], [1, TB - 1]])),
                            reads=[("z", i)], writes=[("hB",)])
                        ln_statsB(lambda c: hB[:, c * TB:(c + 1) * TB], [("hB",)], rowsB, None, x2bB, True)
                        for c in range(8):
                            S.add("vector", lambda e, c=c: e.tensor_mul(
                                out=t1B[:, :], in0=hB[:, c * TB:(c + 1) * TB], in1=PBM[0][:, 0:TB]),
                                reads=[("hB",), ("B", 0)], writes=[("t1B",)])
                            S.add("vector", lambda e: e.tensor_sub(out=t1B[:, :], in0=t1B[:, :], in1=PBM[1][:, 0:TB]),
                                  reads=[("t1B",), ("B", 1)], writes=[("t1B",)])
                            S.add("vector", lambda e, c=c, l=l: e.tensor_scalar(
                                out=lnh[:, c * TB:(c + 1) * TB], in0=t1B[:, :],
                                scalar1=lngT_sb[:, l * 8 + c: l * 8 + c + 1],
                                scalar2=lnbT_sb[:, l * 8 + c: l * 8 + c + 1],
                                op0=ALU.mult, op1=ALU.add),
                                reads=[("t1B",), ("lngT",), ("lnbT",)], writes=[("lnh", c)])
                        for cc in range(22):
                            wc = 128 if cc < 21 else 42
                            par = cc % 2
                            pa_ = PBM[4 + (cc % 2)]
                            pg_ = PBM[6 + (cc % 2)]
                            ra_, rg_ = ("B", 4 + (cc % 2)), ("B", 6 + (cc % 2))
                            epar = cc % 2
                            S.add("sync", lambda e, cc=cc, wc=wc, par=par, l=l: e.dma_start(
                                out=AP(w1a, par * 1024, [[2 * 1024, 128], [128, 8], [1, wc]]),
                                in_=AP(w1b_d, l * 1024 * 5460 + cc * 128, [[5460, 128], [128 * 5460, 8], [1, wc]])),
                                writes=[("w1a", par)], dma=True)
                            S.add("sync", lambda e, cc=cc, wc=wc, par=par, l=l: e.dma_start(
                                out=AP(w1g, par * 1024, [[2 * 1024, 128], [128, 8], [1, wc]]),
                                in_=AP(w1b_d, l * 1024 * 5460 + 2730 + cc * 128, [[5460, 128], [128 * 5460, 8], [1, wc]])),
                                writes=[("w1g", par)], dma=True)
                            for j in range(8):
                                S.add("tensor", lambda e, wc=wc, par=par, j=j, pa_=pa_: e.matmul(
                                    pa_[0:wc, 0:512], w1a[:, par * 1024 + j * 128: par * 1024 + j * 128 + wc],
                                    lnh[:, j * TB:(j + 1) * TB], start=(j == 0), stop=(j == 7), skip_group_check=True),
                                    reads=[("w1a", par), ("lnh", j)], writes=[ra_])
                                S.add("tensor", lambda e, wc=wc, par=par, j=j, pg_=pg_: e.matmul(
                                    pg_[0:wc, 0:512], w1g[:, par * 1024 + j * 128: par * 1024 + j * 128 + wc],
                                    lnh[:, j * TB:(j + 1) * TB], start=(j == 0), stop=(j == 7), skip_group_check=True),
                                    reads=[("w1g", par), ("lnh", j)], writes=[rg_])
                            abuf = a_sb if epar == 0 else a2_sb
                            S.add("vector", lambda e, cc=cc, wc=wc, l=l, pa_=pa_, abuf=abuf: e.tensor_scalar_add(
                                out=abuf[0:wc, :], in0=pa_[0:wc, 0:512],
                                scalar1=b1T_sb[0:wc, l * 44 + cc: l * 44 + cc + 1]),
                                reads=[ra_, ("b1T",)], writes=[("a_sb", epar)])
                            S.add("scalar", lambda e, cc=cc, wc=wc, l=l, pg_=pg_, epar=epar: e.activation(
                                out=gg[0:wc, epar * 512:epar * 512 + 512], in_=pg_[0:wc, 0:512], func=AF.Gelu,
                                bias=b1T_sb[0:wc, l * 44 + 22 + cc: l * 44 + 22 + cc + 1]),
                                reads=[rg_, ("b1T",)], writes=[("gg", epar)])
                            S.add("vector", lambda e, cc=cc, wc=wc, epar=epar, abuf=abuf: e.tensor_mul(
                                out=glu[0:wc, cc * 512:(cc + 1) * 512],
                                in0=abuf[0:wc, :],
                                in1=gg[0:wc, epar * 512:epar * 512 + 512]),
                                reads=[("a_sb", epar), ("gg", epar)], writes=[("glu", cc)])
                        for cc in range(22):
                            wc = 128 if cc < 21 else 42
                            par = cc % 4
                            S.add("sync", lambda e, cc=cc, wc=wc, par=par, l=l: e.dma_start(
                                out=AP(w2b, par * 1024, [[4 * 1024, wc], [1, 1024]]),
                                in_=AP(w2b_d, l * 2730 * 1024 + cc * 128 * 1024, [[1024, wc], [1, 1024]])),
                                writes=[("w2b", par)], dma=True)
                            for f in range(8):
                                S.add("tensor", lambda e, cc=cc, wc=wc, par=par, f=f: e.matmul(
                                    PBM[f][:, 0:512], w2b[0:wc, par * 1024 + f * 128:par * 1024 + (f + 1) * 128],
                                    glu[0:wc, cc * 512:(cc + 1) * 512],
                                    start=(cc == 0), stop=(cc == 21), skip_group_check=True),
                                    reads=[("w2b", par), ("glu", cc)], writes=[("B", f)])
                        for f in range(8):
                            S.add("vector", lambda e, f=f, T0=T0, l=l: e.scalar_tensor_tensor(
                                out=z_ap(f, T0), in0=PBM[f][:, 0:512],
                                scalar=b2T_sb[:, l * 8 + f: l * 8 + f + 1],
                                in1=z_ap(f, T0), op0=ALU.add, op1=ALU.add),
                                reads=[("B", f), ("b2T",), ("z", i)], writes=[("z", i)])
                if ("z0f" in dbg and l == 0) or ("z1f" in dbg and l == 1):
                    key = "z0f" if l == 0 else "z1f"
                    S.add("sync", lambda e, key=key: e.dma_start(out=dbg[key].ap(), in_=z_sb[:, :]),
                          reads=[("z", i2) for i2 in range(8)], dma=True)
                S.barrier()

            if "bfinal" in skip:
                b_tiles = 0
            with ExitStack() as ctxF:
                sbF = lambda name, shape, dt: ctxF.enter_context(nc.sbuf_tensor(name, shape, dt))
                rowsB = sbF("rowsF", [1, 3 * TB], F32)
                xbB = sbF("xbF", [128, 2 * TB], BF16)
                x2bB = sbF("x2bF", [128, 2 * TB], BF16)
                xF = sbF("xFF", [128, 8 * TB], F32)
                PB_holder["PB"] = [ctxF.enter_context(nc.psum_tensor(f"PBF{i}", [128, 512], F32)) for i in range(4)]
                PB = PB_holder["PB"]
                for i in range(b_tiles):
                    S.add("sync", lambda e, i=i: e.dma_start(
                        out=AP(xF, 0, [[8 * TB, 128], [TB, 8], [1, TB]]),
                        in_=AP(x1_d, i * TB, [[N, 128], [128 * N, 8], [1, TB]])),
                        writes=[("xF",)], dma=True)
                    ln_statsB(lambda c, i=i: z_ap(c, i * TB), [("z", i)], rowsB, xbB, x2bB, False)
                    for c in range(8):
                        S.add("vector", lambda e, c=c, i=i, PB=PB: e.tensor_mul(
                            out=t1B[:, :], in0=z_ap(c, i * TB), in1=PB[0][:, 0:TB]),
                            reads=[("z", i), ("B", 0)], writes=[("t1B",)])
                        S.add("vector", lambda e, PB=PB: e.tensor_sub(out=t1B[:, :], in0=t1B[:, :], in1=PB[1][:, 0:TB]),
                              reads=[("t1B",), ("B", 1)], writes=[("t1B",)])
                        S.add("vector", lambda e, c=c: e.tensor_add(
                            out=xF[:, c * TB:(c + 1) * TB], in0=xF[:, c * TB:(c + 1) * TB], in1=t1B[:, :]),
                            reads=[("t1B",), ("xF",)], writes=[("xF",)])
                    S.add("sync", lambda e, i=i: e.dma_start(
                        out=AP(y_d, i * TB, [[N, 128], [128 * N, 8], [1, TB]]),
                        in_=AP(xF, 0, [[8 * TB, 128], [TB, 8], [1, TB]])),
                        reads=[("xF",)], dma=True)

        S.emit()
    return nc, dbg


def make_shared_inputs(ins):
    b1 = np.asarray(ins["L0_ffn_b1"], np.float32)
    b1T = np.zeros((128, 88), np.float32)
    for l in range(2):
        for cc in range(22):
            wc = 128 if cc < 21 else 42
            b1T[0:wc, l * 44 + cc] = b1[l, cc * 128: cc * 128 + wc]
            b1T[0:wc, l * 44 + 22 + cc] = b1[l, 2730 + cc * 128: 2730 + cc * 128 + wc]
    def cols8(v):
        out = np.zeros((128, 16), np.float32)
        for l in range(2):
            for c in range(8):
                out[:, l * 8 + c] = v[l, c * 128:(c + 1) * 128]
        return out
    qs = np.asarray(ins["L0_ffn_q_scale"], np.float32)
    ks = np.asarray(ins["L0_ffn_k_scale"], np.float32)
    qsbc = np.broadcast_to(np.tile(qs, (1, 8))[:, None, :], (2, 128, 512)).copy()
    ksbc = np.broadcast_to(np.tile(ks, (1, 8))[:, None, :], (2, 128, 512)).copy()
    return {
        "wqT": np.ascontiguousarray(np.asarray(ins["L0_attn_Wq"], np.float32).T),
        "wkT": np.ascontiguousarray(np.asarray(ins["L0_attn_Wk"], np.float32).T),
        "wvT": np.ascontiguousarray(np.asarray(ins["L0_attn_Wv"], np.float32).T),
        "woT": np.ascontiguousarray(np.asarray(ins["L0_attn_Wo"], np.float32).T),
        "boT": np.ascontiguousarray(np.asarray(ins["L0_attn_bo"], np.float32).reshape(8, 128).T),
        "wqkv": np.ascontiguousarray(np.asarray(ins["L0_ffn_wqkv"], np.float32)),
        "wo_in": np.ascontiguousarray(np.asarray(ins["L0_ffn_wo"], np.float32)),
        "w1_in": np.ascontiguousarray(np.asarray(ins["L0_ffn_w1"], np.float32)),
        "w2_in": np.ascontiguousarray(np.asarray(ins["L0_ffn_w2"], np.float32)),
        "b1T": b1T,
        "b2T": cols8(np.asarray(ins["L0_ffn_b2"], np.float32)),
        "lngT": cols8(np.asarray(ins["L0_ffn_ln_g"], np.float32)),
        "lnbT": cols8(np.asarray(ins["L0_ffn_ln_b"], np.float32)),
        "qsbcT": qsbc,
        "ksbcT": ksbc,
    }


def make_inputs_core(x_core, shared):
    d = dict(shared)
    d["x0"] = np.ascontiguousarray(x_core.T)
    return d



_PROGRAM_CACHE = {}


def kernel(**inputs):
    from concourse.bass_utils import run_bass_kernel_spmd
    ins = {k: np.asarray(v) for k, v in inputs.items()}
    if "nc" not in _PROGRAM_CACHE:
        _PROGRAM_CACHE["nc"] = build_program(debug=())[0]
    nc = _PROGRAM_CACHE["nc"]
    shared = make_shared_inputs(ins)
    x = np.asarray(ins["L0_attn_x"], np.float32)
    in_maps = [make_inputs_core(x[b], shared) for b in range(8)]
    res = run_bass_kernel_spmd(nc, in_maps, core_ids=list(range(8)))
    out = np.stack([np.ascontiguousarray(r["y"].T) for r in res.results], axis=0)
    return out.astype(np.float32)


# version 24
# speedup vs baseline: 1.0076x; 1.0018x over previous
import numpy as np
import ml_dtypes
import concourse.bass as bass
import concourse.mybir as mybir
from contextlib import ExitStack

F32 = mybir.dt.float32
BF16 = mybir.dt.bfloat16
AF = mybir.ActivationFunctionType
ALU = mybir.AluOpType
AX = mybir.AxisListType
AP = bass.AP

N = 4096
D = 1024
H = 16
HD = 64
TA = 256
NTA = N // TA
NEG = -1.0e30
C_OUTER = 8.0

ENGINES = ("sync", "gpsimd", "scalar", "vector", "tensor")
SKIP_SET = set()


class Sched:
    def __init__(self, nc):
        self.nc = nc
        self.ops = []
        self.last_writer = {}
        self.readers = {}
        self.NSEM = 6
        self.sem_comp = {e: nc.alloc_semaphore(f"sc_{e}") for e in ENGINES}
        self.sem_dma = {e: [nc.alloc_semaphore(f"sd_{e}{i}") for i in range(self.NSEM)]
                        for e in ("sync", "gpsimd", "scalar")}

    def add(self, eng, fn, reads=(), writes=(), dma=False, tag=""):
        op = {"eng": eng, "fn": fn, "dma": dma, "idx": len(self.ops), "deps": set(),
              "need_inc": False, "inc_no": None, "dma_no": None, "tag": tag, "barrier": False}
        deps = set()
        for r in reads:
            w = self.last_writer.get(r)
            if w is not None:
                deps.add(w["idx"])
        for r in writes:
            w = self.last_writer.get(r)
            if w is not None:
                deps.add(w["idx"])
            for rd in self.readers.get(r, ()):
                deps.add(rd["idx"])
        deps.discard(op["idx"])
        op["deps"] = deps
        self.ops.append(op)
        for r in reads:
            self.readers.setdefault(r, []).append(op)
        for r in writes:
            self.last_writer[r] = op
            self.readers[r] = []
        return op

    def cut(self):
        pass

    def barrier(self):
        op = {"eng": None, "fn": None, "dma": False, "idx": len(self.ops), "deps": set(),
              "need_inc": False, "inc_no": None, "dma_no": None, "tag": "barrier", "barrier": True}
        self.ops.append(op)
        self.last_writer = {}
        self.readers = {}
        return op

    def emit(self):
        nc = self.nc
        dma_count = {e: 0 for e in self.sem_dma}
        for op in self.ops:
            if op["barrier"]:
                continue
            if op["dma"]:
                dma_count[op["eng"]] += 1
                op["dma_no"] = dma_count[op["eng"]]
        SYNC_SAME = ("vector", "scalar", "gpsimd")
        last_op = {e: None for e in ENGINES}
        for op in self.ops:
            if op["barrier"]:
                for e in ENGINES:
                    if last_op[e] is not None and not last_op[e]["dma"]:
                        last_op[e]["need_inc"] = True
                    last_op[e] = None
                continue
            if op["eng"] in last_op:
                last_op[op["eng"]] = op
        for op in self.ops:
            if op["barrier"]:
                continue
            for d in op["deps"]:
                dep = self.ops[d]
                if dep["barrier"] or dep["dma"]:
                    continue
                if (dep["eng"] != op["eng"] or op["dma"]
                        or (dep["eng"] == op["eng"] and dep["eng"] in SYNC_SAME)):
                    dep["need_inc"] = True
        inc_no = {e: 0 for e in ENGINES}
        dma_no = {e: 0 for e in self.sem_dma}
        barrier_info = {}
        for op in self.ops:
            if op["barrier"]:
                pre = dict(inc_no)
                for e in ENGINES:
                    inc_no[e] += 1
                barrier_info[op["idx"]] = (dict(inc_no), dict(dma_no), pre)
                continue
            if op["dma"]:
                dma_no[op["eng"]] = op["dma_no"]
            elif op["need_inc"]:
                inc_no[op["eng"]] += 1
                op["inc_no"] = inc_no[op["eng"]]
        self.total_incs = dict(inc_no)
        self.total_dmas = dict(dma_no)

        def dma_wait_args(qe, dma_no):
            i = dma_no - 1
            return self.sem_dma[qe][i % self.NSEM], 16 * (i // self.NSEM + 1)

        def emit_engine(eng_name, eng):
            clock = {}
            def wait(sem, val):
                key = sem.num
                if clock.get(key, -1) >= val:
                    return
                eng.wait_ge(sem, val)
                clock[key] = val
            def wait_dma_full(qe, cnt):
                for k in range(cnt, max(cnt - self.NSEM, 0), -1):
                    s, v = dma_wait_args(qe, k)
                    wait(s, v)
            for op in self.ops:
                if op["barrier"]:
                    binc, bdma, pre = barrier_info[op["idx"]]
                    wait(self.sem_comp[eng_name], pre[eng_name])
                    for qe, cnt in bdma.items():
                        wait_dma_full(qe, cnt)
                    eng.nop().then_inc(self.sem_comp[eng_name], 1)
                    for e2 in ENGINES:
                        wait(self.sem_comp[e2], binc[e2])
                    continue
                if op["eng"] != eng_name:
                    continue
                for d in sorted(op["deps"]):
                    dep = self.ops[d]
                    if dep["barrier"]:
                        continue
                    if dep["dma"]:
                        s, v = dma_wait_args(dep["eng"], dep["dma_no"])
                        wait(s, v)
                    elif (dep["eng"] != eng_name or op["dma"]
                          or (dep["eng"] == eng_name and eng_name in ("vector", "scalar", "gpsimd"))):
                        if dep["inc_no"] is not None:
                            wait(self.sem_comp[dep["eng"]], dep["inc_no"])
                inst = op["fn"](eng)
                if op["dma"]:
                    s, _ = dma_wait_args(eng_name, op["dma_no"])
                    inst.then_inc(s, 16)
                elif op["need_inc"]:
                    inst.then_inc(self.sem_comp[eng_name], 1)
            for qe, cnt in self.total_dmas.items():
                wait_dma_full(qe, cnt)
            for e2 in ENGINES:
                if self.total_incs[e2]:
                    wait(self.sem_comp[e2], self.total_incs[e2])

        with nc.Block() as block:
            block.sync(lambda eng: emit_engine("sync", eng))
            block.gpsimd(lambda eng: emit_engine("gpsimd", eng))
            block.scalar(lambda eng: emit_engine("scalar", eng))
            block.vector(lambda eng: emit_engine("vector", eng))
            block.tensor(lambda eng: emit_engine("tensor", eng))



def rope_tables():
    f = np.arange(32, dtype=np.float64)
    inv = 1.0 / (10000.0 ** (2 * f / 64))
    t = np.arange(N, dtype=np.float64)
    ang = t[:, None] * inv[None, :]
    cos = np.cos(ang)
    sin = np.sin(ang)
    ctab = np.zeros((128, N), dtype=np.float64)
    stab = np.zeros((128, N), dtype=np.float64)
    for p in range(128):
        d = p % 64
        fi = d // 2
        ctab[p] = cos[:, fi]
        stab[p] = (-1.0 if d % 2 == 0 else 1.0) * sin[:, fi]
    return ctab.astype(ml_dtypes.bfloat16), stab.astype(ml_dtypes.bfloat16)


def outer_slide_mask():
    ki = np.arange(128)[:, None]
    u = np.arange(896)[None, :]
    return np.where(ki > u - 384, NEG, 0.0).astype(ml_dtypes.bfloat16)


def swap_matrix():
    P = np.zeros((128, 128), dtype=ml_dtypes.bfloat16)
    for i in range(128):
        P[i ^ 1, i] = 1.0
    return P


def xpos_tables():
    d = 64
    half = np.arange(0, d, 2, dtype=np.float64)
    inv = 1.0 / 10000.0 ** (half / d)
    t = np.arange(N, dtype=np.float64)
    fr = t[:, None] * inv[None, :]
    fr = np.concatenate([fr, fr], axis=1)
    base = (half + 0.4 * d) / (1.4 * d)
    power = (t - N // 2) / 256.0
    sc = base[None, :] ** power[:, None]
    sc = np.concatenate([sc, sc], axis=1)
    sign = np.where(np.arange(64) < 32, -1.0, 1.0)[None, :]
    cos, sin = np.cos(fr), np.sin(fr)
    tabs = np.concatenate([cos * sc, sign * sin * sc, cos / sc, sign * sin / sc], axis=1)
    return tabs.astype(np.float32)


def inner_masks():
    li = np.arange(128)[:, None]
    kk = np.arange(640)[None, :]
    base = np.where((kk >= li) & (kk <= li + 512), 0.0, NEG)
    w0 = [np.full((128, 512 - 128 * s), NEG) for s in range(4)]
    out = np.concatenate([base] + w0, axis=1)
    return out.astype(ml_dtypes.bfloat16)



def build_program(debug=(), b_stop=None, nwindows=8, do_mlp=True, nlayers=2, a_tiles=8, b_tiles=8, skip=()):
    nc = bass.Bass("TRN2", target_bir_lowering=False)
    S = Sched(nc)

    x0_d = nc.dram_tensor("x0", [D, N], F32, kind="ExternalInput")
    wqT_d = nc.dram_tensor("wqT", [D, D], F32, kind="ExternalInput")
    wkT_d = nc.dram_tensor("wkT", [D, D], F32, kind="ExternalInput")
    wvT_d = nc.dram_tensor("wvT", [D, D], F32, kind="ExternalInput")
    woT_d = nc.dram_tensor("woT", [D, D], F32, kind="ExternalInput")
    bo_d = nc.dram_tensor("boT", [128, 8], F32, kind="ExternalInput")

    x1_d = nc.dram_tensor("x1scratch", [D, N], F32, kind="Internal")
    lnx_d = nc.dram_tensor("lnxscratch", [D, N], BF16, kind="Internal")
    y_d = nc.dram_tensor("y", [D, N], F32, kind="ExternalOutput")
    wqkv_d = nc.dram_tensor("wqkv", [2, 1024, 1536], F32, kind="ExternalInput")
    woB_d = nc.dram_tensor("wo_in", [2, 512, 1024], F32, kind="ExternalInput")
    w1_d = nc.dram_tensor("w1_in", [2, 1024, 5460], F32, kind="ExternalInput")
    w2_d = nc.dram_tensor("w2_in", [2, 2730, 1024], F32, kind="ExternalInput")
    wqkvb_d = nc.dram_tensor("wqkvbd", [2, 1024, 1536], BF16, kind="Internal")
    wob_d = nc.dram_tensor("wobd", [2, 512, 1024], BF16, kind="Internal")
    w1b_d = nc.dram_tensor("w1bd", [2, 1024, 5460], BF16, kind="Internal")
    w2b_d = nc.dram_tensor("w2bd", [2, 2730, 1024], BF16, kind="Internal")
    b1T_d = nc.dram_tensor("b1T", [128, 88], F32, kind="ExternalInput")
    b2T_d = nc.dram_tensor("b2T", [128, 16], F32, kind="ExternalInput")
    lngT_d = nc.dram_tensor("lngT", [128, 16], F32, kind="ExternalInput")
    lnbT_d = nc.dram_tensor("lnbT", [128, 16], F32, kind="ExternalInput")
    qsbc_d = nc.dram_tensor("qsbcT", [2, 128, 512], F32, kind="ExternalInput")
    ksbc_d = nc.dram_tensor("ksbcT", [2, 128, 512], F32, kind="ExternalInput")
    xpos_d = nc.inline_tensor(xpos_tables(), name="xpostab")
    maskB_d = nc.inline_tensor(inner_masks(), name="maskinner")

    ctab, stab = rope_tables()
    cos_d = nc.inline_tensor(ctab, name="ropecos")
    sin_d = nc.inline_tensor(stab, name="ropesin")
    maskslide_d = nc.inline_tensor(outer_slide_mask(), name="maskslide")
    pswap_d = nc.inline_tensor(swap_matrix(), name="pswap")
    eye128_d = nc.inline_tensor(np.eye(128, dtype=ml_dtypes.bfloat16), name="eye128")
    onescol_d = nc.inline_tensor(np.ones((128, 1), dtype=ml_dtypes.bfloat16), name="onescol")
    ones64_d = nc.inline_tensor(np.ones((1, 64), dtype=np.float32), name="ones64")
    ones128_d = nc.inline_tensor(np.ones((1, 128), dtype=np.float32), name="ones128")

    dbg = {}
    for name, shape in (("kfm", [128, 8 * N]), ("vtm", [128, 32 * 1040]),
                        ("ln1", [128, 8 * TA]), ("qfm", [128, 8 * TA]),
                        ("x1", [D, N]), ("ofm", [128, 8 * TA]), ("u", [128, 8 * TA]),
                        ("rows2", [1, 4 * TA]),
                        ("zinit", [128, 8 * N]), ("z0a", [128, 8 * N]), ("z0f", [128, 8 * N]),
                        ("z1a", [128, 8 * N]), ("z1f", [128, 8 * N]),
                        ("bqr", [128, 512]), ("bkr", [128, 512]), ("bofm", [128, 4 * 512]),
                        ("be", [128, 640]), ("beT", [128, 640]), ("bden", [128, 8]),
                        ("be1", [128, 640]),
                        ("bkT", [128, 4 * 1024]), ("bvT", [128, 8 * 512])):
        if name in debug:
            dbg[name] = nc.dram_tensor("dbg_" + name, shape, F32, kind="ExternalOutput")

    with ExitStack() as ctx:
        sb = lambda name, shape, dt: ctx.enter_context(nc.sbuf_tensor(name, shape, dt))

        pswap = sb("sb_pswap", [128, 128], BF16)
        eye128 = sb("sb_eye128", [128, 128], BF16)
        onescol = sb("sb_onescol", [128, 1], BF16)
        ones128 = sb("sb_ones128", [1, 128], F32)
        bo_sb = sb("bo", [128, 8], F32)
        onesall = sb("sb_onesall", [128, 64], F32)
        consts = sb("sb_consts", [128, 2], F32)
        S.add("vector", lambda e: e.memset(onesall[:, :], 1.0), writes=[("onesall",)])
        S.add("vector", lambda e: e.memset(consts[:, 0:1], 1e-5), writes=[("c_eps",)])
        S.add("vector", lambda e: e.memset(consts[:, 1:2], -C_OUTER), writes=[("c_negC",)])

        for dst, src, res in ((pswap, pswap_d, "pswap"), (eye128, eye128_d, "eye128"),
                              (onescol, onescol_d, "onescol"),
                              (ones128, ones128_d, "ones128"), (bo_sb, bo_d, "bo")):
            S.add("sync", lambda e, dst=dst, src=src: e.dma_start(out=dst[:, :], in_=src.ap()),
                  writes=[(res,)], dma=True)


        _padc = [0]

        def pad_op(eng, buf, cols=256, rows=(0, 128)):
            _padc[0] += 1
            k = _padc[0]
            r0, r1 = rows
            if eng == "scalar":
                S.add("scalar", lambda e, buf=buf, cols=cols, r0=r0, r1=r1: e.activation(
                    out=buf[r0:r1, 0:cols], in_=buf[r0:r1, 0:cols], func=AF.Copy),
                    writes=[("pad", k)])
            else:
                S.add(eng, lambda e, buf=buf, cols=cols, r0=r0, r1=r1: e.tensor_copy(
                    out=buf[r0:r1, 0:cols], in_=buf[r0:r1, 0:cols]),
                    writes=[("pad", k)])

        TW = 512
        THF = 256
        NTW = N // TW
        with ExitStack() as ctxA:
            sbA = lambda name, shape, dt: ctxA.enter_context(nc.sbuf_tensor(name, shape, dt))
            k_fm = sbA("k_fm", [128, 8 * N], BF16)
            v_tm = sbA("v_tm", [128, 32 * 1040], BF16)
            wA = sbA("wA", [128, 8 * 1024], BF16)
            wB = sbA("wB", [128, 8 * 1024], BF16)

            def dma_wT(src_d, dst, res):
                S.add("gpsimd", lambda e, src_d=src_d, dst=dst: e.dma_start(
                    out=AP(dst, 0, [[8 * 1024, 128], [1024, 8], [1, 1024]]),
                    in_=AP(src_d, 0, [[D, 128], [128 * D, 8], [1, 1024]])),
                    writes=[(res,)], dma=True)

            def phaseA_sub(ctxS, is_a2):
                sbS = lambda name, shape, dt: ctxS.enter_context(
                    nc.sbuf_tensor(("a2_" if is_a2 else "a1_") + name, shape, dt))
                psS = lambda name, shape: ctxS.enter_context(
                    nc.psum_tensor(("a2p_" if is_a2 else "a1p_") + name, shape, F32))
                st = {}
                if is_a2:
                    st["lq"] = sbS("lq", [128, 8 * TW], BF16)
                    st["o_fm"] = sbS("o_fm", [128, 8 * TW], BF16)
                    st["qfu"] = sbS("qfu", [128, 8 * TW], BF16)
                else:
                    st["lq"] = sbS("lq", [128, 2 * 8 * TW], BF16)
                    st["xau"] = sbS("xau", [128, 8 * THF], F32)
                st["ropec"] = sbS("ropec", [128, TW], BF16)
                st["ropes"] = sbS("ropes", [128, TW], BF16)
                st["xb"] = sbS("xb", [128, 2 * THF], BF16)
                st["x2b"] = sbS("x2b", [128, 2 * THF], BF16)
                st["rows"] = sbS("rows", [1, 3 * THF], F32)
                st["kraw"] = sbS("kraw", [128, 2 * TW], BF16)
                st["t1"] = sbS("t1", [128, TW], F32)
                st["t2"] = sbS("t2", [128, TW], F32)
                if is_a2:
                    st["e_sb"] = sbS("e_sb", [128, 4 * TW], BF16)
                    st["maskd"] = sbS("maskd", [128, 896], BF16)
                    st["bc"] = sbS("bc", [128, TW], F32)
                if is_a2:
                    st["P"] = [psS("P0", [128, 512]), psS("P1", [128, 512]),
                               psS("P2", [128, 512]), psS("P3", [128, 512]),
                               psS("P45", [128, 1024]), psS("P67", [128, 1024])]
                else:
                    st["P"] = [psS(f"P{i}", [128, 512]) for i in range(8)]
                return st

            def ln_half(st, x_src_dma_fn, dst_col_fn, tag, xbuf=None, xres=("xau",), lq_res=("lq",), stats_ix=(6, 7)):
                P = st["P"]
                rows = st["rows"]
                xb, x2b = st["xb"], st["x2b"]
                xau = xbuf if xbuf is not None else st["xau"]
                if x_src_dma_fn is not None:
                    x_src_dma_fn()
                si0, si1 = stats_ix
                for c in range(8):
                    sl = c % 2
                    ss = slice(sl * THF, (sl + 1) * THF)
                    cs = slice(c * THF, (c + 1) * THF)
                    S.add("vector", lambda e, cs=cs, ss=ss: e.tensor_copy(out=xb[:, ss], in_=xau[:, cs]),
                          reads=[xres], writes=[("xb", sl)])
                    S.add("scalar", lambda e, cs=cs, ss=ss: e.activation(out=x2b[:, ss], in_=xau[:, cs], func=AF.Square),
                          reads=[xres], writes=[("x2b", sl)])
                    S.add("tensor", lambda e, ss=ss, c=c, si0=si0: e.matmul(
                        P[si0][0:1, 0:THF], onescol[:, :], xb[:, ss], start=(c == 0), stop=(c == 7)),
                        reads=[("xb", sl), ("onescol",)], writes=[("P", si0)])
                    S.add("tensor", lambda e, ss=ss, c=c, si1=si1: e.matmul(
                        P[si1][0:1, 0:THF], onescol[:, :], x2b[:, ss], start=(c == 0), stop=(c == 7)),
                        reads=[("x2b", sl), ("onescol",)], writes=[("P", si1)])
                mu = rows[:, 0 * THF:1 * THF]
                m2 = rows[:, 1 * THF:2 * THF]
                rr = m2
                mur = rows[:, 2 * THF:3 * THF]
                S.add("vector", lambda e, mu=mu, si0=si0: e.tensor_scalar_mul(out=mu, in0=P[si0][0:1, 0:THF], scalar1=1.0 / D),
                      reads=[("P", si0)], writes=[("r_mu",)])
                S.add("vector", lambda e, mu=mu, m2=m2: e.scalar_tensor_tensor(out=m2, in0=mu, scalar=-1.0, in1=mu,
                                                                 op0=ALU.mult, op1=ALU.mult),
                      reads=[("r_mu",)], writes=[("r_m2",)])
                S.add("vector", lambda e, m2=m2, si1=si1: e.scalar_tensor_tensor(out=m2, in0=P[si1][0:1, 0:THF], scalar=1.0 / D,
                                                                 in1=m2, op0=ALU.mult, op1=ALU.add),
                      reads=[("P", si1), ("r_m2",)], writes=[("r_var",)])
                S.add("scalar", lambda e, m2=m2, rr=rr: e.activation(out=rr, in_=m2, func=AF.Ln, bias=consts[0:1, 0:1]),
                      reads=[("r_var",), ("c_eps",)], writes=[("r_lnv",)])
                S.add("scalar", lambda e, rr=rr: e.activation(out=rr, in_=rr, func=AF.Exp, scale=-0.5),
                      reads=[("r_lnv",)], writes=[("r_r",)])
                S.add("vector", lambda e, mu=mu, rr=rr, mur=mur: e.tensor_mul(out=mur, in0=mu, in1=rr),
                      reads=[("r_mu",), ("r_r",)], writes=[("r_mur",)])
                S.add("tensor", lambda e, rr=rr: e.matmul(P[0][:, 0:THF], ones128[:, :], rr, start=True, stop=True),
                      reads=[("r_r",), ("ones128",)], writes=[("P", 0)])
                S.add("tensor", lambda e, mur=mur: e.matmul(P[1][:, 0:THF], ones128[:, :], mur, start=True, stop=True),
                      reads=[("r_mur",), ("ones128",)], writes=[("P", 1)])
                if tag == "ln":
                    lq = st["lq"]
                    t1 = st["t1"]
                    for c in range(8):
                        cs = slice(c * THF, (c + 1) * THF)
                        S.add("vector", lambda e, cs=cs: e.tensor_mul(out=t1[:, 0:THF], in0=xau[:, cs], in1=P[0][:, 0:THF]),
                              reads=[xres, ("P", 0)], writes=[("t1",)])
                        S.add("vector", lambda e, c=c, cs=cs: e.tensor_sub(
                            out=lq[:, dst_col_fn(c)], in0=t1[:, 0:THF], in1=P[1][:, 0:THF]),
                            reads=[("t1",), ("P", 1)], writes=[lq_res])
                else:
                    for c in range(8):
                        cs = slice(c * THF, (c + 1) * THF)
                        S.add("vector", lambda e, cs=cs: e.tensor_mul(out=xau[:, cs], in0=xau[:, cs], in1=P[0][:, 0:THF]),
                              reads=[xres, ("P", 0)], writes=[xres])
                        S.add("vector", lambda e, cs=cs: e.tensor_sub(out=xau[:, cs], in0=xau[:, cs], in1=P[1][:, 0:THF]),
                              reads=[xres, ("P", 1)], writes=[xres])

            def proj_rope(st, wsb, wres, out_sb, out_res, out_col_fn, ti, lq_base=0, lq_res=("lq",), lq_pitch=None,
                          adder=None, p_proj=2, p_swap=3, c_list=None, out_res_fn=None):
                A2_ = adder if adder is not None else S
                P = st["P"]
                lq, kraw, t1, t2 = st["lq"], st["kraw"], st["t1"], st["t2"]
                if lq_pitch is None:
                    lq_pitch = lq.shape[1]
                ropec, ropes = st["ropec"], st["ropes"]
                for c in (c_list if c_list is not None else range(8)):
                    for j in range(8):
                        A2_.add("tensor", lambda e, c=c, j=j, wsb=wsb, lq=lq, lq_base=lq_base, lq_pitch=lq_pitch: e.matmul(
                            P[p_proj][:, :], wsb[:, j * 1024 + c * 128: j * 1024 + (c + 1) * 128],
                            AP(lq, lq_base + j * TW, [[lq_pitch, 128], [1, TW]]),
                            start=(j == 0), stop=(j == 7), skip_group_check=True),
                            reads=[lq_res, (wres,)], writes=[("P", p_proj)])
                        if j == 3:
                            A2_.cut()
                    ks = c % 2
                    ksl = slice(ks * TW, (ks + 1) * TW)
                    A2_.cut()
                    A2_.add("scalar", lambda e, ksl=ksl: e.activation(
                        out=kraw[:, ksl], in_=P[p_proj][:, :], func=AF.Copy),
                        reads=[("P", p_proj)], writes=[("kraw", ks)])
                    A2_.add("tensor", lambda e, ksl=ksl: e.matmul(
                        P[p_swap][:, :], pswap[:, :], kraw[:, ksl], start=True, stop=True,
                        skip_group_check=True),
                        reads=[("kraw", ks), ("pswap",)], writes=[("P", p_swap)])
                    A2_.cut()
                    A2_.add("vector", lambda e, ksl=ksl: e.tensor_mul(
                        out=t1[:, :], in0=kraw[:, ksl], in1=ropec[:, :]),
                        reads=[("kraw", ks), ("ropec",)], writes=[("t1",)])
                    A2_.add("vector", lambda e: e.tensor_mul(
                        out=t2[:, :], in0=P[p_swap][:, :], in1=ropes[:, :]),
                        reads=[("P", p_swap), ("ropes",)], writes=[("t2",)])
                    A2_.add("vector", lambda e, c=c, out_sb=out_sb: e.tensor_add(
                        out=out_sb[:, out_col_fn(c)], in0=t1[:, :], in1=t2[:, :]),
                        reads=[("t1",), ("t2",)],
                        writes=[out_res_fn(c) if out_res_fn is not None else out_res])
                    A2_.cut()

            def dma_x_half(st, T0):
                S.add("sync", lambda e, T0=T0: e.dma_start(
                    out=AP(st["xau"], 0, [[8 * THF, 128], [THF, 8], [1, THF]]),
                    in_=AP(x0_d, T0, [[N, 128], [128 * N, 8], [1, THF]])),
                    writes=[("xau",)], dma=True)

            def dma_rope(st, ti):
                S.add("sync", lambda e, ti=ti: e.dma_start(
                    out=st["ropec"][:, :], in_=AP(cos_d, ti * TW, [[N, 128], [1, TW]])),
                    writes=[("ropec",)], dma=True)
                S.add("sync", lambda e, ti=ti: e.dma_start(
                    out=st["ropes"][:, :], in_=AP(sin_d, ti * TW, [[N, 128], [1, TW]])),
                    writes=[("ropes",)], dma=True)

            dma_wT(wkT_d, wA, "wA")
            dma_wT(wvT_d, wB, "wB")
            for srcd, dstd, rows, width, res in (
                    (wqkv_d, wqkvb_d, 2 * 1024, 1536, "wqkvC"),
                    (woB_d, wob_d, 2 * 512, 1024, "woC"),
                    (w1_d, w1b_d, 2 * 1024, 5460, "w1C"),
                    (w2_d, w2b_d, 2 * 2730, 1024, "w2C")):
                r0 = 0
                while r0 < rows:
                    rc = min(128, rows - r0)
                    S.add("gpsimd", lambda e, srcd=srcd, dstd=dstd, r0=r0, rc=rc, width=width: e.dma_start(
                        out=AP(dstd, r0 * width, [[width, rc], [1, width]]),
                        in_=AP(srcd, r0 * width, [[width, rc], [1, width]])),
                        writes=[(res, r0)], dma=True)
                    r0 += rc
            with ExitStack() as ctxS:
                st = phaseA_sub(ctxS, is_a2=False)
                P = st["P"]
                S.add("vector", lambda e: e.memset(v_tm[:, :], 1.0), writes=[("v_tm_init",)])

                def ln_tile(i):
                    par = i % 2
                    lqo = par * 8 * TW
                    lqres = ("lq", par)
                    for h2 in range(2):
                        ln_half(st, lambda h2=h2, i=i: dma_x_half(st, i * TW + h2 * THF),
                                lambda c, h2=h2, lqo=lqo: slice(lqo + c * TW + h2 * THF,
                                                                lqo + c * TW + (h2 + 1) * THF),
                                "ln", lq_res=lqres)
                    S.add("sync", lambda e, i=i, lqo=lqo, st=st: e.dma_start(
                        out=AP(lnx_d, i * TW, [[N, 128], [128 * N, 8], [1, TW]]),
                        in_=AP(st["lq"], lqo, [[16 * TW, 128], [TW, 8], [1, TW]])),
                        reads=[lqres], dma=True)

                ln_tile(0)
                for i in range(a_tiles):
                    par = i % 2
                    lqo = par * 8 * TW
                    lqres = ("lq", par)
                    dma_rope(st, i)
                    proj_rope(st, wA, "wA", k_fm, ("k_fm", i),
                              lambda c, i=i: slice(c * N + i * TW, c * N + (i + 1) * TW), i,
                              lq_base=lqo, lq_res=lqres, lq_pitch=16 * TW)
                    if i + 1 < a_tiles:
                        ln_tile(i + 1)
                    for s in range(4):
                        for half in (0, 1):
                            pv = P[4] if half == 0 else P[5]
                            pres = ("P", 4) if half == 0 else ("P", 5)
                            lq1 = st["lq"]
                            for j in range(8):
                                S.add("tensor", lambda e, s=s, half=half, pv=pv, j=j, lq1=lq1, lqo=lqo: e.matmul(
                                    pv[:, :], AP(lq1, lqo + j * TW + s * 128, [[16 * TW, 128], [1, 128]]),
                                    wB[:, j * 1024 + half * 512: j * 1024 + (half + 1) * 512],
                                    start=(j == 0), stop=(j == 7)),
                                    reads=[("wB",), lqres], writes=[pres])
                            blk = i * 4 + s
                            S.add("vector", lambda e, blk=blk, half=half, pv=pv: e.tensor_copy(
                                out=AP(v_tm, blk * 1040 + half * 8 * 65, [[32 * 1040, 128], [65, 8], [1, 64]]),
                                in_=AP(pv, 0, [[512, 128], [64, 8], [1, 64]])),
                                reads=[pres, ("v_tm_init",)], writes=[("v_tm", blk)])
                if "kfm" in dbg:
                    S.add("gpsimd", lambda e: e.dma_start(out=dbg["kfm"].ap(), in_=k_fm[:, :]),
                          reads=[("k_fm", i2) for i2 in range(NTW)], dma=True)
                if "vtm" in dbg:
                    S.add("gpsimd", lambda e: e.dma_start(out=dbg["vtm"].ap(), in_=v_tm[:, :]),
                          reads=[("v_tm", b) for b in range(32)] + [("v_tm_init",)], dma=True)
                S.barrier()

            dma_wT(wqT_d, wA, "wA")
            dma_wT(woT_d, wB, "wB")
            with ExitStack() as ctxS:
                st = phaseA_sub(ctxS, is_a2=True)
                P = st["P"]
                e_sb, maskd2, bc = st["e_sb"], st["maskd"], st["bc"]
                lq = st["lq"]
                q_fm = st["qfu"]
                o_fm = st["o_fm"]
                u_f32 = st["lq"].ap().bitcast(F32)
                S.add("sync", lambda e: e.dma_start(out=maskd2[:, :], in_=maskslide_d.ap()),
                      writes=[("maskd2",)], dma=True)

                class UAdder:
                    def __init__(self):
                        self.units = [[]]

                    def add(self, *aa, **kk):
                        self.units[-1].append((aa, kk))

                    def cut(self):
                        if self.units[-1]:
                            self.units.append([])

                    def take(self):
                        out = [u_ for u_ in self.units if u_]
                        self.units = [[]]
                        return out

                fill_q = []

                def pump(n=1):
                    for _ in range(n):
                        if not fill_q:
                            return
                        for aa, kk in fill_q.pop(0):
                            S.add(*aa, **kk)

                qcol = lambda c: slice(c * TW, (c + 1) * TW)
                qres = lambda c: ("qfu", c)

                def dma_lq(Jn):
                    S.add("sync", lambda e, Jn=Jn: e.dma_start(
                        out=AP(lq, 0, [[8 * TW, 128], [TW, 8], [1, TW]]),
                        in_=AP(lnx_d, Jn * TW, [[N, 128], [128 * N, 8], [1, TW]])),
                        writes=[("lq",)], dma=True)

                def enqueue_proj(Jn, c):
                    ua = UAdder()
                    proj_rope(st, wA, "wA", q_fm, None, qcol, Jn,
                              adder=ua, p_proj=0, p_swap=0, c_list=[c], out_res_fn=qres)
                    fill_q.extend(ua.take())

                dma_rope(st, 0)
                dma_lq(0)
                proj_rope(st, wA, "wA", q_fm, None, qcol, 0,
                          p_proj=0, p_swap=0, out_res_fn=qres)
                pending_div = []
                for J in range(a_tiles):
                    if J + 1 < a_tiles:
                        dma_rope(st, J + 1)
                        dma_lq(J + 1)
                    nkt = 4 * J + 4
                    for h in range(H):
                        hc = h // 2
                        hb = (h % 2) * 64
                        pO = P[2] if h % 2 == 0 else P[3]
                        pOres = ("P", 2) if h % 2 == 0 else ("P", 3)

                        def s_pair(kp, nkt=nkt, hb=hb, hc=hc, J=J):
                            pS = P[4] if kp % 2 == 0 else P[5]
                            pRes = ("P", 4) if kp % 2 == 0 else ("P", 5)
                            for half in range(2):
                                kt = 2 * kp + half
                                diag = kt >= nkt - 4
                                S.add("tensor", lambda e, kt=kt, pS=pS, diag=diag, hb=hb, hc=hc, half=half: e.matmul(
                                    pS[:, half * 512:(half + 1) * 512],
                                    k_fm[hb:hb + 64, hc * N + kt * 128: hc * N + (kt + 1) * 128],
                                    q_fm[hb:hb + 64, hc * TW:(hc + 1) * TW],
                                    start=True, stop=not diag, skip_group_check=True),
                                    reads=[("k_fm", kt // 4), ("qfu", hc)], writes=[pRes])
                                if diag:
                                    o = (kt - (nkt - 4)) * 128
                                    S.add("tensor", lambda e, pS=pS, o=o, half=half: e.matmul(
                                        pS[:, half * 512:(half + 1) * 512], eye128[:, :], maskd2[:, 384 - o: 896 - o],
                                        start=False, stop=True, skip_group_check=True),
                                        reads=[("maskd2",), ("eye128",)], writes=[pRes])
                            es = kp % 2
                            S.add("scalar", lambda e, pS=pS, es=es: e.activation(
                                out=e_sb[:, es * 2 * TW:(es + 1) * 2 * TW], in_=pS[:, :], func=AF.Exp,
                                scale=1.0 / HD, bias=consts[:, 1:2]),
                                reads=[pRes, ("c_negC",)], writes=[("e_sb", es)])

                        def pv_pair(kp, nkt=nkt, h=h, pO=pO, pOres=pOres):
                            es = kp % 2
                            for half in range(2):
                                kt = 2 * kp + half
                                S.add("tensor", lambda e, kt=kt, h=h, es=es, nkt=nkt, pO=pO, half=half: e.matmul(
                                    pO[0:65, :], v_tm[:, kt * 1040 + h * 65: kt * 1040 + h * 65 + 65],
                                    e_sb[:, es * 2 * TW + half * TW: es * 2 * TW + (half + 1) * TW],
                                    start=(kt == 0), stop=(kt == nkt - 1),
                                    skip_group_check=True),
                                    reads=[("e_sb", es), ("v_tm", kt), ("v_tm_init",)], writes=[pOres])

                        def emit_divide():
                            for aa, kk in pending_div:
                                S.add(*aa, **kk)
                            pending_div.clear()

                        nkp = nkt // 2
                        s_pair(0)
                        pump(1)
                        for kp in range(1, nkp):
                            s_pair(kp)
                            if pending_div:
                                emit_divide()
                            pump(1)
                            pv_pair(kp - 1)
                            pump(1)
                        pv_pair(nkp - 1)
                        if pending_div:
                            emit_divide()
                        pump(1)

                        p_ = h % 2
                        rbase = 64 if p_ == 0 else 32
                        rd = bc[rbase:rbase + 1, :]
                        S.add("vector", lambda e, rd=rd, pO=pO: e.reciprocal(out=rd, in_=pO[64:65, :]),
                              reads=[pOres], writes=[("r_rd", p_)])
                        pd = []
                        pd.append((("tensor", lambda e, rd=rd, rbase=rbase: e.matmul(
                            P[1][0:64, :], onesall[rbase:rbase + 1, :], rd, start=True, stop=True)),
                            dict(reads=[("r_rd", p_), ("onesall",)], writes=[("P", 1)])))
                        pd.append((("vector", lambda e: e.tensor_copy(out=bc[0:64, :], in_=P[1][0:64, :])),
                            dict(reads=[("P", 1)], writes=[("bc",), ("r_rd", 1)])))
                        if hb == 0:
                            pd.append((("vector", lambda e, hc=hc, pO=pO: e.tensor_mul(
                                out=o_fm[0:64, hc * TW:(hc + 1) * TW], in0=pO[0:64, :], in1=bc[0:64, :])),
                                dict(reads=[pOres, ("bc",)], writes=[("o_fm", hc, 0)])))
                        else:
                            pd.append((("vector", lambda e, hc=hc, pO=pO: e.tensor_mul(
                                out=o_fm[64:128, hc * TW:(hc + 1) * TW], in0=pO[0:64, :], in1=bc[0:64, :])),
                                dict(reads=[pOres, ("bc",)], writes=[("o_fm", hc, 1)])))
                        pending_div.extend((a2args + (kk,)) if False else (a2args, kk)
                                           for a2args, kk in pd)
                        pump(2)
                        if J + 1 < a_tiles and h % 2 == 1:
                            enqueue_proj(J + 1, h // 2)
                    if pending_div:
                        for aa, kk in pending_div:
                            S.add(*aa, **kk)
                        pending_div.clear()
                    pump(len(fill_q))
                    for h2 in range(2):
                        for c in range(8):
                            for j in range(8):
                                S.add("tensor", lambda e, c=c, j=j, h2=h2: e.matmul(
                                    P[2][:, 0:THF], wB[:, j * 1024 + c * 128: j * 1024 + (c + 1) * 128],
                                    o_fm[:, j * TW + h2 * THF: j * TW + (h2 + 1) * THF],
                                    start=(j == 0), stop=(j == 7), skip_group_check=True),
                                    reads=[("wB",), ("o_fm", j, 0), ("o_fm", j, 1)], writes=[("P", 2)])
                            S.add("vector", lambda e, c=c: e.tensor_scalar_add(
                                out=u_f32[:, c * THF:(c + 1) * THF], in0=P[2][:, 0:THF],
                                scalar1=bo_sb[:, c:c + 1]),
                                reads=[("P", 2), ("bo",)], writes=[("lq",)])
                        ln_half(st, None, None, "u", xbuf=u_f32, xres=("lq",), stats_ix=(4, 5))
                        T0 = J * TW + h2 * THF
                        u_dma = st["lq"].ap().bitcast(F32).rearrange("p (c t) -> p c t", c=8)
                        S.add("gpsimd", lambda e, T0=T0, u_dma=u_dma: e.dma_start(
                            out=u_dma,
                            in_=AP(x0_d, T0, [[N, 128], [128 * N, 8], [1, THF]]),
                            accum_op=ALU.add),
                            reads=[("lq",)], writes=[("lq",)], dma=True)
                        S.add("sync", lambda e, T0=T0, u_dma=u_dma: e.dma_start(
                            out=AP(x1_d, T0, [[N, 128], [128 * N, 8], [1, THF]]),
                            in_=u_dma),
                            reads=[("lq",)], dma=True)
                        if "x1" in dbg:
                            S.add("sync", lambda e, T0=T0, u_dma=u_dma: e.dma_start(
                                out=AP(dbg["x1"], T0, [[N, 128], [128 * N, 8], [1, THF]]),
                                in_=u_dma),
                                reads=[("lq",)], dma=True)
                S.barrier()

        S.barrier()
        TB = 512
        if "B" in skip:
            S.emit()
            return nc, dbg
        with ExitStack() as ctxB:
            sbB = lambda name, shape, dt: ctxB.enter_context(nc.sbuf_tensor(name, shape, dt))
            psB = lambda name, shape, dt: ctxB.enter_context(nc.psum_tensor(name, shape, dt))

            z_sb = sbB("z_sb", [128, 8 * N], F32)
            t1B = sbB("t1B", [128, TB], F32)
            t2B = sbB("t2B", [128, TB], F32)
            t3B = sbB("t3B", [128, TB], F32)
            xpb = sbB("xpb", [128, 2 * 256], F32)
            constsB = sbB("constsB", [128, 2], F32)
            colsB = sbB("colsB", [128, 24], F32)
            pa_col = sbB("pa_col", [128, 16], F32)
            padb = sbB("padb", [128, 256], BF16)

            pm_col = sbB("pm_col", [128, 16], F32)
            den_col = sbB("den_col", [128, 8], F32)

            b1T_sb = sbB("b1T_sb", [128, 88], F32)
            b2T_sb = sbB("b2T_sb", [128, 16], F32)
            lngT_sb = sbB("lngT_sb", [128, 16], F32)
            lnbT_sb = sbB("lnbT_sb", [128, 16], F32)
            for dst, srcd, res in ((b1T_sb, b1T_d, "b1T"), (b2T_sb, b2T_d, "b2T"),
                                   (lngT_sb, lngT_d, "lngT"), (lnbT_sb, lnbT_d, "lnbT")):
                S.add("sync", lambda e, dst=dst, srcd=srcd: e.dma_start(out=dst[:, :], in_=srcd.ap()),
                      writes=[(res,)], dma=True)

            PB_holder = {}

            S.add("vector", lambda e: e.memset(constsB[:, 0:1], 1e-5), writes=[("c_eps",)])
            S.add("vector", lambda e: e.memset(constsB[:, 1:2], 1e-24), writes=[("c_eps24",)])

            def z_ap(c, t0, n=TB):
                return z_sb[:, c * N + t0: c * N + t0 + n]

            def dma_xpos(tt_global, par):
                S.add("sync", lambda e, tt_global=tt_global, par=par: e.dma_start(
                    out=xpb[:, par * 256:(par + 1) * 256],
                    in_=AP(xpos_d, tt_global * 128 * 256, [[256, 128], [1, 256]])),
                    writes=[("xpb", par)], dma=True)

            def ln_statsB(src_fn, src_reads, rowsB, xbB, x2bB, src_is_bf16, pb=None, rt="B"):
                PB = pb if pb is not None else PB_holder["PB"]
                if "lnall" in SKIP_SET:
                    return
                for c in range(8):
                    sl = c % 2
                    ss_ = slice(sl * TB, (sl + 1) * TB)
                    s_ap = src_fn(c)
                    if src_is_bf16:
                        xv, xres = s_ap, list(src_reads)
                    else:
                        S.add("vector", lambda e, s_ap=s_ap, ss_=ss_: e.tensor_copy(out=xbB[:, ss_], in_=s_ap),
                              reads=list(src_reads), writes=[(rt + "xb", sl)])
                        xv, xres = xbB[:, ss_], [(rt + "xb", sl)]
                    S.add("scalar", lambda e, s_ap=s_ap, ss_=ss_: e.activation(out=x2bB[:, ss_], in_=s_ap, func=AF.Square),
                          reads=list(src_reads), writes=[(rt + "x2", sl)])
                    S.add("tensor", lambda e, xv=xv, c=c, PB=PB: e.matmul(
                        PB[2][0:1, 0:TB], onescol[:, :], xv, start=(c == 0), stop=(c == 7)),
                        reads=xres + [("onescol",)], writes=[(rt, 2)])
                    S.add("tensor", lambda e, ss_=ss_, c=c, PB=PB: e.matmul(
                        PB[3][0:1, 0:TB], onescol[:, :], x2bB[:, ss_], start=(c == 0), stop=(c == 7)),
                        reads=[(rt + "x2", sl), ("onescol",)], writes=[(rt, 3)])
                if "lnrows" in SKIP_SET:
                    return
                mu = rowsB[:, 0 * TB:1 * TB]
                m2 = rowsB[:, 1 * TB:2 * TB]
                rr = m2
                mur = rowsB[:, 2 * TB:3 * TB]
                S.add("vector", lambda e, mu=mu: e.tensor_scalar_mul(out=mu, in0=PB[2][0:1, 0:TB], scalar1=1.0 / D),
                      reads=[(rt, 2)], writes=[(rt + "mu",)])
                S.add("vector", lambda e, mu=mu, m2=m2: e.scalar_tensor_tensor(out=m2, in0=mu, scalar=-1.0, in1=mu,
                                                                 op0=ALU.mult, op1=ALU.mult),
                      reads=[(rt + "mu",)], writes=[(rt + "m2",)])
                S.add("vector", lambda e, m2=m2: e.scalar_tensor_tensor(out=m2, in0=PB[3][0:1, 0:TB], scalar=1.0 / D,
                                                                 in1=m2, op0=ALU.mult, op1=ALU.add),
                      reads=[(rt, 3), (rt + "m2",)], writes=[(rt + "var",)])
                S.add("scalar", lambda e, m2=m2, rr=rr: e.activation(out=rr, in_=m2, func=AF.Ln, bias=constsB[0:1, 0:1]),
                      reads=[(rt + "var",), ("c_eps",)], writes=[(rt + "lnv",)])
                S.add("scalar", lambda e, rr=rr: e.activation(out=rr, in_=rr, func=AF.Exp, scale=-0.5),
                      reads=[(rt + "lnv",)], writes=[(rt + "r",)])
                S.add("vector", lambda e, mu=mu, rr=rr, mur=mur: e.tensor_mul(out=mur, in0=mu, in1=rr),
                      reads=[(rt + "mu",), (rt + "r",)], writes=[(rt + "mur",)])
                S.add("tensor", lambda e, rr=rr, PB=PB: e.matmul(PB[0][:, 0:TB], ones128[:, :], rr, start=True, stop=True),
                      reads=[(rt + "r",), ("ones128",)], writes=[(rt, 0)])
                S.add("tensor", lambda e, mur=mur, PB=PB: e.matmul(PB[1][:, 0:TB], ones128[:, :], mur, start=True, stop=True),
                      reads=[(rt + "mur",), ("ones128",)], writes=[(rt, 1)])


            if "binit" in skip:
                b_init_tiles = 0
            else:
                b_init_tiles = b_tiles
            with ExitStack() as ctxI:
                sbI = lambda name, shape, dt: ctxI.enter_context(nc.sbuf_tensor(name, shape, dt))
                rows2 = [sbI(f"rowsB{p}", [1, 3 * TB], F32) for p in range(2)]
                xb2 = [sbI(f"xbB{p}", [128, 2 * TB], BF16) for p in range(2)]
                x2b2 = [sbI(f"x2bB{p}", [128, 2 * TB], BF16) for p in range(2)]
                xF2 = [sbI(f"xF{p}", [128, 8 * TB], F32) for p in range(2)]
                t1_2 = [t1B, t2B]
                PBI = [ctxI.enter_context(nc.psum_tensor(f"PBI{i}", [128, 512], F32)) for i in range(8)]
                for i in range(b_init_tiles):
                    p = i % 2
                    xF = xF2[p]
                    PB = PBI[p * 4:p * 4 + 4]
                    rt = f"B{p}"
                    tb_ = t1_2[p]
                    S.add("sync", lambda e, i=i, xF=xF: e.dma_start(
                        out=AP(xF, 0, [[8 * TB, 128], [TB, 8], [1, TB]]),
                        in_=AP(x1_d, i * TB, [[N, 128], [128 * N, 8], [1, TB]])),
                        writes=[("xF", p)], dma=True)
                    ln_statsB(lambda c, xF=xF: xF[:, c * TB:(c + 1) * TB], [("xF", p)],
                              rows2[p], xb2[p], x2b2[p], False, pb=PB, rt=rt)
                    if "lnapply" in SKIP_SET:
                        continue
                    for c in range(8):
                        S.add("vector", lambda e, c=c, PB=PB, xF=xF, tb_=tb_: e.tensor_mul(
                            out=tb_[:, :], in0=xF[:, c * TB:(c + 1) * TB], in1=PB[0][:, 0:TB]),
                            reads=[("xF", p), (rt, 0)], writes=[("t1B", p)])
                        S.add("vector", lambda e, c=c, i=i, PB=PB, tb_=tb_: e.tensor_sub(
                            out=z_ap(c, i * TB), in0=tb_[:, :], in1=PB[1][:, 0:TB]),
                            reads=[("t1B", p), (rt, 1)], writes=[("z", i)])
            if "zinit" in dbg:
                S.add("sync", lambda e: e.dma_start(out=dbg["zinit"].ap(), in_=z_sb[:, :]),
                      reads=[("z", i2) for i2 in range(8)], dma=True)
            S.barrier()

            nlayers = 0 if b_stop == "init" else nlayers
            for l in range(nlayers):
                with ExitStack() as ctxT:
                    sbT = lambda name, shape, dt: ctxT.enter_context(nc.sbuf_tensor(f"{name}_{l}", shape, dt))
                    psT = lambda name, shape, dt: ctxT.enter_context(nc.psum_tensor(f"{name}_{l}", shape, dt))
                    wqkv_sb = sbT("wqkv_sb", [128, 8 * 1536], BF16)
                    wo_sb = sbT("wo_sb", [128, 4 * 1024], BF16)
                    kT = sbT("kT", [128, 4 * 1024], BF16)
                    vT = sbT("vT", [128, 8 * 512], BF16)
                    maskB = sbT("maskB", [128, 640 + 1280], BF16)
                    w0off = [640, 1152, 1536, 1792]
                    qsbc = sbT("qsbc", [128, 512], BF16)
                    ksbc = sbT("ksbc", [128, 512], BF16)
                    zb = sbT("zb", [128, 8 * 128], BF16)
                    q_sb = sbT("q_sb", [128, 512], BF16)
                    k_sb = sbT("k_sb", [128, 512], BF16)
                    sqB = sbT("sqB", [128, 512], BF16)
                    qT = sbT("qT", [128, 512], BF16)
                    eT2 = sbT("eT2", [128, 2 * 640], BF16)
                    e_sb2 = sbT("e_sb2", [128, 2 * 640], BF16)
                    o_fmB = sbT("o_fmB", [128, 4 * 512], BF16)
                    pS2 = [psT("pSa", [128, 640], F32), psT("pSb", [128, 640], F32)]
                    pO_ = psT("pO", [128, 512], F32)
                    pZ_ = psT("pZ", [128, 512], F32)
                    pT2 = [psT("pTa", [128, 1024], BF16), psT("pTb", [128, 1024], BF16)]

                    S.add("sync", lambda e, l=l: e.dma_start(
                        out=AP(wqkv_sb, 0, [[8 * 1536, 128], [1536, 8], [1, 1536]]),
                        in_=AP(wqkvb_d, l * 1024 * 1536, [[1536, 128], [128 * 1536, 8], [1, 1536]])),
                        reads=[("wqkvC", r0) for r0 in range(0, 2048, 128)],
                        writes=[("wqkvB",)], dma=True)
                    S.add("sync", lambda e, l=l: e.dma_start(
                        out=AP(wo_sb, 0, [[4 * 1024, 128], [1024, 4], [1, 1024]]),
                        in_=AP(wob_d, l * 512 * 1024, [[1024, 128], [128 * 1024, 4], [1, 1024]])),
                        reads=[("woC", r0) for r0 in range(0, 1024, 128)],
                        writes=[("woB",)], dma=True)
                    S.add("sync", lambda e: e.dma_start(out=maskB[:, :], in_=maskB_d.ap()),
                          writes=[("maskB",)], dma=True)
                    S.add("gpsimd", lambda e, l=l: e.dma_start(
                        out=qsbc[:, :], in_=AP(qsbc_d, l * 128 * 512, [[512, 128], [1, 512]])),
                        writes=[("qsbc",)], dma=True)
                    S.add("gpsimd", lambda e, l=l: e.dma_start(
                        out=ksbc[:, :], in_=AP(ksbc_d, l * 128 * 512, [[512, 128], [1, 512]])),
                        writes=[("ksbc",)], dma=True)
                    S.add("vector", lambda e: e.memset(kT[:, :], 0.0),
                          writes=[("kT", s2, b2) for s2 in (0, 1) for b2 in range(4)])
                    S.add("vector", lambda e: e.memset(vT[:, :], 0.0),
                          writes=[("vT", s2, b2) for s2 in (0, 1) for b2 in range(4)])
                    S.add("vector", lambda e: e.memset(pa_col[:, :], 0.0), writes=[("pa", 0), ("pa", 1)])

                    for w in range(nwindows):
                        W0 = w * TB
                        S.add("gpsimd", lambda e, w=w, W0=W0: e.tensor_copy(
                            out=pa_col[:, (w % 2) * 8:(w % 2) * 8 + 8],
                            in_=AP(z_sb, W0 + TB - 1, [[8 * N, 128], [N, 8], [1, 1]])),
                            reads=[("z", w)], writes=[("pa", w % 2)])
                        for tt in range(4):
                            T0 = W0 + tt * 128
                            par = (w * 4 + tt) % 2
                            dma_xpos(w * 4 + tt, par)
                            S.add("gpsimd", lambda e, T0=T0: e.tensor_copy(
                                out=zb[:, 0:512],
                                in_=AP(z_sb, T0, [[8 * N, 128], [N, 4], [1, 128]])),
                                reads=[("z", w)], writes=[("zb",)])
                            if tt == 0:
                                S.add("gpsimd", lambda e, w=w: e.tensor_copy(
                                    out=AP(zb, 512, [[8 * 128, 128], [128, 4], [1, 1]]),
                                    in_=pa_col[:, ((w + 1) % 2) * 8 + 4:((w + 1) % 2) * 8 + 8]),
                                    reads=[("pa", (w + 1) % 2)], writes=[("zb",)])
                                S.add("gpsimd", lambda e, T0=T0: e.tensor_copy(
                                    out=AP(zb, 513, [[8 * 128, 128], [128, 4], [1, 127]]),
                                    in_=AP(z_sb, 4 * N + T0, [[8 * N, 128], [N, 4], [1, 127]])),
                                    reads=[("z", w)], writes=[("zb",)])
                            else:
                                S.add("gpsimd", lambda e, T0=T0: e.tensor_copy(
                                    out=zb[:, 512:1024],
                                    in_=AP(z_sb, 4 * N + T0 - 1, [[8 * N, 128], [N, 4], [1, 128]])),
                                    reads=[("z", w)], writes=[("zb",)])
                            for j in range(8):
                                for qi, (bidx, col0) in enumerate(((0, 0), (1, 512), (2, 1024))):
                                    pb = pS2[0] if bidx < 2 else pS2[1]
                                    poff = 0 if bidx % 2 == 0 else 0
                                    tgt = (pS2[0], ("pSx", 0)) if bidx == 0 else (
                                          (pS2[1], ("pSx", 1)) if bidx == 1 else (pO_, ("pOx",)))
                                    S.add("tensor", lambda e, j=j, col0=col0, tgt=tgt: e.matmul(
                                        tgt[0][:, 0:512], zb[:, j * 128:(j + 1) * 128],
                                        wqkv_sb[:, j * 1536 + col0: j * 1536 + col0 + 512],
                                        start=(j == 0), stop=(j == 7), skip_group_check=True),
                                        reads=[("zb",), ("wqkvB",)], writes=[tgt[1]])
                            S.add("scalar", lambda e: e.activation(out=q_sb[:, :], in_=pS2[0][:, 0:512], func=AF.Copy),
                                  reads=[("pSx", 0)], writes=[("q_sb",)])
                            S.add("scalar", lambda e: e.activation(out=k_sb[:, :], in_=pS2[1][:, 0:512], func=AF.Copy),
                                  reads=[("pSx", 1)], writes=[("k_sb",)])
                            S.add("scalar", lambda e, w=w, tt=tt: e.activation(
                                out=vT[:, ((w % 2) * 4 + tt) * 512: ((w % 2) * 4 + tt + 1) * 512],
                                in_=pO_[:, 0:512], func=AF.Copy),
                                reads=[("pOx",)], writes=[("vT", w % 2, tt)])
                            for which, buf, bres, rinv_off, scbc, scres, toff in (
                                    ("q", q_sb, ("q_sb",), 8, qsbc, ("qsbc",), 0),
                                    ("k", k_sb, ("k_sb",), 16, ksbc, ("ksbc",), 128)):
                                S.add("scalar", lambda e, buf=buf, bres=bres: e.activation(out=sqB[:, :], in_=buf[:, :], func=AF.Square),
                                      reads=[bres], writes=[("sqB",)])
                                S.add("vector", lambda e: e.reduce_sum(
                                    out=colsB[:, 0:8], in_=AP(sqB, 0, [[512, 128], [64, 8], [1, 64]]), axis=AX.X),
                                    reads=[("sqB",)], writes=[("ssq",)])
                                S.add("scalar", lambda e, rinv_off=rinv_off: e.activation(
                                    out=colsB[:, rinv_off:rinv_off + 8], in_=colsB[:, 0:8], func=AF.Ln,
                                    bias=constsB[:, 1:2]),
                                    reads=[("ssq",), ("c_eps24",)], writes=[("rinv", which)])
                                S.add("scalar", lambda e, rinv_off=rinv_off: e.activation(
                                    out=colsB[:, rinv_off:rinv_off + 8], in_=colsB[:, rinv_off:rinv_off + 8],
                                    func=AF.Exp, scale=-0.5),
                                    reads=[("rinv", which)], writes=[("rinv", which)])
                                ek = "vector"
                                S.add(ek, lambda e, buf=buf, bres=bres, rinv_off=rinv_off: e.tensor_mul(
                                    out=t1B[:, :], in0=buf[:, :],
                                    in1=AP(colsB, rinv_off, [[24, 128], [1, 8], [0, 64]])),
                                    reads=[bres, ("rinv", which)], writes=[("t1B",)])
                                S.add(ek, lambda e, scbc=scbc, scres=scres: e.tensor_mul(
                                    out=t1B[:, :], in0=t1B[:, :], in1=scbc[:, :]),
                                    reads=[("t1B",), scres], writes=[("t1B",)])
                                S.add(ek, lambda e, par=par, toff=toff: e.tensor_mul(
                                    out=t2B[:, :], in0=t1B[:, :],
                                    in1=AP(xpb, par * 256 + toff, [[512, 128], [0, 8], [1, 64]])),
                                    reads=[("t1B",), ("xpb", par)], writes=[("t2B",)])
                                S.add(ek, lambda e, par=par, toff=toff: e.tensor_mul(
                                    out=t3B[:, :],
                                    in0=AP(t1B, 32, [[512, 128], [64, 8], [-32, 2], [1, 32]]),
                                    in1=AP(xpb, par * 256 + toff + 64, [[512, 128], [0, 8], [1, 64]])),
                                    reads=[("t1B",), ("xpb", par)], writes=[("t3B",)])
                                S.add(ek, lambda e, buf=buf, bres=bres: e.tensor_add(out=buf[:, :], in0=t2B[:, :], in1=t3B[:, :]),
                                      reads=[("t2B",), ("t3B",)], writes=[bres])
                            if "bqr" in dbg and l == 0 and w == 0 and tt == 0:
                                S.add("gpsimd", lambda e: e.dma_start(out=dbg["bqr"].ap(), in_=q_sb[:, :]),
                                      reads=[("q_sb",)], dma=True)
                                S.add("gpsimd", lambda e: e.dma_start(out=dbg["bkr"].ap(), in_=k_sb[:, :]),
                                      reads=[("k_sb",)], dma=True)
                            for hp in range(4):
                                S.add("tensor", lambda e, hp=hp: e.transpose(
                                    pT2[0][:, hp * 128:(hp + 1) * 128], q_sb[:, hp * 128:(hp + 1) * 128], eye128[:, :]),
                                    reads=[("q_sb",), ("eye128",)], writes=[("pT", 0)])
                                S.add("tensor", lambda e, hp=hp: e.transpose(
                                    pT2[0][:, 512 + hp * 128: 512 + (hp + 1) * 128], k_sb[:, hp * 128:(hp + 1) * 128], eye128[:, :]),
                                    reads=[("k_sb",), ("eye128",)], writes=[("pT", 0)])
                            S.add("vector", lambda e: e.tensor_copy(out=qT[:, :], in_=pT2[0][:, 0:512]),
                                  reads=[("pT", 0)], writes=[("qT",)])
                            S.add("vector", lambda e, w=w, tt=tt: e.tensor_copy(
                                out=AP(kT, (w % 2) * 512 + tt * 128, [[4 * 1024, 128], [1024, 4], [1, 128]]),
                                in_=pT2[0][:, 512:1024]),
                                reads=[("pT", 0)], writes=[("kT", w % 2, tt)])
                            spans = []
                            for j5 in range(5):
                                stp = tt + j5
                                slot = ((w + 1) % 2) if stp < 4 else (w % 2)
                                blk = stp if stp < 4 else stp - 4
                                spans.append((j5, slot, blk))

                            runs = []
                            for j5, slot, blk in spans:
                                if runs and runs[-1][1] == slot and runs[-1][2] + runs[-1][3] == blk \
                                        and runs[-1][0] + runs[-1][3] != 4:
                                    runs[-1] = (runs[-1][0], slot, runs[-1][2], runs[-1][3] + 1)
                                else:
                                    runs.append((j5, slot, blk, 1))
                            wide = []
                            for j5, slot, blk, ln_ in runs:
                                while ln_ > 0:
                                    lim = 4 - j5 if j5 < 4 else 1
                                    take = min(ln_, lim)
                                    wide.append((j5, slot, blk, take))
                                    j5 += take
                                    blk += take
                                    ln_ -= take

                            def stage_S(h):
                                hp, parity = h // 2, h % 2
                                hb = parity * 64
                                pS = pS2[h % 2]
                                pRes = ("pSx", h % 2)
                                seen = set()
                                for j5, slot, blk, ln_ in wide:
                                    bank = 0 if j5 < 4 else 1
                                    st_ = bank not in seen
                                    seen.add(bank)
                                    S.add("tensor", lambda e, j5=j5, slot=slot, blk=blk, ln_=ln_, st_=st_, hp=hp, hb=hb, pS=pS: e.matmul(
                                        pS[:, j5 * 128:(j5 + ln_) * 128],
                                        qT[hb:hb + 64, hp * 128:(hp + 1) * 128],
                                        kT[hb:hb + 64, hp * 1024 + slot * 512 + blk * 128: hp * 1024 + slot * 512 + (blk + ln_) * 128],
                                        start=st_, stop=False, skip_group_check=True),
                                        reads=[("qT",)] + [("kT", slot, blk + q_) for q_ in range(ln_)], writes=[pRes])
                                S.add("tensor", lambda e, pS=pS, w=w: e.matmul(
                                    pS[:, 0:512], eye128[:, :], maskB[:, 0:512],
                                    start=False, stop=(w != 0), skip_group_check=True),
                                    reads=[("maskB",), ("eye128",)], writes=[pRes])
                                S.add("tensor", lambda e, pS=pS: e.matmul(
                                    pS[:, 512:640], eye128[:, :], maskB[:, 512:640],
                                    start=False, stop=True, skip_group_check=True),
                                    reads=[("maskB",), ("eye128",)], writes=[pRes])
                                if w == 0:
                                    cut = 512 - 128 * tt
                                    S.add("tensor", lambda e, cut=cut, tt=tt, pS=pS: e.matmul(
                                        pS[:, 0:cut], eye128[:, :], maskB[:, w0off[tt]: w0off[tt] + cut],
                                        start=False, stop=True, skip_group_check=True),
                                        reads=[("maskB",), ("eye128",)], writes=[pRes])

                            def stage_soft1(h):
                                pS = pS2[h % 2]
                                pRes = ("pSx", h % 2)
                                dc = (h % 2) * 4
                                S.add("vector", lambda e, pS=pS, dc=dc: e.reduce_max(
                                    out=den_col[:, dc:dc + 1], in_=pS[:, 0:640], axis=AX.X),
                                    reads=[pRes], writes=[("mx", h % 2)])
                                S.add("vector", lambda e, dc=dc: e.tensor_scalar_mul(
                                    out=den_col[:, dc + 1:dc + 2], in0=den_col[:, dc:dc + 1], scalar1=-8.0),
                                    reads=[("mx", h % 2)], writes=[("mx8", h % 2)])

                            def stage_exp(h):
                                pS = pS2[h % 2]
                                pRes = ("pSx", h % 2)
                                dc = (h % 2) * 4
                                eo = (h % 2) * 640
                                S.add("scalar", lambda e, pS=pS, dc=dc, eo=eo: e.activation(
                                    out=e_sb2[:, eo:eo + 640], in_=pS[:, 0:640], func=AF.Exp, scale=8.0,
                                    bias=den_col[:, dc + 1:dc + 2], accum_out=den_col[:, dc + 2:dc + 3]),
                                    reads=[pRes, ("mx8", h % 2)], writes=[("e1", h % 2), ("d1", h % 2)])

                            def stage_soft2(h):
                                dc = (h % 2) * 4
                                eo = (h % 2) * 640
                                S.add("vector", lambda e, dc=dc: e.reciprocal(
                                    out=den_col[:, dc + 3:dc + 4], in_=den_col[:, dc + 2:dc + 3]),
                                    reads=[("d1", h % 2)], writes=[("rv", h % 2)])
                                S.add("vector", lambda e, dc=dc, eo=eo: e.tensor_scalar_mul(
                                    out=e_sb2[:, eo:eo + 640], in0=e_sb2[:, eo:eo + 640], scalar1=den_col[:, dc + 3:dc + 4]),
                                    reads=[("e1", h % 2), ("rv", h % 2)], writes=[("e1", h % 2)])

                            def stage_T(h):
                                eo = (h % 2) * 640
                                pT = pT2[h % 2]
                                for j5 in range(5):
                                    S.add("tensor", lambda e, j5=j5, eo=eo, pT=pT: e.transpose(
                                        pT[:, j5 * 128:(j5 + 1) * 128], e_sb2[:, eo + j5 * 128: eo + (j5 + 1) * 128], eye128[:, :]),
                                        reads=[("e1", h % 2), ("eye128",)], writes=[("pT", h % 2)])
                                S.add("vector", lambda e, eo=eo, pT=pT: e.tensor_copy(
                                    out=eT2[:, eo:eo + 640], in_=pT[:, 0:640]),
                                    reads=[("pT", h % 2)], writes=[("eT", h % 2)])

                            def stage_PV(h):
                                hp, parity = h // 2, h % 2
                                hb = parity * 64
                                eo = (h % 2) * 640
                                for j5, slot, blk in spans:
                                    kwargs = {}
                                    if parity == 1:
                                        kwargs["tile_position"] = (0, 64)
                                    S.add("tensor", lambda e, j5=j5, slot=slot, blk=blk, h=h, hb=hb, tt=tt, eo=eo, kwargs=kwargs: e.matmul(
                                        pO_[hb:hb + 64, tt * 128:(tt + 1) * 128],
                                        vT[:, (slot * 4 + blk) * 512 + h * 64: (slot * 4 + blk) * 512 + (h + 1) * 64],
                                        eT2[:, eo + j5 * 128: eo + (j5 + 1) * 128],
                                        start=(j5 == 0), stop=(j5 == 4),
                                        skip_group_check=True, **kwargs),
                                        reads=[("eT", h % 2), ("vT", slot, blk)], writes=[("pOx",)])
                                S.add("scalar", lambda e, hp=hp, tt=tt, hb=hb: e.activation(
                                    out=o_fmB[hb:hb + 64, hp * 512 + tt * 128: hp * 512 + (tt + 1) * 128],
                                    in_=pO_[hb:hb + 64, tt * 128:(tt + 1) * 128], func=AF.Copy),
                                    reads=[("pOx",)], writes=[("o_fmB", hp, tt, hb)])

                            stages = [stage_S, stage_soft1, stage_exp, stage_soft2, stage_T, stage_PV]
                            nst = len(stages)
                            for step in range(8 + nst - 1):
                                for si in range(nst - 1, -1, -1):
                                    h = step - si
                                    if 0 <= h < 8:
                                        stages[si](h)
                        if "bofm" in dbg and l == 0 and w == 0:
                            S.add("gpsimd", lambda e: e.dma_start(out=dbg["bofm"].ap(), in_=o_fmB[:, :]),
                                  reads=[("o_fmB", hp2, tt2, b64) for hp2 in range(4) for tt2 in range(4) for b64 in (0, 64)], dma=True)
                        for f in range(8):
                            for j in range(4):
                                S.add("tensor", lambda e, f=f, j=j: e.matmul(
                                    pZ_[:, 0:512], wo_sb[:, j * 1024 + f * 128: j * 1024 + (f + 1) * 128],
                                    o_fmB[:, j * 512:(j + 1) * 512],
                                    start=(j == 0), stop=(j == 3), skip_group_check=True),
                                    reads=[("woB",)] + [("o_fmB", j, tt2, b64) for tt2 in range(4) for b64 in (0, 64)], writes=[("pZx",)])
                            S.add("vector", lambda e, f=f, W0=W0: e.tensor_add(
                                out=z_ap(f, W0), in0=z_ap(f, W0), in1=pZ_[:, 0:512]),
                                reads=[("pZx",), ("z", w)], writes=[("z", w)])
                if ("z0a" in dbg and l == 0) or ("z1a" in dbg and l == 1):
                    key = "z0a" if l == 0 else "z1a"
                    S.add("sync", lambda e, key=key: e.dma_start(out=dbg[key].ap(), in_=z_sb[:, :]),
                          reads=[("z", i2) for i2 in range(8)], dma=True)
                S.barrier()
                if not do_mlp:
                    continue
                with ExitStack() as ctxM:
                    sbM = lambda name, shape, dt: ctxM.enter_context(nc.sbuf_tensor(f"{name}_{l}", shape, dt))
                    psM = lambda name: ctxM.enter_context(nc.psum_tensor(f"{name}_{l}", [128, 512], F32))
                    rowsB = sbM("rowsM", [1, 3 * TB], F32)
                    x2bB = sbM("x2bM", [128, 2 * TB], BF16)
                    hB = sbM("hB", [128, 8 * TB], BF16)
                    lnh = sbM("lnh", [128, 8 * TB], BF16)
                    glu = sbM("glu", [128, 22 * 512], BF16)
                    w1a = sbM("w1a", [128, 2 * 1024], BF16)
                    w1g = sbM("w1g", [128, 2 * 1024], BF16)
                    w2b = sbM("w2b", [128, 4 * 1024], BF16)
                    a_sb = sbM("a_sb", [128, 512], F32)
                    a2_sb = sbM("a2_sb", [128, 512], F32)
                    
                    gg = sbM("gg", [128, 2 * 512], BF16)
                    PBM = [psM(f"PBM{i}") for i in range(8)]
                    PB_holder["PB"] = PBM[0:4]

                    S.add("vector", lambda e: e.memset(pm_col[:, :], 0.0), writes=[("pm", 0), ("pm", 1)])
                    for i in range(8):
                        T0 = i * TB
                        S.add("gpsimd", lambda e, i=i, T0=T0: e.tensor_copy(
                            out=pm_col[:, (i % 2) * 8:(i % 2) * 8 + 8],
                            in_=AP(z_sb, T0 + TB - 1, [[8 * N, 128], [N, 8], [1, 1]])),
                            reads=[("z", i)], writes=[("pm", i % 2)])
                        S.add("gpsimd", lambda e, T0=T0: e.tensor_copy(
                            out=AP(hB, 0, [[8 * TB, 128], [TB, 4], [1, TB]]),
                            in_=AP(z_sb, T0, [[8 * N, 128], [N, 4], [1, TB]])),
                            reads=[("z", i)], writes=[("hB",)])
                        S.add("gpsimd", lambda e, i=i: e.tensor_copy(
                            out=AP(hB, 4 * TB, [[8 * TB, 128], [TB, 4], [1, 1]]),
                            in_=pm_col[:, ((i + 1) % 2) * 8 + 4:((i + 1) % 2) * 8 + 8]),
                            reads=[("pm", (i + 1) % 2)], writes=[("hB",)])
                        S.add("gpsimd", lambda e, T0=T0: e.tensor_copy(
                            out=AP(hB, 4 * TB + 1, [[8 * TB, 128], [TB, 4], [1, TB - 1]]),
                            in_=AP(z_sb, 4 * N + T0, [[8 * N, 128], [N, 4], [1, TB - 1]])),
                            reads=[("z", i)], writes=[("hB",)])
                        ln_statsB(lambda c: hB[:, c * TB:(c + 1) * TB], [("hB",)], rowsB, None, x2bB, True)
                        for c in range(8):
                            S.add("vector", lambda e, c=c: e.tensor_mul(
                                out=t1B[:, :], in0=hB[:, c * TB:(c + 1) * TB], in1=PBM[0][:, 0:TB]),
                                reads=[("hB",), ("B", 0)], writes=[("t1B",)])
                            S.add("vector", lambda e: e.tensor_sub(out=t1B[:, :], in0=t1B[:, :], in1=PBM[1][:, 0:TB]),
                                  reads=[("t1B",), ("B", 1)], writes=[("t1B",)])
                            S.add("vector", lambda e, c=c, l=l: e.tensor_scalar(
                                out=lnh[:, c * TB:(c + 1) * TB], in0=t1B[:, :],
                                scalar1=lngT_sb[:, l * 8 + c: l * 8 + c + 1],
                                scalar2=lnbT_sb[:, l * 8 + c: l * 8 + c + 1],
                                op0=ALU.mult, op1=ALU.add),
                                reads=[("t1B",), ("lngT",), ("lnbT",)], writes=[("lnh", c)])
                        for cc in range(22):
                            wc = 128 if cc < 21 else 42
                            par = cc % 2
                            pa_ = PBM[4 + (cc % 2)]
                            pg_ = PBM[6 + (cc % 2)]
                            ra_, rg_ = ("B", 4 + (cc % 2)), ("B", 6 + (cc % 2))
                            epar = cc % 2
                            S.add("sync", lambda e, cc=cc, wc=wc, par=par, l=l: e.dma_start(
                                out=AP(w1a, par * 1024, [[2 * 1024, 128], [128, 8], [1, wc]]),
                                in_=AP(w1b_d, l * 1024 * 5460 + cc * 128, [[5460, 128], [128 * 5460, 8], [1, wc]])),
                                writes=[("w1a", par)], dma=True)
                            S.add("sync", lambda e, cc=cc, wc=wc, par=par, l=l: e.dma_start(
                                out=AP(w1g, par * 1024, [[2 * 1024, 128], [128, 8], [1, wc]]),
                                in_=AP(w1b_d, l * 1024 * 5460 + 2730 + cc * 128, [[5460, 128], [128 * 5460, 8], [1, wc]])),
                                writes=[("w1g", par)], dma=True)
                            for j in range(8):
                                S.add("tensor", lambda e, wc=wc, par=par, j=j, pa_=pa_: e.matmul(
                                    pa_[0:wc, 0:512], w1a[:, par * 1024 + j * 128: par * 1024 + j * 128 + wc],
                                    lnh[:, j * TB:(j + 1) * TB], start=(j == 0), stop=(j == 7), skip_group_check=True),
                                    reads=[("w1a", par), ("lnh", j)], writes=[ra_])
                                S.add("tensor", lambda e, wc=wc, par=par, j=j, pg_=pg_: e.matmul(
                                    pg_[0:wc, 0:512], w1g[:, par * 1024 + j * 128: par * 1024 + j * 128 + wc],
                                    lnh[:, j * TB:(j + 1) * TB], start=(j == 0), stop=(j == 7), skip_group_check=True),
                                    reads=[("w1g", par), ("lnh", j)], writes=[rg_])
                            abuf = a_sb if epar == 0 else a2_sb
                            S.add("vector", lambda e, cc=cc, wc=wc, l=l, pa_=pa_, abuf=abuf: e.tensor_scalar_add(
                                out=abuf[0:wc, :], in0=pa_[0:wc, 0:512],
                                scalar1=b1T_sb[0:wc, l * 44 + cc: l * 44 + cc + 1]),
                                reads=[ra_, ("b1T",)], writes=[("a_sb", epar)])
                            S.add("scalar", lambda e, cc=cc, wc=wc, l=l, pg_=pg_, epar=epar: e.activation(
                                out=gg[0:wc, epar * 512:epar * 512 + 512], in_=pg_[0:wc, 0:512], func=AF.Gelu,
                                bias=b1T_sb[0:wc, l * 44 + 22 + cc: l * 44 + 22 + cc + 1]),
                                reads=[rg_, ("b1T",)], writes=[("gg", epar)])
                            S.add("vector", lambda e, cc=cc, wc=wc, epar=epar, abuf=abuf: e.tensor_mul(
                                out=glu[0:wc, cc * 512:(cc + 1) * 512],
                                in0=abuf[0:wc, :],
                                in1=gg[0:wc, epar * 512:epar * 512 + 512]),
                                reads=[("a_sb", epar), ("gg", epar)], writes=[("glu", cc)])
                        for cc in range(22):
                            wc = 128 if cc < 21 else 42
                            par = cc % 4
                            S.add("sync", lambda e, cc=cc, wc=wc, par=par, l=l: e.dma_start(
                                out=AP(w2b, par * 1024, [[4 * 1024, wc], [1, 1024]]),
                                in_=AP(w2b_d, l * 2730 * 1024 + cc * 128 * 1024, [[1024, wc], [1, 1024]])),
                                writes=[("w2b", par)], dma=True)
                            for f in range(8):
                                S.add("tensor", lambda e, cc=cc, wc=wc, par=par, f=f: e.matmul(
                                    PBM[f][:, 0:512], w2b[0:wc, par * 1024 + f * 128:par * 1024 + (f + 1) * 128],
                                    glu[0:wc, cc * 512:(cc + 1) * 512],
                                    start=(cc == 0), stop=(cc == 21), skip_group_check=True),
                                    reads=[("w2b", par), ("glu", cc)], writes=[("B", f)])
                        for f in range(8):
                            S.add("vector", lambda e, f=f, T0=T0, l=l: e.scalar_tensor_tensor(
                                out=z_ap(f, T0), in0=PBM[f][:, 0:512],
                                scalar=b2T_sb[:, l * 8 + f: l * 8 + f + 1],
                                in1=z_ap(f, T0), op0=ALU.add, op1=ALU.add),
                                reads=[("B", f), ("b2T",), ("z", i)], writes=[("z", i)])
                if ("z0f" in dbg and l == 0) or ("z1f" in dbg and l == 1):
                    key = "z0f" if l == 0 else "z1f"
                    S.add("sync", lambda e, key=key: e.dma_start(out=dbg[key].ap(), in_=z_sb[:, :]),
                          reads=[("z", i2) for i2 in range(8)], dma=True)
                S.barrier()

            if "bfinal" in skip:
                b_tiles = 0
            with ExitStack() as ctxF:
                sbF = lambda name, shape, dt: ctxF.enter_context(nc.sbuf_tensor(name, shape, dt))
                rows2 = [sbF(f"rowsF{p}", [1, 3 * TB], F32) for p in range(2)]
                xb2 = [sbF(f"xbF{p}", [128, 2 * TB], BF16) for p in range(2)]
                x2b2 = [sbF(f"x2bF{p}", [128, 2 * TB], BF16) for p in range(2)]
                xF2 = [sbF(f"xFF{p}", [128, 8 * TB], F32) for p in range(2)]
                t1_2 = [t1B, t2B]
                PBF = [ctxF.enter_context(nc.psum_tensor(f"PBF{i}", [128, 512], F32)) for i in range(8)]
                for i in range(b_tiles):
                    p = i % 2
                    xF = xF2[p]
                    PB = PBF[p * 4:p * 4 + 4]
                    rt = f"F{p}"
                    tb_ = t1_2[p]
                    S.add("sync", lambda e, i=i, xF=xF: e.dma_start(
                        out=AP(xF, 0, [[8 * TB, 128], [TB, 8], [1, TB]]),
                        in_=AP(x1_d, i * TB, [[N, 128], [128 * N, 8], [1, TB]])),
                        writes=[("xF", p)], dma=True)
                    ln_statsB(lambda c, i=i: z_ap(c, i * TB), [("z", i)],
                              rows2[p], xb2[p], x2b2[p], False, pb=PB, rt=rt)
                    for c in range(8):
                        S.add("vector", lambda e, c=c, i=i, PB=PB, tb_=tb_: e.tensor_mul(
                            out=tb_[:, :], in0=z_ap(c, i * TB), in1=PB[0][:, 0:TB]),
                            reads=[("z", i), (rt, 0)], writes=[("t1B", p)])
                        S.add("vector", lambda e, PB=PB, tb_=tb_: e.tensor_sub(
                            out=tb_[:, :], in0=tb_[:, :], in1=PB[1][:, 0:TB]),
                            reads=[("t1B", p), (rt, 1)], writes=[("t1B", p)])
                        S.add("vector", lambda e, c=c, xF=xF, tb_=tb_: e.tensor_add(
                            out=xF[:, c * TB:(c + 1) * TB], in0=xF[:, c * TB:(c + 1) * TB], in1=tb_[:, :]),
                            reads=[("t1B", p), ("xF", p)], writes=[("xF", p)])
                    S.add("sync", lambda e, i=i, xF=xF: e.dma_start(
                        out=AP(y_d, i * TB, [[N, 128], [128 * N, 8], [1, TB]]),
                        in_=AP(xF, 0, [[8 * TB, 128], [TB, 8], [1, TB]])),
                        reads=[("xF", p)], dma=True)

        S.emit()
    return nc, dbg


def make_shared_inputs(ins):
    b1 = np.asarray(ins["L0_ffn_b1"], np.float32)
    b1T = np.zeros((128, 88), np.float32)
    for l in range(2):
        for cc in range(22):
            wc = 128 if cc < 21 else 42
            b1T[0:wc, l * 44 + cc] = b1[l, cc * 128: cc * 128 + wc]
            b1T[0:wc, l * 44 + 22 + cc] = b1[l, 2730 + cc * 128: 2730 + cc * 128 + wc]
    def cols8(v):
        out = np.zeros((128, 16), np.float32)
        for l in range(2):
            for c in range(8):
                out[:, l * 8 + c] = v[l, c * 128:(c + 1) * 128]
        return out
    qs = np.asarray(ins["L0_ffn_q_scale"], np.float32)
    ks = np.asarray(ins["L0_ffn_k_scale"], np.float32)
    qsbc = np.broadcast_to(np.tile(qs, (1, 8))[:, None, :], (2, 128, 512)).copy()
    ksbc = np.broadcast_to(np.tile(ks, (1, 8))[:, None, :], (2, 128, 512)).copy()
    return {
        "wqT": np.ascontiguousarray(np.asarray(ins["L0_attn_Wq"], np.float32).T),
        "wkT": np.ascontiguousarray(np.asarray(ins["L0_attn_Wk"], np.float32).T),
        "wvT": np.ascontiguousarray(np.asarray(ins["L0_attn_Wv"], np.float32).T),
        "woT": np.ascontiguousarray(np.asarray(ins["L0_attn_Wo"], np.float32).T),
        "boT": np.ascontiguousarray(np.asarray(ins["L0_attn_bo"], np.float32).reshape(8, 128).T),
        "wqkv": np.ascontiguousarray(np.asarray(ins["L0_ffn_wqkv"], np.float32)),
        "wo_in": np.ascontiguousarray(np.asarray(ins["L0_ffn_wo"], np.float32)),
        "w1_in": np.ascontiguousarray(np.asarray(ins["L0_ffn_w1"], np.float32)),
        "w2_in": np.ascontiguousarray(np.asarray(ins["L0_ffn_w2"], np.float32)),
        "b1T": b1T,
        "b2T": cols8(np.asarray(ins["L0_ffn_b2"], np.float32)),
        "lngT": cols8(np.asarray(ins["L0_ffn_ln_g"], np.float32)),
        "lnbT": cols8(np.asarray(ins["L0_ffn_ln_b"], np.float32)),
        "qsbcT": qsbc,
        "ksbcT": ksbc,
    }


def make_inputs_core(x_core, shared):
    d = dict(shared)
    d["x0"] = np.ascontiguousarray(x_core.T)
    return d



_PROGRAM_CACHE = {}


def kernel(**inputs):
    from concourse.bass_utils import run_bass_kernel_spmd
    ins = {k: np.asarray(v) for k, v in inputs.items()}
    if "nc" not in _PROGRAM_CACHE:
        _PROGRAM_CACHE["nc"] = build_program(debug=())[0]
    nc = _PROGRAM_CACHE["nc"]
    shared = make_shared_inputs(ins)
    x = np.asarray(ins["L0_attn_x"], np.float32)
    in_maps = [make_inputs_core(x[b], shared) for b in range(8)]
    res = run_bass_kernel_spmd(nc, in_maps, core_ids=list(range(8)))
    out = np.stack([np.ascontiguousarray(r["y"].T) for r in res.results], axis=0)
    return out.astype(np.float32)
